# version 06
# baseline (speedup 1.0000x reference)
import os
import numpy as np
import ml_dtypes
from contextlib import ExitStack, contextmanager

import concourse.bass as bass
import concourse.mybir as mybir
from concourse.bass_utils import run_bass_kernel_spmd

F32 = mybir.dt.float32
F32R = mybir.dt.float32r
BF16 = mybir.dt.bfloat16
F16 = mybir.dt.float16
I32 = mybir.dt.int32
AF = mybir.ActivationFunctionType
ALU = mybir.AluOpType
AX = mybir.AxisListType

NCORES = 8
BPC = 2
S = 2048
D = 1024
KT = 8
TC = 4
TCW = 512


class TR:
    def __init__(self, nc, es):
        self.nc = nc
        self.eng = {"t": nc.tensor, "v": nc.vector, "s": nc.scalar,
                    "g": nc.gpsimd, "q": nc.sync}
        self.sem = {}
        self.tick = {}
        for e in self.eng:
            self.sem[e] = es.enter_context(nc.semaphore(f"tk_{e}"))
            self.tick[e] = 0
        self.dsem = {}
        self.dcnt = {}
        for e in ("q", "s", "g", "v"):
            self.dsem[e] = es.enter_context(nc.semaphore(f"dm_{e}"))
            self.dcnt[e] = 0
        self.state = {}
        self.waited = {e: {} for e in self.eng}
        self.dma_strict = os.environ.get("DMA_STRICT", "g")

    def _wait(self, e, rec, fence=False):
        sem, val, eng = rec
        if eng == e and not fence:
            return
        wd = self.waited[e]
        k = sem.name
        if wd.get(k, -1) >= val:
            return
        self.eng[e].wait_ge(sem, val)
        wd[k] = val

    def pre(self, e, reads=(), writes=(), fence=False):
        for r in reads:
            st = self.state.get(r)
            if st and st["w"]:
                self._wait(e, st["w"], fence)
        for r in writes:
            st = self.state.get(r)
            if st:
                if st["w"]:
                    self._wait(e, st["w"], fence)
                for rec in st["r"]:
                    self._wait(e, rec, fence)
        return self.eng[e]

    def post(self, e, last_ins, reads=(), writes=(), dma=False):
        if dma:
            self.dcnt[e] += 16
            rec = (self.dsem[e], self.dcnt[e], None)
            last_ins.then_inc(self.dsem[e], 16)
        else:
            self.tick[e] += 1
            rec = (self.sem[e], self.tick[e], e)
            last_ins.then_inc(self.sem[e], 1)
        for r in writes:
            self.state[r] = {"w": rec, "r": []}
        for r in reads:
            st = self.state.setdefault(r, {"w": None, "r": []})
            st["r"].append(rec)
        return rec

    def barrier(self):
        for e in self.eng:
            for e2, sem in self.sem.items():
                if e2 != e and self.tick[e2] > 0:
                    self._wait(e, (sem, self.tick[e2], e2))
            for e2, sem in self.dsem.items():
                if self.dcnt[e2] > 0:
                    self._wait(e, (sem, self.dcnt[e2], None))
        self.state.clear()


class OpCtx:
    def __init__(self, tr, e, reads, writes, dma=False, fence=False):
        self.tr, self.e, self.reads, self.writes, self.dma = tr, e, tuple(reads), tuple(writes), dma
        self.fence = fence
        self.last = None

    def __enter__(self):
        self.E = self.tr.pre(self.e, self.reads, self.writes, self.fence)
        return self

    def __call__(self, ins):
        self.last = ins
        return ins

    def __exit__(self, et, ev, tb):
        if et is None:
            assert self.last is not None, "empty op group"
            self.tr.post(self.e, self.last, self.reads, self.writes, self.dma)
        return False


class G:
    def __init__(self, nc, es):
        self.nc = nc
        self.es = es
        self.tr = TR(nc, es)
        self.uid = 0
        self.ps = [es.enter_context(nc.psum_tensor(f"psb{i}", [128, 512], F32))
                   for i in range(8)]
        self.psr = [f"psb{i}" for i in range(8)]
        self.consts = {}
        self.dram = {}
        self.dbg = {}

        self._scopes = []

    def u(self):
        self.uid += 1
        return self.uid

    @contextmanager
    def scope(self):
        es = ExitStack()
        self._scopes.append(es)
        try:
            yield
        finally:
            self.tr.barrier()
            self._scopes.pop()
            es.close()

    def op(self, e, reads=(), writes=(), fence=False):
        return OpCtx(self.tr, e, reads, writes, fence=fence)

    def const(self, name, arr):
        if name not in self.consts:
            self.consts[name] = self.nc.inline_tensor(np.ascontiguousarray(arr), name).ap()
        return self.consts[name]

    def const_sb(self, name, arr, dt=F32R):
        key = ("sb", name)
        if key not in self.consts:
            ap = self.const(name, arr)
            t = self.sb_root("sb_" + name, list(arr.shape), dt)
            self.dma("g", t[:], ap, writes=("c_" + name,))
            rec = (self.tr.dsem["g"], self.tr.dcnt["g"], None)
            for e in self.tr.eng:
                self.tr._wait(e, rec)
            self.consts[key] = t
        return self.consts[key]

    def fimm(self, val, dt=F32):
        key = ("imm", dt, float(val))
        if key not in self.consts:
            nm = f"imm{len([k for k in self.consts if k[0] == 'imm' if isinstance(k, tuple)])}_{self.u()}"
            t = self.const_sb(nm, np.full((128, 1), val, np.float32), dt)
            self.nc.const_aps.register(dt, float(val), t[:])
            self.consts[key] = float(val)
        return float(val)

    def ones_col(self):
        return self.const_sb("c_ones_col", np.ones((128, 1), np.float32))[:]

    def ones_row(self):
        return self.const_sb("c_ones_row", np.ones((1, 128), np.float32))[:]

    def ident128(self):
        return self.const("c_ident128", np.eye(128, dtype=np.float32))

    def sb(self, name, shape, dt):
        tgt = self._scopes[-1] if self._scopes else self.es
        self._n = getattr(self, "_n", 0) + 1
        return tgt.enter_context(self.nc.sbuf_tensor(f"{name}_{self._n}", shape, dt))

    def sb_root(self, name, shape, dt):
        return self.es.enter_context(self.nc.sbuf_tensor(name, shape, dt, side="right"))

    def dma(self, e, out, in_, reads=(), writes=(), **kw):
        E = self.tr.pre(e, reads, writes)
        if (self.tr.dma_strict in ("1", "all") or e in self.tr.dma_strict) \
                and self.tr.dcnt[e] > 0:
            self.tr._wait(e, (self.tr.dsem[e], self.tr.dcnt[e], None))
        ins = E.dma_start(out=out, in_=in_, **kw)
        self.tr.post(e, ins, reads, writes, dma=True)

    def dmat(self, out, in_, reads=(), writes=()):
        E = self.tr.pre("q", reads, writes)
        ins = E.dma_start_transpose(out, in_)
        self.tr.post("q", ins, reads, writes, dma=True)


def emit_ln_v2(g, xfeat_b, XN, xn_region):
    tr = g.tr
    u = g.u()
    xt = g.sb(f"lnx{u}", [128, 2 * TCW], F32R)
    sq = g.sb(f"lnq{u}", [128, TCW], F32R)
    mu = g.sb(f"lnm{u}", [1, TCW], F32)
    mu2 = g.sb(f"lnm2{u}", [1, TCW], F32)
    var = g.sb(f"lnv{u}", [1, TCW], F32)
    srt = g.sb(f"lns{u}", [1, TCW], F32)
    rst = g.sb(f"lnr{u}", [1, TCW], F32R)
    nmr = g.sb(f"lnn{u}", [1, TCW], F32R)
    rstb = g.sb(f"lnrb{u}", [128, TCW], F32)
    nmrb = g.sb(f"lnnb{u}", [128, TCW], F32)
    tmpn = g.sb(f"lntp{u}", [128, TCW], F32)
    ones = g.ones_col()
    onesr = g.ones_row()
    R_X, R_SQ, R_ROW, R_B = f"lnx{u}_", f"lnq{u}", f"lnrow{u}", f"lnb{u}"

    for tc in range(TC):
        ts = slice(tc * TCW, (tc + 1) * TCW)
        for kt in range(KT):
            pb = kt % 2
            xv = xt[:, pb * TCW:(pb + 1) * TCW]
            g.dma("g", xv, xfeat_b[kt * 128:(kt + 1) * 128, ts],
                  reads=("xf",), writes=(R_X + str(pb),))
            with g.op("s", reads=(R_X + str(pb),), writes=(R_SQ,)) as o:
                o(o.E.activation(sq[:], xv.bitcast(F32), AF.Square))
            with g.op("t", reads=(R_X + str(pb), R_SQ), writes=("psb0", "psb1")) as o:
                o.E.matmul(g.ps[0][0:1, :], ones, xv,
                           start=(kt == 0), stop=(kt == KT - 1))
                o(o.E.matmul(g.ps[1][0:1, :], ones, sq[:],
                             start=(kt == 0), stop=(kt == KT - 1)))
        with g.op("v", reads=("psb0", "psb1"), writes=(R_ROW,)) as o:
            o.E.tensor_scalar_mul(mu[:], g.ps[0][0:1, :], 1.0 / D)
            o(o.E.tensor_scalar_mul(var[:], g.ps[1][0:1, :], 1.0 / D))
        with g.op("v", reads=(R_ROW,), writes=(R_ROW,), fence=True) as o:
            o(o.E.tensor_mul(mu2[:], mu[:], mu[:]))
        with g.op("v", reads=(R_ROW,), writes=(R_ROW,), fence=True) as o:
            o(o.E.tensor_sub(var[:], var[:], mu2[:]))
        with g.op("s", reads=(R_ROW,), writes=(R_ROW,), fence=True) as o:
            o(o.E.activation(srt[:], var[:], AF.Ln, bias=g.fimm(1e-5)))
        with g.op("s", reads=(R_ROW,), writes=(R_ROW,), fence=True) as o:
            with g.nc.allow_low_precision(reason="f32r matmul operand prep"):
                o(o.E.activation(rst[:], srt[:], AF.Exp, scale=-0.5))
        with g.op("v", reads=(R_ROW,), writes=(R_ROW,), fence=True) as o:
            with g.nc.allow_low_precision(reason="f32r matmul operand prep"):
                o(o.E.scalar_tensor_tensor(nmr[:], mu[:], -1.0, rst[:].bitcast(F32),
                                           ALU.mult, ALU.mult))
        with g.op("t", reads=(R_ROW,), writes=("psb2", "psb3")) as o:
            o.E.matmul(g.ps[2][:, :], onesr, rst[:], start=True, stop=True)
            o(o.E.matmul(g.ps[3][:, :], onesr, nmr[:], start=True, stop=True))
        with g.op("v", reads=("psb2", "psb3"), writes=(R_B,)) as o:
            o.E.tensor_copy(rstb[:], g.ps[2][:, :])
            o(o.E.tensor_copy(nmrb[:], g.ps[3][:, :]))
        for kt in range(KT):
            pb = kt % 2
            xv = xt[:, pb * TCW:(pb + 1) * TCW]
            g.dma("g", xv, xfeat_b[kt * 128:(kt + 1) * 128, ts],
                  reads=("xf",), writes=(R_X + str(pb),))
            with g.op("v", reads=(R_X + str(pb), R_B), writes=(xn_region,)) as o:
                dst = XN[:, kt * S + tc * TCW: kt * S + (tc + 1) * TCW]
                o.E.tensor_mul(tmpn[:], xv.bitcast(F32), rstb[:])
                with g.nc.allow_low_precision(reason="stream dtype"):
                    o(o.E.tensor_add(dst, tmpn[:], nmrb[:]))
    return XN


def emit_l0_ffn(g, sl, xfeat, yfeat):
    tr = g.tr
    sc = 1.0 / 127.0
    wgT = g.dram["t_L0_ffn_w_gate"]
    wuT = g.dram["t_L0_ffn_w_up"]
    wdT = g.dram["t_L0_ffn_w_down"]
    u = g.u()
    XN = g.sb(f"ffnxn{u}", [128, KT * S], F32R)
    wg_sb = g.sb(f"ffnwg{u}", [128, KT * TCW], F32R)
    wu_sb = g.sb(f"ffnwu{u}", [128, KT * TCW], F32R)
    wd_sb = [g.sb(f"ffnwd{u}_{i}", [128, D], F32R) for i in range(2)]
    sil = g.sb(f"ffnsil{u}", [128, TCW], F32)
    hT = g.sb(f"ffnh{u}", [128, 24 * TCW], F32R)
    xres = g.sb(f"ffnxr{u}", [128, TCW], F32)
    yt = g.sb(f"ffny{u}", [128, TCW], F32)

    for b in range(BPC):
        with g.scope():
            emit_ln_v2(g, xfeat[b], XN, f"xn{u}")
        for tc in range(TC):
            ts = slice(tc * TCW, (tc + 1) * TCW)
            for oc in range(6):
                osl = slice(oc * TCW, (oc + 1) * TCW)
                for kt in range(KT):
                    g.dma("g", wg_sb[:, kt * TCW:(kt + 1) * TCW],
                          wgT[kt * 128:(kt + 1) * 128, osl],
                          writes=(f"wg{u}_{kt}",))
                    g.dma("g", wu_sb[:, kt * TCW:(kt + 1) * TCW],
                          wuT[kt * 128:(kt + 1) * 128, osl],
                          writes=(f"wu{u}_{kt}",))
                for mo in range(4):
                    ht_idx = oc * 4 + mo
                    pg, pu = g.ps[mo], g.ps[4 + mo]
                    rg, ru = g.psr[mo], g.psr[4 + mo]
                    wreads = tuple(f"wg{u}_{kt}" for kt in range(KT)) + (f"xn{u}",)
                    with g.op("t", reads=wreads, writes=(rg,)) as o:
                        for kt in range(KT):
                            o(o.E.matmul(pg[:, :],
                                         wg_sb[:, kt * TCW + mo * 128: kt * TCW + (mo + 1) * 128],
                                         XN[:, kt * S + tc * TCW: kt * S + (tc + 1) * TCW],
                                         start=(kt == 0), stop=(kt == KT - 1)))
                    ureads = tuple(f"wu{u}_{kt}" for kt in range(KT)) + (f"xn{u}",)
                    with g.op("t", reads=ureads, writes=(ru,)) as o:
                        for kt in range(KT):
                            o(o.E.matmul(pu[:, :],
                                         wu_sb[:, kt * TCW + mo * 128: kt * TCW + (mo + 1) * 128],
                                         XN[:, kt * S + tc * TCW: kt * S + (tc + 1) * TCW],
                                         start=(kt == 0), stop=(kt == KT - 1)))
                    with g.op("s", reads=(rg,), writes=(f"sil{u}",)) as o:
                        o(o.E.activation(sil[:], pg[:, :], AF.Silu, scale=g.fimm(sc)))
                    with g.op("v", reads=(ru, f"sil{u}"), writes=(f"h{u}_{ht_idx}",)) as o:
                        o(o.E.scalar_tensor_tensor(
                            hT[:, ht_idx * TCW:(ht_idx + 1) * TCW],
                            pu[:, :], sc, sil[:], ALU.mult, ALU.mult))
            hreads = tuple(f"h{u}_{i}" for i in range(24))
            for hk in range(24):
                wb = wd_sb[hk % 2]
                g.dma("g", wb[:], wdT[hk * 128:(hk + 1) * 128, :],
                      writes=(f"wd{u}_{hk % 2}",))
                with g.op("t", reads=(f"wd{u}_{hk % 2}", f"h{u}_{hk}"),
                          writes=tuple(g.psr)) as o:
                    for mo in range(8):
                        o(o.E.matmul(g.ps[mo][:, :],
                                     wb[:, mo * 128:(mo + 1) * 128],
                                     hT[:, hk * TCW:(hk + 1) * TCW],
                                     start=(hk == 0), stop=(hk == 23)))
            for mo in range(8):
                g.dma("q", xres[:], xfeat[b, mo * 128:(mo + 1) * 128, ts],
                      reads=("xf",), writes=(f"xr{u}",))
                with g.op("v", reads=(g.psr[mo], f"xr{u}"), writes=(f"y{u}",)) as o:
                    o(o.E.scalar_tensor_tensor(yt[:], g.ps[mo][:, :], sc, xres[:],
                                               ALU.mult, ALU.add))
                g.dma("q", yfeat[b, mo * 128:(mo + 1) * 128, ts], yt[:],
                      reads=(f"y{u}",), writes=("yf",))
        tr.barrier()


def prep_weights(inputs):
    t = {}
    def T(name, arr):
        t[name] = np.ascontiguousarray(arr)
    T("t_L0_ffn_w_gate", inputs["L0_ffn_w_gate"].T)
    T("t_L0_ffn_w_up", inputs["L0_ffn_w_up"].T)
    T("t_L0_ffn_w_down", inputs["L0_ffn_w_down"].T)
    T("t_L1_ffn_ih", inputs["L1_ffn_gru_w_ih"].T)
    T("t_L1_ffn_hh", inputs["L1_ffn_gru_w_hh"].T)
    T("t_L1_ffn_expr", inputs["L1_ffn_expr_w"].T)
    T("t_L1_ffn_ctx", inputs["L1_ffn_ctx_w"].T)
    T("t_L1_ffn_ctxb", inputs["L1_ffn_ctx_b"].reshape(KT, 128).T)
    T("t_L1_ffn_gate", inputs["L1_ffn_gate_w"].transpose(0, 2, 1))
    T("t_L1_ffn_up", inputs["L1_ffn_up_w"].transpose(0, 2, 1))
    T("t_L1_ffn_down", inputs["L1_ffn_down_w"].transpose(0, 2, 1))
    T("t_L1_ffn_sg", inputs["L1_ffn_shared_gate_w"].T)
    T("t_L1_ffn_su", inputs["L1_ffn_shared_up_w"].T)
    T("t_L1_ffn_sd", inputs["L1_ffn_shared_down_w"].T)
    T("t_L0_attn_a_q", inputs["L0_attn_a_q_w"].T)
    T("t_L0_attn_a_k", inputs["L0_attn_a_k_w"].T)
    T("t_L0_attn_a_v", inputs["L0_attn_a_v_w"].T)
    T("t_L0_attn_a_hvg", inputs["L0_attn_a_hvg_w"].T)
    T("t_L0_attn_a_him", inputs["L0_attn_a_him_w"][512:].T)
    T("t_L0_attn_a_hxp", inputs["L0_attn_a_hxp_w"].T)
    T("t_L0_attn_a_hmo", inputs["L0_attn_a_hmo_w"].T)
    T("t_L0_attn_a_out", inputs["L0_attn_a_out_w"].T)
    cw = inputs["L0_attn_a_conv_w"]
    cols = np.zeros((128, 112), np.float32)
    for kt in range(KT):
        for j in range(3):
            cols[:, j * 8 + kt] = cw[kt * 128:(kt + 1) * 128, j]
        cols[:, 24 + kt] = inputs["L0_attn_a_conv_b"][kt * 128:(kt + 1) * 128]
        cols[:, 32 + kt] = inputs["L0_attn_a_dt_bias"][kt * 128:(kt + 1) * 128]
        cols[:, 40 + kt] = inputs["L0_attn_a_D_param"][kt * 128:(kt + 1) * 128]
    cols[:, 48:112] = np.transpose(
        inputs["L0_attn_a_A_log"].reshape(KT, 128, 8), (1, 0, 2)).reshape(128, 64)
    T("t_L0_attn_a_cols", cols)
    row = np.zeros((16, 2), np.float32)
    row[:, 0] = inputs["L0_attn_a_q_gain"]
    row[0, 1] = inputs["L0_attn_a_merge_alpha"][0]
    T("t_L0_attn_a_row", row)
    T("t_L0_attn_b_Wq", inputs["L0_attn_b_Wq"])
    T("t_L0_attn_b_Wk", inputs["L0_attn_b_Wk"])
    T("t_L0_attn_b_Wv", inputs["L0_attn_b_Wv"])
    T("t_L0_attn_b_Wo", inputs["L0_attn_b_Wo"])
    T("t_L0_attn_b_b", np.concatenate(
        [inputs[f"L0_attn_b_b{w}"].reshape(KT, 128).T for w in "qkvo"], axis=1))
    T("t_L1_attn_a_Wv", inputs["L1_attn_a_Wv"].T)
    T("t_L1_attn_a_Wo", inputs["L1_attn_a_Wo"].T)
    T("t_L1_attn_a_Wg", inputs["L1_attn_a_Wg"].T)
    T("t_L1_attn_a_Wb", inputs["L1_attn_a_Wb"].T)
    T("t_L1_attn_b_Wg", inputs["L1_attn_b_Wg"].T)
    T("t_L1_attn_b_bg", inputs["L1_attn_b_bg"].reshape(1, 1))
    T("t_L1_attn_b_Wout", inputs["L1_attn_b_Wout"].T)
    T("t_L1_attn_b_bout", inputs["L1_attn_b_bout"].reshape(KT, 128).T)
    return t


WEIGHT_SPECS = {
    "t_L0_attn_a_q": ([D, D], F32),
    "t_L0_attn_a_k": ([D, 256], F32),
    "t_L0_attn_a_v": ([D, 256], F32),
    "t_L0_attn_a_hvg": ([D, 256], F32),
    "t_L0_attn_a_him": ([D, 2048], F32),
    "t_L0_attn_a_hxp": ([D, 80], F32),
    "t_L0_attn_a_hmo": ([D, D], F32),
    "t_L0_attn_a_out": ([D, D], F32),
    "t_L0_attn_a_cols": ([128, 112], F32),
    "t_L0_attn_a_row": ([16, 2], F32),
    "t_L0_attn_b_Wq": ([D, D], F16),
    "t_L0_attn_b_Wk": ([D, D], F16),
    "t_L0_attn_b_Wv": ([D, D], F16),
    "t_L0_attn_b_Wo": ([D, D], F16),
    "t_L0_attn_b_b": ([128, 4 * KT], F16),
    "t_L1_ffn_ih": ([D, 3072], F32),
    "t_L1_ffn_hh": ([D, 3072], F32),
    "t_L1_ffn_expr": ([D, D], F32),
    "t_L1_ffn_ctx": ([D, D], F32),
    "t_L1_ffn_ctxb": ([128, KT], F32),
    "t_L1_ffn_gate": ([8, D, 2048], F32),
    "t_L1_ffn_up": ([8, D, 2048], F32),
    "t_L1_ffn_down": ([8, 2048, D], F32),
    "t_L1_ffn_sg": ([D, 2048], F32),
    "t_L1_ffn_su": ([D, 2048], F32),
    "t_L1_ffn_sd": ([2048, D], F32),
    "t_L1_attn_a_Wv": ([D, D], F32),
    "t_L1_attn_a_Wo": ([D, D], F32),
    "t_L1_attn_a_Wg": ([D, 16], F32),
    "t_L1_attn_a_Wb": ([D, 16], F32),
    "t_L1_attn_b_Wg": ([D, 1], F32),
    "t_L1_attn_b_bg": ([1, 1], F32),
    "t_L1_attn_b_Wout": ([D, D], F32),
    "t_L1_attn_b_bout": ([128, KT], F32),
    "t_L0_ffn_w_gate": ([D, 3072], I32),
    "t_L0_ffn_w_up": ([D, 3072], I32),
    "t_L0_ffn_w_down": ([3072, D], I32),
}

SUBLAYER_WEIGHTS = {
    "L0_ffn": ["t_L0_ffn_w_gate", "t_L0_ffn_w_up", "t_L0_ffn_w_down"],
    "L1_attn_b": ["t_L1_attn_b_Wg", "t_L1_attn_b_bg", "t_L1_attn_b_Wout",
                  "t_L1_attn_b_bout"],
    "L0_attn_b": ["t_L0_attn_b_Wq", "t_L0_attn_b_Wk", "t_L0_attn_b_Wv",
                  "t_L0_attn_b_Wo", "t_L0_attn_b_b"],
    "L1_ffn": ["t_L1_ffn_ih", "t_L1_ffn_hh", "t_L1_ffn_expr", "t_L1_ffn_ctx",
               "t_L1_ffn_ctxb", "t_L1_ffn_gate", "t_L1_ffn_up", "t_L1_ffn_down",
               "t_L1_ffn_sg", "t_L1_ffn_su", "t_L1_ffn_sd"],
    "L1_attn_a": ["t_L1_attn_a_Wv", "t_L1_attn_a_Wo", "t_L1_attn_a_Wg",
                  "t_L1_attn_a_Wb"],
    "L0_attn_a": ["t_L0_attn_a_q", "t_L0_attn_a_k", "t_L0_attn_a_v",
                  "t_L0_attn_a_hvg", "t_L0_attn_a_him", "t_L0_attn_a_hxp",
                  "t_L0_attn_a_hmo", "t_L0_attn_a_out", "t_L0_attn_a_cols",
                  "t_L0_attn_a_row"],
}

SUBLAYER_EMIT = {}


def _emit_l0_ffn(g, sl, xf, yf):
    emit_l0_ffn(g, sl, xf, yf)


SUBLAYER_EMIT["L0_ffn"] = _emit_l0_ffn



def emit_l1_attn_b(g, sl, xfeat, yfeat):
    tr = g.tr
    u = g.u()
    wgT = g.dram["t_L1_attn_b_Wg"]
    bg = g.dram["t_L1_attn_b_bg"]
    woutT = g.dram["t_L1_attn_b_Wout"]
    bout = g.dram["t_L1_attn_b_bout"]

    XN = g.sb(f"abxn{u}", [128, KT * S], F32R)
    wg_sb = g.sb(f"abwg{u}", [128, KT], F32R)
    bg_sb = g.sb(f"abbg{u}", [1, 1], F32)
    wout_sb = g.sb(f"abwo{u}", [128, KT * D], BF16)
    bout_sb = g.sb(f"abbo{u}", [128, KT], F32)
    grow = g.sb(f"abgr{u}", [1, S], F32R)
    ogrow = g.sb(f"abog{u}", [1, S], F32R)
    ogb = g.sb(f"abogb{u}", [128, S], F32)
    gb = g.sb(f"abgb{u}", [128, S], F32)
    gx = g.sb(f"abgx{u}", [128, S], F32)
    H = g.sb(f"abh{u}", [128, KT * S], BF16)
    xres = g.sb(f"abxr{u}", [128, TCW], F32)
    yt = g.sb(f"aby{u}", [128, TCW], F32)

    for kt in range(KT):
        g.dma("g", wg_sb[:, kt:kt + 1], wgT[kt * 128:(kt + 1) * 128, :],
              writes=(f"abwg{u}",))
        g.dma("g", wout_sb[:, kt * D:(kt + 1) * D], woutT[kt * 128:(kt + 1) * 128, :],
              writes=(f"abwo{u}_{kt}",))
    g.dma("q", bg_sb[:], bg, writes=(f"abbg{u}",))
    g.dma("q", bout_sb[:], bout, writes=(f"abbo{u}",))

    for b in range(BPC):
        with g.scope():
            emit_ln_v2(g, xfeat[b], XN, f"abxn{u}")
        for tc in range(TC):
            with g.op("t", reads=(f"abwg{u}", f"abxn{u}"), writes=("psb0",)) as o:
                for kt in range(KT):
                    o(o.E.matmul(g.ps[0][0:1, :], wg_sb[:, kt:kt + 1],
                                 XN[:, kt * S + tc * TCW: kt * S + (tc + 1) * TCW],
                                 start=(kt == 0), stop=(kt == KT - 1)))
            with g.op("s", reads=("psb0", f"abbg{u}"), writes=(f"abgr{u}",)) as o:
                with g.nc.allow_low_precision(reason="f32r row"):
                    o(o.E.activation(grow[0:1, tc * TCW:(tc + 1) * TCW], g.ps[0][0:1, :],
                                     AF.Sigmoid, bias=bg_sb[0:1, :]))
        with g.op("v", reads=(f"abgr{u}",), writes=(f"abog{u}",)) as o:
            with g.nc.allow_low_precision(reason="row math"):
                o(o.E.tensor_scalar(ogrow[:], grow[:].bitcast(F32), -1.0, 1.0,
                                    ALU.mult, ALU.add))
        for tc in range(TC):
            with g.op("t", reads=(f"abgr{u}", f"abog{u}"), writes=("psb1", "psb2")) as o:
                with g.nc.allow_low_precision(reason="broadcast"):
                    o.E.matmul(g.ps[1][:, :], g.ones_row(),
                               ogrow[0:1, tc * TCW:(tc + 1) * TCW],
                               start=True, stop=True)
                    o(o.E.matmul(g.ps[2][:, :], g.ones_row(),
                                 grow[0:1, tc * TCW:(tc + 1) * TCW],
                                 start=True, stop=True))
            with g.op("v", reads=("psb1", "psb2"), writes=(f"abbc{u}",)) as o:
                o.E.tensor_copy(ogb[:, tc * TCW:(tc + 1) * TCW], g.ps[1][:, :])
                o(o.E.tensor_copy(gb[:, tc * TCW:(tc + 1) * TCW], g.ps[2][:, :]))
        for kt in range(KT):
            with g.op("v", reads=(f"abbc{u}", f"abxn{u}"), writes=(f"abgx{u}",)) as o:
                o(o.E.tensor_mul(gx[:], gb[:],
                                 XN[:, kt * S:(kt + 1) * S].bitcast(F32)))
            with g.op("v", reads=(f"abgx{u}", f"abbc{u}"), writes=(f"abh{u}_{kt}",)) as o:
                with g.nc.allow_low_precision(reason="bf16 scan out"):
                    o(o.E.tensor_tensor_scan(H[:, kt * S:(kt + 1) * S], ogb[:], gx[:],
                                             0.0, ALU.mult, ALU.add))
        hreads = tuple(f"abh{u}_{k}" for k in range(KT))
        for mo in range(KT):
            for tc in range(TC):
                with g.op("t", reads=hreads + (f"abwo{u}_{mo}",), writes=("psb3",)) as o:
                    for kt in range(KT):
                        o(o.E.matmul(g.ps[3][:, :],
                                     wout_sb[:, kt * D + mo * 128: kt * D + (mo + 1) * 128],
                                     H[:, kt * S + tc * TCW: kt * S + (tc + 1) * TCW],
                                     start=(kt == 0), stop=(kt == KT - 1)))
                g.dma("q", xres[:], xfeat[b, mo * 128:(mo + 1) * 128,
                                          tc * TCW:(tc + 1) * TCW],
                      reads=("xf",), writes=(f"abxr{u}",))
                with g.op("v", reads=("psb3", f"abxr{u}", f"abbo{u}"),
                          writes=(f"aby{u}",)) as o:
                    o(o.E.scalar_tensor_tensor(yt[:], g.ps[3][:, :],
                                               bout_sb[:, mo:mo + 1], xres[:],
                                               ALU.add, ALU.add))
                g.dma("q", yfeat[b, mo * 128:(mo + 1) * 128, tc * TCW:(tc + 1) * TCW],
                      yt[:], reads=(f"aby{u}",), writes=("yf",))
        tr.barrier()


SUBLAYER_EMIT["L1_attn_b"] = emit_l1_attn_b


def _rope_tables_b():
    inv = np.exp(np.arange(0, 64, 2, dtype=np.float32) * (-np.log(10000.0) / 64))
    ang = np.arange(S, dtype=np.float32)[:, None] * inv[None, :]
    sin = np.sin(ang).astype(np.float16).astype(np.float32)
    cos = np.cos(ang).astype(np.float16).astype(np.float32)
    cosF = np.empty((128, S), np.float32)
    sinF = np.empty((128, S), np.float32)
    for r in range(128):
        f = r % 32
        cosF[r] = cos[:, f]
        sinF[r] = (-1.0 if (r % 64) < 32 else 1.0) * sin[:, f]
    return cosF, sinF


def _swap_mat():
    swap = np.zeros((128, 128), np.float32)
    for m in range(128):
        swap[(m + 32) if (m % 64) < 32 else (m - 32), m] = 1.0
    return swap


def _emit_qkv_proj_b(g, u, W, wname, bias_sb, bidx, XN, xnr, dst, dstr, rope,
                     Wsb, swp, cosT, sinT, qraw, qsw, qtmp):
    for kt in range(KT):
        g.dma("g", Wsb[:, kt * D:(kt + 1) * D], W[kt * 128:(kt + 1) * 128, :],
              writes=(f"{wname}_{kt}",))
    wreads = tuple(f"{wname}_{kt}" for kt in range(KT))
    for mo in range(KT):
        bcol = bias_sb[:, bidx * KT + mo: bidx * KT + mo + 1]
        for tc in range(TC):
            with g.op("t", reads=wreads + (xnr,), writes=("psb0",)) as o:
                for kt in range(KT):
                    o(o.E.matmul(g.ps[0][:, :],
                                 Wsb[:, kt * D + mo * 128: kt * D + (mo + 1) * 128],
                                 XN[:, kt * S + tc * TCW: kt * S + (tc + 1) * TCW],
                                 start=(kt == 0), stop=(kt == KT - 1)))
            tgt = (qraw if rope else dst)[:, (0 if rope else mo * S) + tc * TCW:
                                          (0 if rope else mo * S) + (tc + 1) * TCW]
            with g.op("v", reads=("psb0", f"qb{u}"),
                      writes=((f"qraw{u}",) if rope else (dstr,))) as o:
                with g.nc.allow_low_precision(reason="bf16 attn"):
                    o(o.E.tensor_scalar_add(tgt, g.ps[0][:, :], bcol))
        if rope:
            for tc in range(TC):
                tsl = slice(tc * TCW, (tc + 1) * TCW)
                with g.op("t", reads=(f"qraw{u}", "c_swapb"), writes=("psb1",)) as o:
                    o(o.E.matmul(g.ps[1][:, :], swp, qraw[:, tsl],
                                 start=True, stop=True))
                with g.op("v", reads=("psb1", "c_ropeb"), writes=(f"qsw{u}",)) as o:
                    with g.nc.allow_low_precision(reason="bf16 attn"):
                        o(o.E.tensor_mul(qsw[:, tsl], g.ps[1][:, :], sinT[:, tsl]))
            with g.op("v", reads=(f"qraw{u}", f"qsw{u}", "c_ropeb"),
                      writes=(dstr, f"qtmp{u}")) as o:
                with g.nc.allow_low_precision(reason="bf16 attn"):
                    o.E.tensor_mul(qtmp[:], qraw[:].bitcast(F32), cosT[:])
                    o(o.E.tensor_add(dst[:, mo * S:(mo + 1) * S], qtmp[:], qsw[:]))


def emit_attn_core(g, u, Q, K, V, ya, rQ, rK, rV, rYA, kv_of_h, sc, mask_arr,
                   mask_name):
    tr = g.tr
    idbf = g.const_sb("c_idbf", np.eye(128, dtype=np.float32), BF16)
    id64s = g.const_sb("c_id64s", np.concatenate(
        [np.eye(64, dtype=np.float32)] * 2, axis=0), BF16)
    mask = g.const_sb(mask_name, mask_arr, F32)
    P = g.sb(f"atP{u}", [128, S], F32)
    Pb = g.sb(f"atPb{u}", [128, S], BF16)
    vtok = g.sb(f"atvt{u}", [128, 16 * 64], BF16)
    ptT = g.sb(f"atpt{u}", [128, 128], BF16)
    mx = g.sb(f"atmx{u}", [128, 8], F32)
    rP, rPb, rVT, rPT, rMX, rMXl = (f"atP{u}", f"atPb{u}", f"atvt{u}",
                                    f"atpt{u}", f"atmx{u}", f"atmx{u}l")
    last_kv = None
    for h in range(16):
        ht, rb = h // 2, (h % 2) * 64
        kt_v, rb_v = kv_of_h(h)
        if (kt_v, rb_v) != last_kv:
            last_kv = (kt_v, rb_v)
            for tt in range(16):
                with g.op("t", reads=(rV,), writes=("psb5",)) as o:
                    o(o.E.transpose(g.ps[5][:].bitcast(BF16)[:, 0:64],
                                    V[rb_v:rb_v + 64,
                                      kt_v * S + tt * 128: kt_v * S + (tt + 1) * 128],
                                    id64s[rb_v:rb_v + 64, 0:64]))
                with g.op("v", reads=("psb5",), writes=(rVT,)) as o:
                    with g.nc.allow_low_precision(reason="bf16 attn"):
                        o(o.E.tensor_copy(vtok[:, tt * 64:(tt + 1) * 64],
                                          g.ps[5][:].bitcast(BF16)[:, 0:64]))
        for qt in range(16):
            nk = (qt + 1) * 128
            nch = (nk + TCW - 1) // TCW
            qsl = slice(ht * S + qt * 128, ht * S + (qt + 1) * 128)
            with g.op("t", reads=(rQ, rK),
                      writes=tuple(f"psb{c}" for c in range(nch))) as o:
                for c in range(nch):
                    w = min(TCW, nk - c * TCW)
                    o(o.E.matmul(g.ps[c][:, 0:w], Q[rb:rb + 64, qsl],
                                 K[rb_v:rb_v + 64,
                                   kt_v * S + c * TCW: kt_v * S + c * TCW + w],
                                 start=True, stop=True))
            dc, off = divmod(qt * 128, TCW)
            with g.op("v", reads=(), writes=(f"psb{dc}",)) as o:
                o(o.E.tensor_add(g.ps[dc][:, off:off + 128],
                                 g.ps[dc][:, off:off + 128], mask[:]))
            with g.op("v", reads=tuple(f"psb{c}" for c in range(nch)),
                      writes=(rMX,)) as o:
                for c in range(nch):
                    w = min(TCW, nk - c * TCW)
                    o(o.E.reduce_max(mx[:, c:c + 1], g.ps[c][:, 0:w], axis=AX.X))
            with g.op("v", reads=(rMX,), writes=(rMX,), fence=True) as o:
                o(o.E.reduce_max(mx[:, 6:7], mx[:, 0:nch], axis=AX.X))
            with g.op("v", reads=(rMX,), writes=(rMX,), fence=True) as o:
                o(o.E.tensor_scalar_mul(mx[:, 6:7], mx[:, 6:7], -sc))
            with g.op("v", reads=(), writes=(rMXl,)) as o:
                o(o.E.memset(mx[:, 0:nch], 0.0))
            with g.op("s", reads=tuple(f"psb{c}" for c in range(nch)) +
                      (rMX,), writes=(rP, rMXl)) as o:
                for c in range(nch):
                    w = min(TCW, nk - c * TCW)
                    o(o.E.activation(P[:, c * TCW:c * TCW + w],
                                     g.ps[c][:, 0:w], AF.Exp,
                                     bias=mx[:, 6:7], scale=sc,
                                     accum_out=mx[:, c:c + 1]))
            with g.op("v", reads=(rMXl,), writes=(rMX,)) as o:
                o(o.E.reduce_sum(mx[:, 6:7], mx[:, 0:nch], axis=AX.X))
            with g.op("s", reads=(rMX,), writes=(rMX,)) as o:
                o(o.E.activation(mx[:, 7:8], mx[:, 6:7], AF.Ln))
            with g.op("s", reads=(rMX,), writes=(rMX,), fence=True) as o:
                o(o.E.activation(mx[:, 7:8], mx[:, 7:8], AF.Exp, scale=-1.0))
            with g.op("v", reads=(rP, rMX), writes=(rPb,)) as o:
                with g.nc.allow_low_precision(reason="bf16 attn"):
                    o(o.E.tensor_scalar_mul(Pb[:, 0:nk], P[:, 0:nk], mx[:, 7:8]))
            for k2 in range(qt + 1):
                with g.op("t", reads=(rPb,), writes=("psb5",)) as o:
                    o(o.E.transpose(g.ps[5][:].bitcast(BF16)[:, 0:128],
                                    Pb[:, k2 * 128:(k2 + 1) * 128], idbf[:, :]))
                with g.op("v", reads=("psb5",), writes=(rPT,)) as o:
                    with g.nc.allow_low_precision(reason="bf16 attn"):
                        o(o.E.tensor_copy(ptT[:], g.ps[5][:].bitcast(BF16)[:, 0:128]))
                with g.op("t", reads=(rPT, rVT), writes=("psb4",)) as o:
                    o(o.E.matmul(g.ps[4][rb:rb + 64, 0:128],
                                 vtok[:, k2 * 64:(k2 + 1) * 64], ptT[:],
                                 start=(k2 == 0), stop=(k2 == qt),
                                 tile_position=(0, rb)))
            with g.op("v", reads=("psb4",), writes=(rYA,)) as o:
                with g.nc.allow_low_precision(reason="bf16 attn"):
                    o(o.E.tensor_copy(
                        ya[rb:rb + 64, ht * S + qt * 128: ht * S + (qt + 1) * 128],
                        g.ps[4][rb:rb + 64, 0:128]))


def emit_l0_attn_b(g, sl, xfeat, yfeat):
    tr = g.tr
    u = g.u()
    Wq = g.dram["t_L0_attn_b_Wq"]
    Wk = g.dram["t_L0_attn_b_Wk"]
    Wv = g.dram["t_L0_attn_b_Wv"]
    Wo = g.dram["t_L0_attn_b_Wo"]
    bqkvo = g.dram["t_L0_attn_b_b"]
    cosF, sinF = _rope_tables_b()

    Q = g.sb(f"bQ{u}", [128, KT * S], BF16)
    K = g.sb(f"bK{u}", [128, KT * S], BF16)
    V = g.sb(f"bV{u}", [128, KT * S], BF16)

    for b in range(BPC):
        with g.scope():
            bias_sb = g.sb(f"bqb{u}", [128, 4 * KT], F32)
            g.dma("g", bias_sb[:], bqkvo, writes=(f"qb{u}",))
            XN = g.sb(f"bxn{u}", [128, KT * S], BF16)
            with g.scope():
                emit_ln_v2(g, xfeat[b], XN, f"bxn{u}")
            with g.scope():
                cosT = g.sb(f"bcos{u}", [128, S], F32)
                sinT = g.sb(f"bsin{u}", [128, S], F32)
                g.dma("q", cosT[:], g.const("c_cosFb", cosF), writes=("c_ropeb",))
                g.dma("q", sinT[:], g.const("c_sinFb", sinF), writes=("c_ropeb",))
                swp = g.const_sb("c_swapb", _swap_mat(), F32R)
                Wsb = g.sb(f"bW{u}", [128, KT * D], BF16)
                qraw = g.sb(f"bqr{u}", [128, S], F32R)
                qsw = g.sb(f"bqs{u}", [128, S], F32)
                qtmp = g.sb(f"bqt{u}", [128, S], F32)
                _emit_qkv_proj_b(g, u, Wq, f"bW{u}", bias_sb, 0, XN, f"bxn{u}",
                                 Q, f"bQ{u}", True, Wsb, swp[:], cosT[:], sinT[:],
                                 qraw, qsw, qtmp)
                _emit_qkv_proj_b(g, u, Wk, f"bW{u}", bias_sb, 1, XN, f"bxn{u}",
                                 K, f"bK{u}", True, Wsb, swp[:], cosT[:], sinT[:],
                                 qraw, qsw, qtmp)
                _emit_qkv_proj_b(g, u, Wv, f"bW{u}", bias_sb, 2, XN, f"bxn{u}",
                                 V, f"bV{u}", False, Wsb, swp[:], cosT[:], sinT[:],
                                 qraw, qsw, qtmp)
            with g.scope():
                ya = g.sb(f"bya{u}", [128, KT * S], BF16)
                emit_attn_core(g, u, Q, K, V, ya, f"bQ{u}", f"bK{u}", f"bV{u}",
                               f"bya{u}", lambda h: (h // 2, (h % 2) * 64), 0.125,
                               np.triu(np.full((128, 128), -8e4, np.float32), 1),
                               "c_maskb")
                Wosb = g.sb(f"bWo{u}", [128, KT * D], BF16)
                f16t = g.sb(f"bf16{u}", [128, TCW], F16)
                xres = g.sb(f"bxr{u}", [128, TCW], F32)
                yt = g.sb(f"byy{u}", [128, TCW], F32)
                for kt in range(KT):
                    g.dma("g", Wosb[:, kt * D:(kt + 1) * D],
                          Wo[kt * 128:(kt + 1) * 128, :], writes=(f"bwo{u}_{kt}",))
                woreads = tuple(f"bwo{u}_{kt}" for kt in range(KT))
                for mo in range(KT):
                    bcol = bias_sb[:, 3 * KT + mo: 3 * KT + mo + 1]
                    for tc in range(TC):
                        with g.op("t", reads=woreads + (f"bya{u}",), writes=("psb0",)) as o:
                            for kt in range(KT):
                                o(o.E.matmul(g.ps[0][:, :],
                                             Wosb[:, kt * D + mo * 128: kt * D + (mo + 1) * 128],
                                             ya[:, kt * S + tc * TCW: kt * S + (tc + 1) * TCW],
                                             start=(kt == 0), stop=(kt == KT - 1)))
                        g.dma("q", xres[:], xfeat[b, mo * 128:(mo + 1) * 128,
                                                  tc * TCW:(tc + 1) * TCW],
                              reads=("xf",), writes=(f"bxr{u}",))
                        with g.op("v", reads=("psb0", f"qb{u}"), writes=(f"bf16{u}",)) as o:
                            with g.nc.allow_low_precision(reason="f16 output cast"):
                                o(o.E.tensor_scalar_add(f16t[:], g.ps[0][:, :], bcol))
                        with g.op("v", reads=(f"bf16{u}", f"bxr{u}"), writes=(f"byy{u}",)) as o:
                            o(o.E.tensor_add(yt[:], xres[:], f16t[:]))
                        g.dma("q", yfeat[b, mo * 128:(mo + 1) * 128,
                                         tc * TCW:(tc + 1) * TCW], yt[:],
                              reads=(f"byy{u}",), writes=("yf",))
        tr.barrier()


SUBLAYER_EMIT["L0_attn_b"] = emit_l0_attn_b



ROPE_A = 32


def _rope_tables_a():
    inv = 1.0 / 10000.0 ** (np.arange(0, ROPE_A, 2, dtype=np.float32) / ROPE_A)
    ang = np.arange(S, dtype=np.float32)[:, None] * inv[None, :]
    sin, cos = np.sin(ang), np.cos(ang)
    cosF = np.ones((128, S), np.float32)
    sinF = np.zeros((128, S), np.float32)
    for r in range(128):
        rr = r % 64
        if rr < 16:
            cosF[r], sinF[r] = cos[:, rr], sin[:, rr]
        elif rr < 32:
            cosF[r], sinF[r] = cos[:, rr - 16], -sin[:, rr - 16]
    return cosF, sinF


def _swap_mat_a():
    sw = np.zeros((128, 128), np.float32)
    for m in range(128):
        rr = m % 64
        p = m + 16 if rr < 16 else (m - 16 if rr < 32 else m)
        sw[p, m] = 1.0
    return sw


def _hmask():
    hm = np.zeros((128, 128), np.float32)
    hm[0:64, 0] = 1.0
    hm[64:128, 64] = 1.0
    return hm


def _onesrow2():
    c = np.zeros((128, 64), np.float32)
    c[0, :] = 1.0
    c[64, :] = 1.0
    return c


def _gain_sel(kt):
    selg = np.zeros((16, 128), np.float32)
    selg[2 * kt, 0:64] = 1.0
    selg[2 * kt + 1, 64:128] = 1.0
    return selg


def _dup_mat(half):
    dd = np.zeros((128, 128), np.float32)
    for m in range(128):
        dd[half * 64 + (m % 64), m] = 1.0
    return dd


def _bc_sel(n):
    sel = np.zeros((16, 128), np.float32)
    sel[n, :] = 1.0
    return sel


def _proj_a(g, u, wT, wname, XN, xnr, Wsb, dout, emit_evac):
    mot = dout // 128
    for kt in range(KT):
        g.dma("g", Wsb[:, kt * dout:(kt + 1) * dout],
              wT[kt * 128:(kt + 1) * 128, :], writes=(f"{wname}_{kt}",))
    wreads = tuple(f"{wname}_{kt}" for kt in range(KT))
    for mo in range(mot):
        for tc in range(TC):
            with g.op("t", reads=wreads + (xnr,), writes=("psb0",)) as o:
                for kt in range(KT):
                    o(o.E.matmul(g.ps[0][:, :],
                                 Wsb[:, kt * dout + mo * 128: kt * dout + (mo + 1) * 128],
                                 XN[:, kt * S + tc * TCW: kt * S + (tc + 1) * TCW],
                                 start=(kt == 0), stop=(kt == KT - 1)))
            emit_evac(mo, tc, g.ps[0][:, :])


def emit_softplus(g, out_ap, x_ap, t1, t2, rin, rout, rtmp):
    with g.op("s", reads=(rin,), writes=(rtmp,)) as o:
        o(o.E.activation(t1, x_ap, AF.Abs))
    with g.op("s", reads=(rtmp,), writes=(rtmp + "2",), fence=True) as o:
        o(o.E.activation(t2, t1, AF.Exp, scale=-1.0))
    with g.op("s", reads=(rtmp + "2",), writes=(rtmp + "2",), fence=True) as o:
        o(o.E.activation(t2, t2, AF.Ln, bias=1.0))
    with g.op("v", reads=(rin,), writes=(rout,)) as o:
        o(o.E.tensor_scalar_max(out_ap, x_ap, 0.0))
    with g.op("v", reads=(rout, rtmp + "2"), writes=(rout,)) as o:
        o(o.E.tensor_add(out_ap, out_ap, t2))


def emit_l0_attn_a(g, sl, xfeat, yfeat):
    tr = g.tr
    u = g.u()
    qT = g.dram["t_L0_attn_a_q"]
    kT = g.dram["t_L0_attn_a_k"]
    vT = g.dram["t_L0_attn_a_v"]
    hvgT = g.dram["t_L0_attn_a_hvg"]
    himT = g.dram["t_L0_attn_a_him"]
    hxpT = g.dram["t_L0_attn_a_hxp"]
    hmoT = g.dram["t_L0_attn_a_hmo"]
    outT = g.dram["t_L0_attn_a_out"]
    colsD = g.dram["t_L0_attn_a_cols"]
    rowD = g.dram["t_L0_attn_a_row"]
    cosF, sinF = _rope_tables_a()
    nc = g.nc
    gate_hbm = nc.dram_tensor(f"aa_gate{u}", [D, S], BF16, kind="Internal").ap()
    xs_hbm = nc.dram_tensor(f"aa_xs{u}", [D, S], BF16, kind="Internal").ap()
    u_hbm = nc.dram_tensor(f"aa_u{u}", [D, S], F32, kind="Internal").ap()
    EPS_RMS = 1.1920929e-07

    for b in range(BPC):
        with g.scope():
            cols = g.sb(f"aacol{u}", [128, 112], F32)
            rowi = g.sb(f"aarow{u}", [16, 2], F32)
            g.dma("q", cols[:], colsD, writes=(f"aacol{u}",))
            g.dma("q", rowi[:], rowD, writes=(f"aarow{u}",))
            der = g.sb(f"aader{u}", [128, 96], F32)
            R_DER = f"aader{u}"
            with g.op("s", reads=(f"aacol{u}",), writes=(R_DER,)) as o:
                o(o.E.activation(der[:, 8:72], cols[:, 48:112], AF.Exp))
            with g.op("v", reads=(R_DER,), writes=(R_DER,), fence=True) as o:
                o(o.E.tensor_scalar_mul(der[:, 8:72], der[:, 8:72], -1.0))
            spt1 = g.sb(f"aasp{u}", [128, 16], F32)
            emit_softplus(g, der[:, 0:8], cols[:, 32:40], spt1[:, 0:8], spt1[:, 8:16],
                          f"aacol{u}", R_DER, f"aasp{u}")
            gr32 = g.sb(f"aag32{u}", [16, 2], F32R)
            wsc = g.sb(f"aaw32{u}", [1, 2], F32R)
            with g.op("v", reads=(f"aarow{u}",), writes=(f"aagr{u}",)) as o:
                with g.nc.allow_low_precision(reason="gain f32r"):
                    o.E.tensor_copy(gr32[:, 0:1], rowi[:, 0:1])
                    o(o.E.tensor_copy(gr32[:, 1:2], rowi[:, 0:1]))
            for kt in range(KT):
                selg = g.const_sb(f"c_selg{kt}", _gain_sel(kt), F32R)
                with g.op("t", reads=(f"aagr{u}",), writes=("psb7",)) as o:
                    o(o.E.matmul(g.ps[7][:, 0:2], selg[:], gr32[:],
                                 start=True, stop=True))
                with g.op("v", reads=("psb7",), writes=(R_DER,)) as o:
                    o(o.E.tensor_copy(der[:, 72 + kt:73 + kt], g.ps[7][:, 0:1]))
            with g.op("s", reads=(f"aarow{u}",), writes=(f"aaw{u}",)) as o:
                with g.nc.allow_low_precision(reason="f32r row"):
                    o.E.activation(wsc[:, 0:1], rowi[0:1, 1:2], AF.Sigmoid)
                    o(o.E.activation(wsc[:, 1:2], rowi[0:1, 1:2], AF.Sigmoid))
            with g.op("t", reads=(f"aaw{u}",), writes=("psb7",)) as o:
                o(o.E.matmul(g.ps[7][:, 0:2], g.ones_row(), wsc[:],
                             start=True, stop=True))
            with g.op("v", reads=("psb7",), writes=(R_DER,)) as o:
                o.E.tensor_copy(der[:, 80:81], g.ps[7][:, 0:1])
                o(o.E.tensor_scalar(der[:, 81:82], g.ps[7][:, 0:1], -1.0, 1.0,
                                    ALU.mult, ALU.add))

            ya = g.sb(f"aaya{u}", [128, KT * S], BF16)

            with g.scope():
                Q = g.sb(f"aaQ{u}", [128, KT * S], BF16)
                Kb = g.sb(f"aaK2{u}", [128, 4 * S], BF16)
                Vb = g.sb(f"aaV2{u}", [128, 4 * S], BF16)
                with g.scope():
                    XN = g.sb(f"aaxn{u}", [128, KT * S], BF16)
                    with g.scope():
                        emit_ln_v2(g, xfeat[b], XN, f"aaxn{u}")
                    with g.scope():
                        Wsb = g.sb(f"aaW{u}", [128, KT * 2048], BF16)
                        gtmp = g.sb(f"aagt{u}", [128, TCW], BF16)

                        def ev_him(mo, tc, ps):
                            with g.op("v", reads=("psb0",), writes=(f"aagt{u}",)) as o:
                                with g.nc.allow_low_precision(reason="bf16"):
                                    o(o.E.tensor_copy(gtmp[:], ps))
                            tgt = xs_hbm if mo < 8 else gate_hbm
                            g.dma("q", tgt[(mo % 8) * 128:(mo % 8 + 1) * 128,
                                           tc * TCW:(tc + 1) * TCW],
                                  gtmp[:], reads=(f"aagt{u}",), writes=("gh",))
                        _proj_a(g, u, himT, f"aaW{u}", XN, f"aaxn{u}", Wsb, 2048, ev_him)
                    with g.scope():
                        Wsb = g.sb(f"aaW2{u}", [128, KT * D], BF16)
                        qpre = g.sb(f"aaqp{u}", [128, S], F32R)
                        sq = g.sb(f"aasq{u}", [128, TCW], F32R)
                        rstd_sc = g.sb(f"aars{u}", [128, S], F32)
                        rstdb = g.sb(f"aarb{u}", [128, S], F32)
                        ropet = g.sb(f"aart{u}", [128, S], F32)
                        qsw = rstd_sc
                        cosT = g.sb(f"aacos{u}", [128, S], F32)
                        sinT = g.sb(f"aasin{u}", [128, S], F32)
                        g.dma("g", cosT[:], g.const("c_cosFa", cosF), writes=("c_ropea",))
                        g.dma("g", sinT[:], g.const("c_sinFa", sinF), writes=("c_ropea",))
                        swpa = g.const_sb("c_swapa", _swap_mat_a(), F32R)
                        hmsk = g.const_sb("c_hmask", _hmask(), F32R)
                        or2 = g.const_sb("c_onesrow2f", _onesrow2(), F32)
                        RQP, RSQ, RRS, RRB = (f"aaqp{u}", f"aasq{u}", f"aars{u}",
                                              f"aarb{u}")
                        RRT, RQS = f"aart{u}", f"aars{u}"

                        def tile_finalize(dst, dstr, dcol, gain, mo=0):
                            for tc in range(TC):
                                tsl = slice(tc * TCW, (tc + 1) * TCW)
                                with g.op("s", reads=(RQP,), writes=(RSQ,)) as o:
                                    with g.nc.allow_low_precision(reason="f32r"):
                                        o(o.E.activation(sq[:], qpre[:, tsl].bitcast(F32),
                                                         AF.Square))
                                with g.op("t", reads=(RSQ,), writes=("psb6",)) as o:
                                    o(o.E.matmul(g.ps[6][:, :], hmsk[:], sq[:],
                                                 start=True, stop=True))
                                with g.op("v", reads=("psb6",), writes=(RRS,)) as o:
                                    o(o.E.tensor_copy(rstd_sc[:, tsl], g.ps[6][:, :]))
                            for rb2 in (0, 64):
                                row = rstd_sc[rb2:rb2 + 1, :]
                                with g.op("v", reads=(RRS,), writes=(RRS,)) as o:
                                    o(o.E.tensor_scalar(row, row, 1.0 / 64, EPS_RMS,
                                                        ALU.mult, ALU.add))
                                with g.op("s", reads=(RRS,), writes=(RRS,)) as o:
                                    o(o.E.activation(row, row, AF.Sqrt))
                                with g.op("v", reads=(RRS,), writes=(RRS,), fence=True) as o:
                                    o(o.E.reciprocal(row, row))
                                for tc in range(TC):
                                    tsl = slice(tc * TCW, (tc + 1) * TCW)
                                    with g.op("t", reads=(RRS,), writes=("psb7",)) as o:
                                        o(o.E.matmul(
                                            g.ps[7][rb2:rb2 + 64, :],
                                            or2[rb2:rb2 + 1, :],
                                            rstd_sc[rb2:rb2 + 1, tsl],
                                            start=True, stop=True,
                                            tile_position=(rb2, rb2)))
                                    with g.op("v", reads=("psb7",), writes=(RRB,)) as o:
                                        o(o.E.tensor_copy(rstdb[rb2:rb2 + 64, tsl],
                                                          g.ps[7][rb2:rb2 + 64, :]))
                            for tc in range(TC):
                                tsl = slice(tc * TCW, (tc + 1) * TCW)
                                with g.op("t", reads=(RQP,), writes=("psb1",)) as o:
                                    o(o.E.matmul(g.ps[1][:, :], swpa[:], qpre[:, tsl],
                                                 start=True, stop=True))
                                with g.op("v", reads=("psb1", "c_ropea"),
                                          writes=(RQS,)) as o:
                                    o(o.E.tensor_mul(qsw[:, tsl], g.ps[1][:, :],
                                                     sinT[:, tsl]))
                            with g.op("v", reads=(RQP, RQS, "c_ropea"), writes=(RRT,)) as o:
                                o.E.tensor_mul(ropet[:], qpre[:].bitcast(F32), cosT[:])
                                o(o.E.tensor_add(ropet[:], ropet[:], qsw[:]))
                            with g.op("v", reads=(RRT, RRB, R_DER), writes=(dstr,)) as o:
                                with g.nc.allow_low_precision(reason="bf16"):
                                    if gain:
                                        o.E.tensor_mul(ropet[:], ropet[:], rstdb[:])
                                        o(o.E.tensor_scalar(
                                            dst[:, dcol * S:(dcol + 1) * S], ropet[:],
                                            der[:, 72 + mo:73 + mo], 0.125,
                                            ALU.mult, ALU.mult))
                                    else:
                                        o(o.E.tensor_mul(dst[:, dcol * S:(dcol + 1) * S],
                                                         ropet[:], rstdb[:]))

                        def ev_q(mo, tc, ps):
                            with g.op("v", reads=("psb0",), writes=(RQP,)) as o:
                                with g.nc.allow_low_precision(reason="f32r"):
                                    o(o.E.tensor_copy(qpre[:, tc * TCW:(tc + 1) * TCW], ps))
                            if tc == TC - 1:
                                tile_finalize(Q, f"aaQ{u}", mo, True, mo)

                        _proj_a(g, u, qT, f"aaW2{u}", XN, f"aaxn{u}", Wsb, D, ev_q)

                        def ev_k(mo, tc, ps):
                            with g.op("v", reads=("psb0",), writes=(RQP,)) as o:
                                with g.nc.allow_low_precision(reason="f32r"):
                                    o(o.E.tensor_copy(qpre[:, tc * TCW:(tc + 1) * TCW], ps))
                            if tc == TC - 1:
                                tile_finalize(Kb, f"aaK2{u}", 2 * mo, False)

                        _proj_a(g, u, kT, f"aaW2{u}", XN, f"aaxn{u}", Wsb, 256, ev_k)
                        for kt in range(KT):
                            g.dma("g", Wsb[:, kt * 256:(kt + 1) * 256],
                                  vT[kt * 128:(kt + 1) * 128, :], writes=(f"aaW2{u}_{kt}",))
                            g.dma("g", Wsb[:, KT * 256 + kt * 256: KT * 256 + (kt + 1) * 256],
                                  hvgT[kt * 128:(kt + 1) * 128, :],
                                  writes=(f"aaW2{u}_h{kt}",))
                        vwr = tuple(f"aaW2{u}_{k}" for k in range(KT))
                        hwr = tuple(f"aaW2{u}_h{k}" for k in range(KT))
                        sig = ropet
                        for mo in range(2):
                            for tc in range(TC):
                                with g.op("t", reads=vwr + (f"aaxn{u}",),
                                          writes=("psb0",)) as o:
                                    for kt in range(KT):
                                        o(o.E.matmul(
                                            g.ps[0][:, :],
                                            Wsb[:, kt * 256 + mo * 128: kt * 256 + (mo + 1) * 128],
                                            XN[:, kt * S + tc * TCW: kt * S + (tc + 1) * TCW],
                                            start=(kt == 0), stop=(kt == KT - 1)))
                                with g.op("t", reads=hwr + (f"aaxn{u}",),
                                          writes=("psb1",)) as o:
                                    for kt in range(KT):
                                        o(o.E.matmul(
                                            g.ps[1][:, :],
                                            Wsb[:, KT * 256 + kt * 256 + mo * 128:
                                                KT * 256 + kt * 256 + (mo + 1) * 128],
                                            XN[:, kt * S + tc * TCW: kt * S + (tc + 1) * TCW],
                                            start=(kt == 0), stop=(kt == KT - 1)))
                                with g.op("s", reads=("psb1",), writes=(f"aart{u}",)) as o:
                                    o(o.E.activation(sig[:, 0:TCW], g.ps[1][:, :],
                                                     AF.Sigmoid))
                                with g.op("v", reads=("psb0", f"aart{u}"),
                                          writes=(f"aaV2{u}",)) as o:
                                    with g.nc.allow_low_precision(reason="bf16"):
                                        o(o.E.tensor_mul(
                                            Vb[:, 2 * mo * S + tc * TCW:
                                               2 * mo * S + (tc + 1) * TCW],
                                            g.ps[0][:, :], sig[:, 0:TCW]))
                        dup0 = g.const_sb("c_dup0", _dup_mat(0), BF16)
                        dup1 = g.const_sb("c_dup1", _dup_mat(1), BF16)
                        for gkv in (1, 3, 0, 2):
                            dup = (dup0, dup1)[gkv % 2]
                            st = (gkv // 2) * 2
                            for tc in range(TC):
                                tsl0 = slice(st * S + tc * TCW, st * S + (tc + 1) * TCW)
                                tsl1 = slice(gkv * S + tc * TCW, gkv * S + (tc + 1) * TCW)
                                for dstb, rdst in ((Kb, f"aaK2{u}"), (Vb, f"aaV2{u}")):
                                    with g.op("t", reads=(rdst,), writes=("psb2",)) as o:
                                        o(o.E.matmul(g.ps[2][:, :], dup[:],
                                                     dstb[:, tsl0], start=True, stop=True))
                                    with g.op("v", reads=("psb2",), writes=(rdst,)) as o:
                                        with g.nc.allow_low_precision(reason="bf16"):
                                            o(o.E.tensor_copy(dstb[:, tsl1], g.ps[2][:, :]))
                with g.scope():
                    emit_attn_core(g, u, Q, Kb, Vb, ya, f"aaQ{u}", f"aaK2{u}",
                                   f"aaV2{u}", f"aaya{u}",
                                   lambda h: (h // 4, (h % 2) * 64), 1.0,
                                   np.triu(np.full((128, 128), -3e4, np.float32), 1),
                                   "c_maska")

            Y = g.sb(f"aaY{u}", [128, KT * S], BF16)
            with g.scope():
                dt_sb = g.sb(f"aadt{u}", [64, S], F32)
                bc_sb = g.sb(f"aabc{u}", [16, S], BF16)
                with g.scope():
                    xkt = g.sb(f"aaxk{u}", [128, 2050], BF16)
                    ukt = g.sb(f"aau{u}", [128, S], F32R)
                    cva = g.sb(f"aacv{u}", [128, S], F32)
                    cvb = g.sb(f"aacvb{u}", [128, S], F32)
                    hxA = g.sb(f"aahxA{u}", [128, KT * 64], F32R)
                    hxB = g.sb(f"aahxB{u}", [128, KT * 16], F32R)
                    for kt in range(KT):
                        g.dma("g", hxA[:, kt * 64:(kt + 1) * 64],
                              hxpT[kt * 128:(kt + 1) * 128, 0:64], writes=(f"aahx{u}",))
                        g.dma("g", hxB[:, kt * 16:(kt + 1) * 16],
                              hxpT[kt * 128:(kt + 1) * 128, 64:80], writes=(f"aahx{u}",))
                    RU, RCV = f"aau{u}", f"aacv{u}"
                    with g.op("v", reads=(), writes=(f"aaxk{u}",)) as o:
                        o(o.E.memset(xkt[:, 0:2], 0.0))
                    for kt in range(KT):
                        g.dma("q", xkt[:, 2:2 + S], xs_hbm[kt * 128:(kt + 1) * 128, :],
                              reads=("gh",), writes=(f"aaxk{u}",))
                        xs0 = xkt[:, 0: S]
                        xs1 = xkt[:, 1: 1 + S]
                        xs2 = xkt[:, 2: 2 + S]
                        w0c, w1c, w2c, cbc = (cols[:, kt:kt + 1], cols[:, 8 + kt:9 + kt],
                                              cols[:, 16 + kt:17 + kt], cols[:, 24 + kt:25 + kt])
                        with g.op("v", reads=(f"aaxk{u}",), writes=(RCV,)) as o:
                            o.E.tensor_scalar_mul(cva[:], xs0, w0c)
                            o.E.tensor_scalar_mul(cvb[:], xs1, w1c)
                            o.E.tensor_add(cva[:], cva[:], cvb[:])
                            o.E.tensor_scalar_mul(cvb[:], xs2, w2c)
                            o(o.E.tensor_add(cva[:], cva[:], cvb[:]))
                        with g.op("s", reads=(RCV,), writes=(RU,)) as o:
                            with g.nc.allow_low_precision(reason="f32r"):
                                o(o.E.activation(ukt[:], cva[:], AF.Silu, bias=cbc))
                        g.dma("q", u_hbm[kt * 128:(kt + 1) * 128, :], ukt[:].bitcast(F32),
                              reads=(RU,), writes=("uh",))
                        for tc in range(TC):
                            tsl = slice(tc * TCW, (tc + 1) * TCW)
                            with g.op("t", reads=(RU, f"aahx{u}"),
                                      writes=(f"psb{tc}", f"psb{4 + tc}")) as o:
                                o.E.matmul(g.ps[tc][0:64, :],
                                           hxA[:, kt * 64:(kt + 1) * 64], ukt[:, tsl],
                                           start=(kt == 0), stop=(kt == KT - 1))
                                o(o.E.matmul(g.ps[4 + tc][0:16, :],
                                             hxB[:, kt * 16:(kt + 1) * 16], ukt[:, tsl],
                                             start=(kt == 0), stop=(kt == KT - 1)))
                for tc in range(TC):
                    tsl = slice(tc * TCW, (tc + 1) * TCW)
                    with g.op("v", reads=(f"psb{tc}", f"psb{4 + tc}"),
                              writes=(f"aadt{u}", f"aabc{u}")) as o:
                        o.E.tensor_copy(dt_sb[:, tsl], g.ps[tc][0:64, :])
                        with g.nc.allow_low_precision(reason="bf16"):
                            o(o.E.tensor_copy(bc_sb[:, tsl], g.ps[4 + tc][0:16, :]))
                delta0 = g.sb(f"aad0{u}", [64, S], F32)
                with g.scope():
                    d0a = g.sb(f"aad0a{u}", [64, S], F32)
                    d0b = g.sb(f"aad0b{u}", [64, S], F32)
                    with g.op("v", reads=(f"aadt{u}", f"aacol{u}"), writes=(f"aadt{u}",)) as o:
                        o(o.E.tensor_scalar_add(dt_sb[:], dt_sb[:], cols[0:64, 32:33]))
                    emit_softplus(g, delta0[:], dt_sb[:], d0a[:], d0b[:],
                                  f"aadt{u}", f"aad0{u}", f"aad0t{u}")
                delta = g.sb(f"aadel{u}", [128, S], F32)
                du = g.sb(f"aadu{u}", [128, S], F32)
                uktf = g.sb(f"aauf{u}", [128, S], F32)
                asb = g.sb(f"aaa{u}", [128, S], F32)
                bt = g.sb(f"aabt{u}", [128, S], F32)
                hsb = g.sb(f"aah{u}", [128, S], F32)
                ysb = g.sb(f"aay{u}", [128, S], F32)
                gkt = g.sb(f"aagk{u}", [128, S], BF16)
                gsl = g.sb(f"aagsl{u}", [128, S], F32)
                RDL, RDU, RUF = f"aadel{u}", f"aadu{u}", f"aauf{u}"
                RA, RBT, RH, RY = f"aaa{u}", f"aabt{u}", f"aah{u}", f"aay{u}"
                for n in range(16):
                    g.const_sb(f"c_seln{n}", _bc_sel(n), BF16)
                for kt in range(KT):
                    g.dma("q", uktf[:], u_hbm[kt * 128:(kt + 1) * 128, :],
                          reads=("uh",), writes=(RUF,))
                    g.dma("q", gkt[:], gate_hbm[kt * 128:(kt + 1) * 128, :],
                          reads=("gh",), writes=(f"aagk{u}",))
                    with g.op("v", reads=(RUF, R_DER), writes=(RDL,)) as o:
                        o(o.E.tensor_scalar(delta[:], uktf[:], 0.0,
                                            der[:, kt:kt + 1], ALU.mult, ALU.add))
                    if kt == 0:
                        with g.op("v", reads=(f"aad0{u}",), writes=(RDL,)) as o:
                            o(o.E.tensor_copy(delta[0:64, :], delta0[:]))
                    with g.op("v", reads=(RDL, RUF), writes=(RDU,)) as o:
                        o(o.E.tensor_mul(du[:], delta[:], uktf[:]))
                    for n in range(8):
                        selB = g.consts[("sb", f"c_seln{n}")]
                        selC = g.consts[("sb", f"c_seln{8 + n}")]
                        for tc in range(TC):
                            tsl = slice(tc * TCW, (tc + 1) * TCW)
                            with g.op("t", reads=(f"aabc{u}",),
                                      writes=(f"psb{tc}", f"psb{4 + tc}")) as o:
                                o.E.matmul(g.ps[tc][:, :], selB[:], bc_sb[:, tsl],
                                           start=True, stop=True)
                                o(o.E.matmul(g.ps[4 + tc][:, :], selC[:], bc_sb[:, tsl],
                                             start=True, stop=True))
                        with g.op("s", reads=(RDL, R_DER), writes=(RA,)) as o:
                            o(o.E.activation(asb[:], delta[:], AF.Exp,
                                             scale=der[:, 8 + kt * 8 + n: 9 + kt * 8 + n]))
                        for tc in range(TC):
                            tsl = slice(tc * TCW, (tc + 1) * TCW)
                            with g.op("v", reads=(RDU, f"psb{tc}"), writes=(RBT,)) as o:
                                o(o.E.tensor_mul(bt[:, tsl], du[:, tsl], g.ps[tc][:, :]))
                        with g.op("v", reads=(RA, RBT), writes=(RH,)) as o:
                            o(o.E.tensor_tensor_scan(hsb[:], asb[:], bt[:], 0.0,
                                                     ALU.mult, ALU.add))
                        for tc in range(TC):
                            tsl = slice(tc * TCW, (tc + 1) * TCW)
                            if n == 0:
                                with g.op("v", reads=(RH, f"psb{4 + tc}"),
                                          writes=(RY,)) as o:
                                    o(o.E.tensor_mul(ysb[:, tsl], hsb[:, tsl],
                                                     g.ps[4 + tc][:, :]))
                            else:
                                with g.op("v", reads=(RH, f"psb{4 + tc}"),
                                          writes=(RBT,)) as o:
                                    o(o.E.tensor_mul(bt[:, tsl], hsb[:, tsl],
                                                     g.ps[4 + tc][:, :]))
                                with g.op("v", reads=(RBT,), writes=(RY,)) as o:
                                    o(o.E.tensor_add(ysb[:, tsl], ysb[:, tsl],
                                                     bt[:, tsl]))
                    with g.op("v", reads=(RUF, RY, f"aacol{u}"), writes=(RY,)) as o:
                        o(o.E.scalar_tensor_tensor(ysb[:], uktf[:],
                                                   cols[:, 40 + kt:41 + kt], ysb[:],
                                                   ALU.mult, ALU.add))
                    with g.op("s", reads=(f"aagk{u}",), writes=(f"aagsl{u}",)) as o:
                        o(o.E.activation(gsl[:], gkt[:], AF.Silu))
                    with g.op("v", reads=(RY, f"aagsl{u}"), writes=(f"aaY{u}",)) as o:
                        with g.nc.allow_low_precision(reason="bf16"):
                            o(o.E.tensor_mul(Y[:, kt * S:(kt + 1) * S], ysb[:], gsl[:]))

            with g.scope():
                Wsb3 = g.sb(f"aaW3{u}", [128, KT * D], BF16)
                m1 = g.sb(f"aam1{u}", [128, TCW], F32)
                m2 = g.sb(f"aam2{u}", [128, TCW], F32)
                RM = f"aam{u}"

                def ev_ym(mo, tc, ps):
                    ysl = slice(mo * S + tc * TCW, mo * S + (tc + 1) * TCW)
                    with g.op("v", reads=("psb0", f"aaya{u}", R_DER),
                              writes=(RM,)) as o:
                        o.E.tensor_scalar_mul(m1[:], ps, der[:, 81:82])
                        o(o.E.tensor_scalar_mul(m2[:], ya[:, ysl], der[:, 80:81]))
                    with g.op("v", reads=(RM,), writes=(f"aaya{u}",)) as o:
                        with g.nc.allow_low_precision(reason="bf16"):
                            o(o.E.tensor_add(ya[:, ysl], m1[:], m2[:]))

                _proj_a(g, u, hmoT, f"aaW3{u}", Y, f"aaY{u}", Wsb3, D, ev_ym)
                xres = g.sb(f"aaxr{u}", [128, TCW], F32)
                yt = g.sb(f"aayt{u}", [128, TCW], F32)

                def ev_out(mo, tc, ps):
                    g.dma("q", xres[:], xfeat[b, mo * 128:(mo + 1) * 128,
                                              tc * TCW:(tc + 1) * TCW],
                          reads=("xf",), writes=(f"aaxr{u}",))
                    with g.op("v", reads=("psb0", f"aaxr{u}"), writes=(f"aayt{u}",)) as o:
                        o(o.E.tensor_add(yt[:], ps, xres[:]))
                    g.dma("q", yfeat[b, mo * 128:(mo + 1) * 128,
                                     tc * TCW:(tc + 1) * TCW], yt[:],
                          reads=(f"aayt{u}",), writes=("yf",))

                _proj_a(g, u, outT, f"aaW3{u}", ya, f"aaya{u}", Wsb3, D, ev_out)
        tr.barrier()


SUBLAYER_EMIT["L0_attn_a"] = emit_l0_attn_a


def emit_l1_attn_a(g, sl, xfeat, yfeat):
    tr = g.tr
    u = g.u()
    WvT = g.dram["t_L1_attn_a_Wv"]
    WoT = g.dram["t_L1_attn_a_Wo"]
    WgT = g.dram["t_L1_attn_a_Wg"]
    WbT = g.dram["t_L1_attn_a_Wb"]
    L = 128
    NCH = S // L

    maskSL = np.tril(np.ones((L, L), np.float32), -1)
    maskTI = np.triu(np.ones((L, L), np.float32), 0)

    for b in range(BPC):
        with g.scope():
            RT = g.sb(f"daR{u}", [128, KT * (S + 1)], BF16)
            VT = g.sb(f"daV{u}", [128, KT * S], BF16)
            GB = g.sb(f"daGB{u}", [128, 2 * 16 * 16], F32)
            SSf = g.sb(f"daSf{u}", [128, 16 * 64], F32)
            SSb = g.sb(f"daSb{u}", [128, 16 * 64], BF16)
            R_RT, R_VT, R_OT, R_GB = f"daR{u}", f"daV{u}", f"daO{u}", f"daGB{u}"
            R_SF, R_SB = f"daSf{u}", f"daSb{u}"
            with g.op("v", reads=(), writes=(R_SF, R_SB, R_RT)) as o:
                o.E.memset(SSf[:], 0.0)
                with g.nc.allow_low_precision(reason="bf16"):
                    o.E.memset(SSb[:], 0.0)
                    for kt in range(KT):
                        o(o.E.memset(RT[:, kt * (S + 1):kt * (S + 1) + 1], 0.0))
            with g.scope():
                XN = g.sb(f"daxn{u}", [128, KT * S], F32R)
                with g.scope():
                    emit_ln_v2(g, xfeat[b], XN, f"daxn{u}")
                with g.scope():
                    sq = g.sb(f"dasq{u}", [128, TCW], F32R)
                    rstd_sc = g.sb(f"dars{u}", [128, S], F32)
                    hmsk = g.consts.get(("sb", "c_hmask")) or g.const_sb(
                        "c_hmask", _hmask(), F32R)
                    or2 = g.consts.get(("sb", "c_onesrow2f")) or g.const_sb(
                        "c_onesrow2f", _onesrow2(), F32)
                    RSQ, RRS = f"dasq{u}", f"dars{u}"
                    for kt in range(KT):
                        for tc in range(TC):
                            tsl = slice(tc * TCW, (tc + 1) * TCW)
                            with g.op("s", reads=(f"daxn{u}",), writes=(RSQ,)) as o:
                                with g.nc.allow_low_precision(reason="f32r"):
                                    o(o.E.activation(
                                        sq[:], XN[:, kt * S + tc * TCW:
                                                  kt * S + (tc + 1) * TCW].bitcast(F32),
                                        AF.Square))
                            with g.op("t", reads=(RSQ,), writes=("psb6",)) as o:
                                o(o.E.matmul(g.ps[6][:, :], hmsk[:], sq[:],
                                             start=True, stop=True))
                            with g.op("v", reads=("psb6",), writes=(RRS,)) as o:
                                o(o.E.tensor_copy(rstd_sc[:, tsl], g.ps[6][:, :]))
                        for rb2 in (0, 64):
                            row = rstd_sc[rb2:rb2 + 1, :]
                            with g.op("s", reads=(RRS,), writes=(RRS,)) as o:
                                o(o.E.activation(row, row, AF.Sqrt))
                            with g.op("v", reads=(RRS,), writes=(RRS,), fence=True) as o:
                                o.E.tensor_scalar_max(row, row, 1e-12)
                                o(o.E.reciprocal(row, row))
                            for tc in range(TC):
                                tsl = slice(tc * TCW, (tc + 1) * TCW)
                                with g.op("t", reads=(RRS,), writes=("psb7",)) as o:
                                    o(o.E.matmul(g.ps[7][rb2:rb2 + 64, :],
                                                 or2[rb2:rb2 + 1, :],
                                                 rstd_sc[rb2:rb2 + 1, tsl],
                                                 start=True, stop=True,
                                                 tile_position=(rb2, rb2)))
                                with g.op("v", reads=("psb7", f"daxn{u}"),
                                          writes=(R_RT,)) as o:
                                    with g.nc.allow_low_precision(reason="bf16"):
                                        o(o.E.tensor_mul(
                                            RT[rb2:rb2 + 64,
                                               kt * (S + 1) + 1 + tc * TCW:
                                               kt * (S + 1) + 1 + (tc + 1) * TCW],
                                            XN[rb2:rb2 + 64, kt * S + tc * TCW:
                                               kt * S + (tc + 1) * TCW].bitcast(F32),
                                            g.ps[7][rb2:rb2 + 64, :]))
                with g.scope():
                    Wsb = g.sb(f"daW{u}", [128, KT * D], F32R)

                    def ev_v(mo, tc, ps):
                        with g.op("v", reads=("psb0",), writes=(R_VT,)) as o:
                            with g.nc.allow_low_precision(reason="bf16"):
                                o(o.E.tensor_copy(
                                    VT[:, mo * S + tc * TCW: mo * S + (tc + 1) * TCW],
                                    ps))
                    _proj_a(g, u, WvT, f"daW{u}", XN, f"daxn{u}", Wsb, D, ev_v)
                with g.scope():
                    Wgb = g.sb(f"dawg{u}", [128, KT * 32], F32R)
                    gbt = g.sb(f"dagt{u}", [16, TCW], F32R)
                    idf = g.const_sb("c_idf32", np.eye(128, dtype=np.float32), F32R)
                    for kt in range(KT):
                        g.dma("g", Wgb[:, kt * 32:kt * 32 + 16],
                              WgT[kt * 128:(kt + 1) * 128, :], writes=(f"dawg{u}",))
                        g.dma("g", Wgb[:, kt * 32 + 16:kt * 32 + 32],
                              WbT[kt * 128:(kt + 1) * 128, :], writes=(f"dawg{u}",))
                    for which in range(2):
                        for tc in range(TC):
                            with g.op("t", reads=(f"dawg{u}", f"daxn{u}"),
                                      writes=("psb0",)) as o:
                                for kt in range(KT):
                                    o(o.E.matmul(
                                        g.ps[0][0:16, :],
                                        Wgb[:, kt * 32 + 16 * which:
                                            kt * 32 + 16 * which + 16],
                                        XN[:, kt * S + tc * TCW: kt * S + (tc + 1) * TCW],
                                        start=(kt == 0), stop=(kt == KT - 1)))
                            with g.op("s", reads=("psb0",), writes=(f"dagt{u}",)) as o:
                                with g.nc.allow_low_precision(reason="f32r"):
                                    o(o.E.activation(gbt[:], g.ps[0][0:16, :], AF.Sigmoid))
                            for j in range(4):
                                blk = tc * 4 + j
                                with g.op("t", reads=(f"dagt{u}",), writes=("psb1",)) as o:
                                    o(o.E.transpose(
                                        g.ps[1][:].bitcast(F32R)[:, 0:16],
                                        gbt[:, j * 128:(j + 1) * 128], idf[0:16, 0:16]))
                                with g.op("v", reads=("psb1",), writes=(R_GB,)) as o:
                                    o(o.E.tensor_copy(
                                        GB[:, which * 256 + blk * 16:
                                           which * 256 + blk * 16 + 16],
                                        g.ps[1][:].bitcast(F32R)[:, 0:16]))
            OT = g.sb(f"daO{u}", [128, KT * S], BF16)
            with g.scope():
                idbf = g.const_sb("c_idbf", np.eye(128, dtype=np.float32), BF16)
                id64s = g.const_sb("c_id64s", np.concatenate(
                    [np.eye(64, dtype=np.float32)] * 2, axis=0), BF16)
                idr = g.const_sb("c_idr", np.eye(128, dtype=np.float32), F32R)
                mSL = g.const_sb("c_maskSL", maskSL, F32)
                mTI = g.const_sb("c_maskTI", maskTI, F32)
                Bp = g.sb(f"daB{u}", [128, 128], F32R)
                Btmp = g.sb(f"daBm{u}", [128, 128], F32)
                BpT = g.sb(f"daBT{u}", [128, 128], F32R)
                BT_all = g.sb(f"daBTs{u}", [128, 7 * 128], F32R)
                Ut = g.sb(f"daU{u}", [128, 64], F32)
                Ub = g.sb(f"daUb{u}", [128, 64], BF16)
                Ur = g.sb(f"daUr{u}", [128, 64], F32R)
                Vt32 = g.sb(f"daVt{u}", [128, 64], F32)
                GTm = g.sb(f"daG{u}", [128, 128], F32R)
                Wtok = g.sb(f"daWt{u}", [128, 64], BF16)
                Otok = g.sb(f"daOt{u}", [128, 64], BF16)
                RB, RBT, RBS = f"daB{u}", f"daBT{u}", f"daBTs{u}"
                RU, RUB, RURR = f"daU{u}", f"daUb{u}", f"daUr{u}"
                RVt, RG, RWt, ROt = f"daVt{u}", f"daG{u}", f"daWt{u}", f"daOt{u}"
                for h in range(16):
                    ht, rb = h // 2, (h % 2) * 64
                    for c in range(NCH):
                        wsl = slice(ht * (S + 1) + c * L, ht * (S + 1) + (c + 1) * L)
                        rsl = slice(ht * (S + 1) + 1 + c * L, ht * (S + 1) + 1 + (c + 1) * L)
                        bcol = GB[:, 256 + c * 16 + h:256 + c * 16 + h + 1]
                        gcol = GB[:, c * 16 + h:c * 16 + h + 1]
                        ssl = slice(h * 64, (h + 1) * 64)
                        with g.op("t", reads=(R_RT,), writes=("psb0",)) as o:
                            o(o.E.matmul(g.ps[0][:, :128], RT[rb:rb + 64, wsl],
                                         RT[rb:rb + 64, wsl], start=True, stop=True))
                        with g.op("v", reads=("psb0", R_GB), writes=(RB + "m",)) as o:
                            o(o.E.tensor_mul(Btmp[:], g.ps[0][:, :128], mSL[:]))
                        with g.op("v", reads=(RB + "m", R_GB), writes=(RB,),
                                  fence=True) as o:
                            with g.nc.allow_low_precision(reason="f32r"):
                                o(o.E.tensor_scalar_mul(Bp[:], Btmp[:], bcol))
                        with g.op("t", reads=(RB,), writes=("psb1",)) as o:
                            o(o.E.matmul(g.ps[1][:, :128], Bp[:], idr[:],
                                         start=True, stop=True))
                        with g.op("v", reads=("psb1",), writes=(RBT, RBS)) as o:
                            with g.nc.allow_low_precision(reason="f32r"):
                                o.E.tensor_copy(BpT[:], g.ps[1][:, :128])
                                o(o.E.tensor_copy(BT_all[:, 0:128], g.ps[1][:, :128]))
                        for j in range(1, 7):
                            with g.op("t", reads=(RB, RBT), writes=("psb2", "psb3")) as o:
                                o.E.matmul(g.ps[2][:, :128], BpT[:], Bp[:],
                                           start=True, stop=True)
                                o(o.E.matmul(g.ps[3][:, :128], Bp[:], BpT[:],
                                             start=True, stop=True))
                            with g.op("v", reads=("psb2", "psb3"),
                                      writes=(RB, RBT, RBS)) as o:
                                with g.nc.allow_low_precision(reason="f32r"):
                                    o.E.tensor_copy(Bp[:], g.ps[2][:, :128])
                                    o.E.tensor_copy(BpT[:], g.ps[3][:, :128])
                                    o(o.E.tensor_copy(BT_all[:, j * 128:(j + 1) * 128],
                                                      g.ps[3][:, :128]))
                        with g.op("t", reads=(R_VT,), writes=("psb4",)) as o:
                            o(o.E.transpose(
                                g.ps[4][:].bitcast(BF16)[:, 0:64],
                                VT[rb:rb + 64, ht * S + c * L: ht * S + (c + 1) * L],
                                (g.consts[("sb", "c_id64s")][rb:rb + 64, 0:64])))
                        with g.op("v", reads=("psb4",), writes=(RVt,)) as o:
                            o(o.E.tensor_copy(Vt32[:], g.ps[4][:].bitcast(BF16)[:, 0:64]))
                        with g.op("t", reads=(R_RT, R_SB), writes=("psb5",)) as o:
                            o(o.E.matmul(g.ps[5][:, 0:64], RT[rb:rb + 64, wsl],
                                         SSb[rb:rb + 64, ssl], start=True, stop=True))
                        with g.op("v", reads=("psb5", RVt, R_GB), writes=(RU,)) as o:
                            o(o.E.tensor_sub(Ut[:], Vt32[:], g.ps[5][:, 0:64]))
                        with g.op("v", reads=(RU, R_GB), writes=(RU,), fence=True) as o:
                            o(o.E.tensor_scalar_mul(Ut[:], Ut[:], bcol))
                        for j in range(6, -1, -1):
                            with g.op("v", reads=(RU,), writes=(RURR,)) as o:
                                with g.nc.allow_low_precision(reason="f32r"):
                                    o(o.E.tensor_copy(Ur[:], Ut[:]))
                            with g.op("t", reads=(RURR, RBS), writes=("psb2",)) as o:
                                o(o.E.matmul(g.ps[2][:, 0:64],
                                             BT_all[:, j * 128:(j + 1) * 128], Ur[:],
                                             start=True, stop=True))
                            with g.op("v", reads=("psb2",), writes=(RU,)) as o:
                                if j == 0:
                                    o(o.E.tensor_sub(Ut[:], Ut[:], g.ps[2][:, 0:64]))
                                else:
                                    o(o.E.tensor_add(Ut[:], Ut[:], g.ps[2][:, 0:64]))
                        with g.op("t", reads=(R_RT,), writes=("psb0",)) as o:
                            o(o.E.matmul(g.ps[0][:, :128], RT[rb:rb + 64, wsl],
                                         RT[rb:rb + 64, rsl], start=True, stop=True))
                        with g.op("v", reads=("psb0",), writes=(RG,)) as o:
                            with g.nc.allow_low_precision(reason="f32r"):
                                o(o.E.tensor_mul(GTm[:], g.ps[0][:, :128], mTI[:]))
                        with g.op("v", reads=(RU,), writes=(RURR, RUB)) as o:
                            with g.nc.allow_low_precision(reason="f32r"):
                                o.E.tensor_copy(Ur[:], Ut[:])
                                o(o.E.tensor_copy(Ub[:], Ut[:]))
                        with g.op("t", reads=(RG, RURR, R_RT, R_SB), writes=("psb5",)) as o:
                            o.E.matmul(g.ps[5][:, 0:64], GTm[:], Ur[:],
                                       start=True, stop=False)
                            o(o.E.matmul(g.ps[5][:, 0:64], RT[rb:rb + 64, rsl],
                                         SSb[rb:rb + 64, ssl], start=False, stop=True))
                        with g.op("v", reads=("psb5", R_GB), writes=(ROt,)) as o:
                            with g.nc.allow_low_precision(reason="bf16"):
                                o(o.E.tensor_scalar_mul(Otok[:], g.ps[5][:, 0:64], gcol))
                        with g.op("t", reads=(ROt,), writes=("psb6",)) as o:
                            o(o.E.transpose(g.ps[6][:].bitcast(BF16)[rb:rb + 64, 0:128],
                                            Otok[:], idbf[:, :], tile_position=(0, rb)))
                        with g.op("v", reads=("psb6",), writes=(R_OT,)) as o:
                            with g.nc.allow_low_precision(reason="bf16"):
                                o(o.E.tensor_copy(
                                    OT[rb:rb + 64, ht * S + c * L: ht * S + (c + 1) * L],
                                    g.ps[6][:].bitcast(BF16)[rb:rb + 64, 0:128]))
                        with g.op("t", reads=(R_RT,), writes=("psb7",)) as o:
                            o(o.E.transpose(
                                g.ps[7][:].bitcast(BF16)[:, 0:64],
                                RT[rb:rb + 64, wsl],
                                g.consts[("sb", "c_id64s")][rb:rb + 64, 0:64]))
                        with g.op("v", reads=("psb7",), writes=(RWt,)) as o:
                            with g.nc.allow_low_precision(reason="bf16"):
                                o(o.E.tensor_copy(Wtok[:], g.ps[7][:].bitcast(BF16)[:, 0:64]))
                        with g.op("t", reads=(RWt, RUB), writes=("psb3",)) as o:
                            o.E.matmul(g.ps[3][0:64, 0:64], Wtok[:], Ub[:],
                                       start=True, stop=True)
                            o(o.E.matmul(g.ps[3][64:128, 0:64], Wtok[:], Ub[:],
                                         start=True, stop=True, tile_position=(0, 64)))
                        with g.op("v", reads=("psb3", R_SF), writes=(R_SF,),
                                  fence=True) as o:
                            o(o.E.tensor_add(SSf[:, ssl], SSf[:, ssl],
                                             g.ps[3][:, 0:64]))
                        with g.op("v", reads=(R_SF,), writes=(R_SB,), fence=True) as o:
                            with g.nc.allow_low_precision(reason="bf16"):
                                o(o.E.tensor_copy(SSb[:, ssl], SSf[:, ssl]))
            with g.scope():
                Wsb2 = g.sb(f"daW2{u}", [128, KT * D], BF16)
                xres = g.sb(f"daxr{u}", [128, TCW], F32)
                yt = g.sb(f"dayt{u}", [128, TCW], F32)

                def ev_out(mo, tc, ps):
                    g.dma("q", xres[:], xfeat[b, mo * 128:(mo + 1) * 128,
                                              tc * TCW:(tc + 1) * TCW],
                          reads=("xf",), writes=(f"daxr{u}",))
                    with g.op("v", reads=("psb0", f"daxr{u}"), writes=(f"dayt{u}",)) as o:
                        o(o.E.tensor_add(yt[:], ps, xres[:]))
                    g.dma("q", yfeat[b, mo * 128:(mo + 1) * 128,
                                     tc * TCW:(tc + 1) * TCW], yt[:],
                          reads=(f"dayt{u}",), writes=("yf",))
                _proj_a(g, u, WoT, f"daW2{u}", OT, f"daO{u}", Wsb2, D, ev_out)
        tr.barrier()


SUBLAYER_EMIT["L1_attn_a"] = emit_l1_attn_a


def emit_l1_ffn(g, sl, xfeat, yfeat, inst):
    tr = g.tr
    u = g.u()
    nc = g.nc
    ihT = g.dram["t_L1_ffn_ih"]
    hhT = g.dram["t_L1_ffn_hh"]
    exprT = g.dram["t_L1_ffn_expr"]
    ctxT = g.dram["t_L1_ffn_ctx"]
    ctxb = g.dram["t_L1_ffn_ctxb"]
    gwT = g.dram["t_L1_ffn_gate"]
    uwT = g.dram["t_L1_ffn_up"]
    dwT = g.dram["t_L1_ffn_down"]
    sgT = g.dram["t_L1_ffn_sg"]
    suT = g.dram["t_L1_ffn_su"]
    sdT = g.dram["t_L1_ffn_sd"]
    G3 = 3072
    xw_hbm = nc.dram_tensor(f"lf_xw{u}_{inst}", [S, 2, G3], F32, kind="Internal").ap()
    hs_hbm = nc.dram_tensor(f"lf_hs{u}_{inst}", [2, D, S], F32, kind="Internal").ap()

    XN0 = g.sb(f"lfxn0{u}", [128, KT * S], BF16)
    XN1 = g.sb(f"lfxn1{u}", [128, KT * S], BF16)
    XNs = [XN0, XN1]
    RXN = [f"lfxn0{u}", f"lfxn1{u}"]
    for b in range(BPC):
        with g.scope():
            emit_ln_v2(g, xfeat[b], XNs[b], RXN[b])

    hrow = g.sb(f"lfhr{u}", [2, D], F32)
    h_sb = g.sb(f"lfhsb{u}", [128, 2 * KT], BF16)
    R_HR, R_HSB = f"lfhr{u}", f"lfhsb{u}"
    idr = g.const_sb("c_idr", np.eye(128, dtype=np.float32), F32R)

    with g.scope():
        ctx_sb = g.sb(f"lfctx{u}", [128, KT * D], F32R)
        ctb = g.sb(f"lfctb{u}", [128, KT], F32)
        xmean = g.sb(f"lfxm{u}", [128, 2 * KT], F32R)
        h0t = g.sb(f"lfh0t{u}", [128, 2], F32R)
        g.dma("q", ctb[:], ctxb, writes=(f"lfctb{u}",))
        for kt in range(KT):
            g.dma("g", ctx_sb[:, kt * D:(kt + 1) * D],
                  ctxT[kt * 128:(kt + 1) * 128, :], writes=(f"lfctx{u}",))
        for b in range(BPC):
            with g.op("v", reads=(RXN[b],), writes=(f"lfxm{u}",)) as o:
                for kt in range(KT):
                    with g.nc.allow_low_precision(reason="f32r"):
                        o(o.E.reduce_sum(xmean[:, 2 * kt + b:2 * kt + b + 1],
                                         XNs[b][:, kt * S:(kt + 1) * S], axis=AX.X))
        with g.op("v", reads=(f"lfxm{u}",), writes=(f"lfxm{u}",), fence=True) as o:
            with g.nc.allow_low_precision(reason="f32r"):
                o(o.E.tensor_scalar_mul(xmean[:], xmean[:].bitcast(F32), 1.0 / S))
        for mo in range(KT):
            with g.op("t", reads=(f"lfctx{u}", f"lfxm{u}"), writes=("psb0",)) as o:
                for kt in range(KT):
                    o(o.E.matmul(g.ps[0][:, 0:2],
                                 ctx_sb[:, kt * D + mo * 128: kt * D + (mo + 1) * 128],
                                 xmean[:, 2 * kt:2 * kt + 2],
                                 start=(kt == 0), stop=(kt == KT - 1)))
            with g.op("v", reads=("psb0", f"lfctb{u}"),
                      writes=(R_HSB, f"lfh0t{u}")) as o:
                with g.nc.allow_low_precision(reason="bf16"):
                    o.E.tensor_scalar_add(h_sb[:, 2 * mo:2 * mo + 2], g.ps[0][:, 0:2],
                                          ctb[:, mo:mo + 1])
                    o(o.E.tensor_scalar_add(h0t[:], g.ps[0][:, 0:2], ctb[:, mo:mo + 1]))
            with g.op("t", reads=(f"lfh0t{u}",), writes=("psb1",)) as o:
                o(o.E.matmul(g.ps[1][0:2, :128], h0t[:], idr[:], start=True, stop=True))
            with g.op("v", reads=("psb1",), writes=(R_HR,)) as o:
                o(o.E.tensor_copy(hrow[:, mo * 128:(mo + 1) * 128], g.ps[1][0:2, :128]))

    if "h0row" in g.dbg:
        g.dma("q", g.dbg["h0row"], hrow[:], reads=(R_HR,), writes=("dbgh0",))
    with g.scope():
        ihsb = g.sb(f"lfih{u}", [128, KT * G3], BF16)
        xwt = g.sb(f"lfxw{u}", [128, G3], F32)
        for kt in range(KT):
            g.dma("g", ihsb[:, kt * G3:(kt + 1) * G3],
                  ihT[kt * 128:(kt + 1) * 128, :], writes=(f"lfih{u}",))
        for b in range(BPC):
            for tt in range(16):
                for oc in range(6):
                    with g.op("t", reads=(f"lfih{u}", RXN[b]), writes=("psb0",)) as o:
                        for kt in range(KT):
                            o(o.E.matmul(
                                g.ps[0][:, :],
                                XNs[b][:, kt * S + tt * 128: kt * S + (tt + 1) * 128],
                                ihsb[:, kt * G3 + oc * TCW: kt * G3 + (oc + 1) * TCW],
                                start=(kt == 0), stop=(kt == KT - 1)))
                    with g.op("v", reads=("psb0",), writes=(f"lfxwt{u}",)) as o:
                        o(o.E.tensor_copy(xwt[:, oc * TCW:(oc + 1) * TCW],
                                          g.ps[0][:, :]))
                g.dma("q", xw_hbm[tt * 128:(tt + 1) * 128, b, :], xwt[:],
                      reads=(f"lfxwt{u}",), writes=("lfxwh",))

    with g.scope():
        whh = g.sb(f"lfwh{u}", [128, KT * G3], BF16)
        for kt in range(KT):
            g.dma("g", whh[:, kt * G3:(kt + 1) * G3],
                  hhT[kt * 128:(kt + 1) * 128, :], writes=(f"lfwh{u}",))
        NRING = 3
        ring = [g.sb(f"lfr{u}_{i}", [2, G3], F32) for i in range(NRING)]
        grow = g.sb(f"lfgr{u}", [2, 2 * D], F32)
        nrow = g.sb(f"lfnr{u}", [2, D], F32)
        drow = g.sb(f"lfdr{u}", [2, D], F32)
        HS = g.sb(f"lfHS{u}", [128, 64 * 2 * KT], F32)
        idf32 = g.const_sb("c_idf32f", np.eye(128, dtype=np.float32), F32)
        R_G, R_N, R_D2, R_HS = f"lfgr{u}", f"lfnr{u}", f"lfdr{u}", f"lfHS{u}"
        for i in range(NRING):
            g.dma("q", ring[i][:], xw_hbm[i, :, :], reads=("lfxwh",),
                  writes=(f"lfr{u}_{i}",))
        for st in range(S):
            slt = st % NRING
            xw = ring[slt]
            RXW = f"lfr{u}_{slt}"
            with g.op("t", reads=(f"lfwh{u}", R_HSB),
                      writes=tuple(f"psb{c}" for c in range(6))) as o:
                for oc in range(6):
                    for kt in range(KT):
                        o(o.E.matmul(
                            g.ps[oc][0:2, :], h_sb[:, 2 * kt:2 * kt + 2],
                            whh[:, kt * G3 + oc * TCW: kt * G3 + (oc + 1) * TCW],
                            start=(kt == 0), stop=(kt == KT - 1)))
            with g.op("v", reads=tuple(f"psb{c}" for c in range(4)) + (RXW,),
                      writes=(R_G,)) as o:
                for oc in range(4):
                    o(o.E.tensor_add(grow[:, oc * TCW:(oc + 1) * TCW],
                                     g.ps[oc][0:2, :],
                                     xw[:, oc * TCW:(oc + 1) * TCW]))
            with g.op("s", reads=(R_G,), writes=(R_G,)) as o:
                o(o.E.activation(grow[:], grow[:], AF.Sigmoid))
            with g.op("v", reads=(R_G, "psb4", "psb5", RXW), writes=(R_N,),
                      fence=True) as o:
                o.E.tensor_mul(nrow[:, 0:TCW], grow[:, 0:TCW], g.ps[4][0:2, :])
                o.E.tensor_mul(nrow[:, TCW:D], grow[:, TCW:D], g.ps[5][0:2, :])
                o(o.E.tensor_add(nrow[:], nrow[:], xw[:, 2 * D:3 * D]))
            with g.op("s", reads=(R_N,), writes=(R_N,)) as o:
                o(o.E.activation(nrow[:], nrow[:], AF.Tanh))
            with g.op("v", reads=(R_N, R_G, R_HR), writes=(R_HR, R_D2), fence=True) as o:
                o.E.tensor_sub(drow[:], hrow[:], nrow[:])
                o.E.tensor_mul(drow[:], drow[:], grow[:, D:2 * D])
                o(o.E.tensor_add(hrow[:], nrow[:], drow[:]))
            with g.op("t", reads=(R_HR,), writes=("psb6",)) as o:
                for kt in range(KT):
                    o(o.E.matmul(g.ps[6][:, 2 * kt:2 * kt + 2],
                                 hrow[:, kt * 128:(kt + 1) * 128],
                                 idf32[0:2, 0:2], start=True, stop=True))
            with g.op("v", reads=("psb6",), writes=(R_HSB, R_HS)) as o:
                with g.nc.allow_low_precision(reason="bf16"):
                    o.E.tensor_copy(h_sb[:], g.ps[6][:, 0:2 * KT])
                o(o.E.tensor_copy(
                    HS[:].rearrange("p (kb t) -> p kb t", t=64)[:, :, st % 64],
                    g.ps[6][:, 0:2 * KT]))
            if st % 64 == 63:
                c0 = st - 63
                for kt in range(KT):
                    for b2 in range(2):
                        g.dma("q", hs_hbm[b2, kt * 128:(kt + 1) * 128, c0:c0 + 64],
                              HS[:, (kt * 2 + b2) * 64:(kt * 2 + b2 + 1) * 64],
                              reads=(R_HS,), writes=("lfhsh",))
            if st + NRING < S:
                g.dma("q", ring[slt][:], xw_hbm[st + NRING, :, :], reads=("lfxwh",),
                      writes=(RXW,))

    if "hs" in g.dbg:
        g.dma("q", g.dbg["hs"], hs_hbm[:, :, :], reads=("lfhsh",), writes=("dbghs",))
    CW = [g.sb(f"lfcw{u}_{b}", [128, 16 * 8], F32) for b in range(BPC)]
    with g.scope():
        RO = g.sb(f"lfro{u}", [128, S], F32R)
        EX = g.sb(f"lfex{u}", [128, S], F32R)
        prod = g.sb(f"lfpr{u}", [128, TCW], F32R)
        exw = g.sb(f"lfexw{u}", [128, KT * D], BF16)
        stall = g.sb(f"lfsa{u}", [128, 16 * 32], F32)
        sc8 = g.sb(f"lfs8{u}", [128, 16], F32)
        srow = g.sb(f"lfsr{u}", [1, S], F32)
        rtile = g.sb(f"lfrt{u}", [128, S], F32R)
        ones2 = g.const_sb("c_ones2r", np.ones((128, 2), np.float32), F32R)
        onesrf = g.const_sb("c_onesrowf", np.ones((1, 128), np.float32), F32)
        R_RO, R_EX, R_PR, R_SA = f"lfro{u}", f"lfex{u}", f"lfpr{u}", f"lfsa{u}"
        R_SR, R_RT2, R_S8 = f"lfsr{u}", f"lfrt{u}", f"lfs8{u}"
        for kt in range(KT):
            g.dma("g", exw[:, kt * D:(kt + 1) * D],
                  exprT[kt * 128:(kt + 1) * 128, :], writes=(f"lfexw{u}",))
        for b in range(BPC):
            for e in range(KT):
                g.dma("g", rtile[:], hs_hbm[b, e * 128:(e + 1) * 128, :],
                      reads=("lfhsh",), writes=(R_RT2,))
                for tc in range(TC):
                    tsl = slice(tc * TCW, (tc + 1) * TCW)
                    with g.op("v", reads=(R_RT2,), writes=(R_PR,)) as o:
                        with g.nc.allow_low_precision(reason="f32r"):
                            o(o.E.tensor_mul(prod[:], rtile[:, tsl].bitcast(F32),
                                             rtile[:, tsl].bitcast(F32)))
                    with g.op("t", reads=(R_PR,), writes=("psb6",)) as o:
                        o(o.E.matmul(g.ps[6][0:1, :], g.ones_col(), prod[:],
                                     start=True, stop=True))
                    with g.op("v", reads=("psb6",), writes=(R_SR,)) as o:
                        o(o.E.tensor_copy(srow[0:1, tsl], g.ps[6][0:1, :]))
                with g.op("s", reads=(R_SR,), writes=(R_SR,)) as o:
                    o(o.E.activation(srow[:], srow[:], AF.Sqrt))
                with g.op("v", reads=(R_SR,), writes=(R_SR,), fence=True) as o:
                    o.E.tensor_scalar_max(srow[:], srow[:], 1e-12)
                    o(o.E.reciprocal(srow[:], srow[:]))
                for tc in range(TC):
                    tsl = slice(tc * TCW, (tc + 1) * TCW)
                    with g.op("t", reads=(R_SR,), writes=("psb7",)) as o:
                        o(o.E.matmul(g.ps[7][:, :], onesrf[:], srow[0:1, tsl],
                                     start=True, stop=True))
                    with g.op("v", reads=("psb7", R_RT2), writes=(R_RO,)) as o:
                        with g.nc.allow_low_precision(reason="f32r"):
                            o(o.E.tensor_mul(RO[:, tsl], rtile[:, tsl].bitcast(F32),
                                             g.ps[7][:, :]))
                for tc in range(TC):
                    with g.op("t", reads=(f"lfexw{u}", RXN[b]), writes=("psb0",)) as o:
                        for kt in range(KT):
                            o(o.E.matmul(
                                g.ps[0][:, :],
                                exw[:, kt * D + e * 128: kt * D + (e + 1) * 128],
                                XNs[b][:, kt * S + tc * TCW: kt * S + (tc + 1) * TCW],
                                start=(kt == 0), stop=(kt == KT - 1)))
                    with g.op("v", reads=("psb0",), writes=(R_EX,)) as o:
                        with g.nc.allow_low_precision(reason="f32r"):
                            o(o.E.tensor_copy(EX[:, tc * TCW:(tc + 1) * TCW],
                                              g.ps[0][:, :]))
                for tt in range(16):
                    tsl = slice(tt * 128, (tt + 1) * 128)
                    for stat in range(3):
                        with g.op("v", reads=(R_EX, R_RO), writes=(R_PR,)) as o:
                            with g.nc.allow_low_precision(reason="f32r"):
                                aa = (EX if stat != 2 else RO)[:, tsl].bitcast(F32)
                                bb = (RO if stat != 1 else EX)[:, tsl].bitcast(F32)
                                o(o.E.tensor_mul(prod[:, 0:128], aa, bb))
                        with g.op("t", reads=(R_PR,), writes=("psb5",)) as o:
                            o(o.E.matmul(g.ps[5][:, 2 * stat:2 * stat + 2],
                                         prod[:, 0:128], ones2[:],
                                         start=True, stop=True))
                    with g.op("v", reads=("psb5",), writes=(R_SA,)) as o:
                        o(o.E.tensor_copy(
                            stall[:, tt * 32 + e:tt * 32 + 24:8], g.ps[5][:, 0:6:2]))
            if b == 0 and "stall" in g.dbg:
                g.dma("q", g.dbg["stall"], stall[:], reads=(R_SA,), writes=("dbgsa",))
            for tt in range(16):
                _emit_topk(g, u, tt, stall, sc8, CW[b], R_SA, R_S8, f"lfcw{u}_{b}")

    if "cw0" in g.dbg:
        g.dma("q", g.dbg["cw0"], CW[0][:], reads=(f"lfcw{u}_0",), writes=("dbgcw",))
    _emit_moe(g, u, XNs, RXN, CW, xfeat, yfeat, gwT, uwT, dwT, sgT, suT, sdT)
    tr.barrier()


STALL_STRIDE = 32


def _emit_topk(g, u, tt, stall, sc8, CWb, R_SA, R_S8, R_CW):
    EPSC = 1e-08
    o0 = tt * STALL_STRIDE
    num = stall[:, o0:o0 + 8]
    sE = stall[:, o0 + 8:o0 + 16]
    sR = stall[:, o0 + 16:o0 + 24]
    with g.op("s", reads=(R_SA,), writes=(R_SA,)) as o:
        o.E.activation(sE, sE, AF.Sqrt)
        o(o.E.activation(sR, sR, AF.Sqrt))
    with g.op("v", reads=(R_SA,), writes=(R_SA,), fence=True) as o:
        o.E.tensor_scalar_max(sE, sE, EPSC)
        o(o.E.tensor_scalar_max(sR, sR, EPSC))
    with g.op("v", reads=(R_SA,), writes=(R_SA,), fence=True) as o:
        o(o.E.tensor_mul(sE, sE, sR))
    with g.op("s", reads=(R_SA,), writes=(R_SA,), fence=True) as o:
        o(o.E.activation(sR, sE, AF.Ln))
    with g.op("s", reads=(R_SA,), writes=(R_SA,), fence=True) as o:
        o(o.E.activation(sR, sR, AF.Exp, scale=-1.0))
    with g.op("v", reads=(R_SA,), writes=(R_S8,), fence=True) as o:
        o(o.E.tensor_mul(sc8[:, 0:8], num, sR))
    with g.op("v", reads=(R_S8,), writes=(R_S8,), fence=True) as o:
        o(o.E.tensor_scalar(sc8[:, 0:8], sc8[:, 0:8], -9.0, 9.0, ALU.mult, ALU.add))
    with g.op("v", reads=(R_S8,), writes=(R_S8,), fence=True) as o:
        o(o.E.reduce_max(sc8[:, 8:9], sc8[:, 0:8], axis=AX.X))
    with g.op("v", reads=(R_S8,), writes=(R_SA + "1",), fence=True) as o:
        o(o.E.tensor_scalar(stall[:, o0 + 8:o0 + 16], sc8[:, 0:8], sc8[:, 8:9],
                            None, ALU.is_ge))
    with g.op("v", reads=(R_SA + "1", R_S8), writes=(R_S8,), fence=True) as o:
        o(o.E.scalar_tensor_tensor(sc8[:, 0:8], stall[:, o0 + 8:o0 + 16], -1e30,
                                   sc8[:, 0:8], ALU.mult, ALU.add))
    with g.op("v", reads=(R_S8,), writes=(R_S8,), fence=True) as o:
        o(o.E.reduce_max(sc8[:, 9:10], sc8[:, 0:8], axis=AX.X))
    with g.op("v", reads=(R_S8,), writes=(R_SA + "2",), fence=True) as o:
        o(o.E.tensor_scalar(stall[:, o0 + 16:o0 + 24], sc8[:, 0:8], sc8[:, 9:10],
                            None, ALU.is_ge))
    with g.op("v", reads=(R_S8,), writes=(R_S8,), fence=True) as o:
        o(o.E.tensor_scalar_mul(sc8[:, 10:11], sc8[:, 9:10], -1.0))
    with g.op("s", reads=(R_S8,), writes=(R_S8,), fence=True) as o:
        o(o.E.activation(sc8[:, 11:12], sc8[:, 8:9], AF.Sigmoid, bias=sc8[:, 10:11]))
    with g.op("v", reads=(R_S8,), writes=(R_S8,), fence=True) as o:
        o(o.E.tensor_scalar(sc8[:, 12:13], sc8[:, 11:12], -1.0, 1.0, ALU.mult,
                            ALU.add))
    with g.op("v", reads=(R_S8, R_SA + "1", R_SA + "2"), writes=(R_SA,),
              fence=True) as o:
        o.E.tensor_scalar_mul(stall[:, o0 + 8:o0 + 16], stall[:, o0 + 8:o0 + 16],
                              sc8[:, 11:12])
        o(o.E.tensor_scalar_mul(stall[:, o0 + 16:o0 + 24], stall[:, o0 + 16:o0 + 24],
                                sc8[:, 12:13]))
    with g.op("v", reads=(R_SA,), writes=(R_CW,), fence=True) as o:
        o(o.E.tensor_add(CWb[:, tt * 8:(tt + 1) * 8], stall[:, o0 + 8:o0 + 16],
                         stall[:, o0 + 16:o0 + 24]))


def _emit_moe(g, u, XNs, RXN, CW, xfeat, yfeat, gwT, uwT, dwT, sgT, suT, sdT):
    F_ = 2048
    gsl = g.sb(f"moes{u}", [128, TCW], F32)
    he = g.sb(f"moeh{u}", [128, F_], BF16)
    heT = g.sb(f"moet{u}", [128, 16 * TCW], BF16)
    Ysb = g.sb(f"moey{u}", [128, KT * TCW], F32)
    wg_sb = g.sb(f"moewg{u}", [128, KT * F_], BF16)
    wu_sb = g.sb(f"moewu{u}", [128, KT * F_], BF16)
    wd_sb = [g.sb(f"moewd{u}_{i}", [128, D], BF16) for i in range(2)]
    xres = g.sb(f"moex{u}", [128, TCW], F32)
    yt = g.sb(f"moeyt{u}", [128, TCW], F32)
    idbf = g.const_sb("c_idbf", np.eye(128, dtype=np.float32), BF16)
    R_GS, R_HE, R_HT, R_Y = f"moes{u}", f"moeh{u}", f"moet{u}", f"moey{u}"
    for b in range(BPC):
        XNb, rxn = XNs[b], RXN[b]
        for TT4 in range(4):
            for e in range(9):
                for kt in range(KT):
                    gsrc = gwT[e, kt * 128:(kt + 1) * 128, :] if e < 8 else \
                        sgT[kt * 128:(kt + 1) * 128, :]
                    usrc = uwT[e, kt * 128:(kt + 1) * 128, :] if e < 8 else \
                        suT[kt * 128:(kt + 1) * 128, :]
                    g.dma("g", wg_sb[:, kt * F_:(kt + 1) * F_], gsrc,
                          writes=(f"moewg{u}",))
                    g.dma("g", wu_sb[:, kt * F_:(kt + 1) * F_], usrc,
                          writes=(f"moewu{u}",))
                for t4 in range(4):
                    tt = TT4 * 4 + t4
                    with g.op("t", reads=(f"moewg{u}", f"moewu{u}", rxn),
                              writes=tuple(f"psb{c}" for c in range(8))) as o:
                        for kt in range(KT):
                            for oc in range(4):
                                o.E.matmul(
                                    g.ps[oc][:, :],
                                    XNb[:, kt * S + tt * 128: kt * S + (tt + 1) * 128],
                                    wg_sb[:, kt * F_ + oc * TCW:
                                          kt * F_ + (oc + 1) * TCW],
                                    start=(kt == 0), stop=(kt == KT - 1))
                                o(o.E.matmul(
                                    g.ps[4 + oc][:, :],
                                    XNb[:, kt * S + tt * 128: kt * S + (tt + 1) * 128],
                                    wu_sb[:, kt * F_ + oc * TCW:
                                          kt * F_ + (oc + 1) * TCW],
                                    start=(kt == 0), stop=(kt == KT - 1)))
                    for oc in range(4):
                        with g.op("s", reads=(f"psb{oc}",), writes=(R_GS,)) as o:
                            o(o.E.activation(gsl[:], g.ps[oc][:, :], AF.Silu))
                        rds = (R_GS, f"psb{4 + oc}")
                        if e < 8:
                            rds = rds + (f"lfcw{u}_{b}",)
                        with g.op("v", reads=rds, writes=(R_HE,)) as o:
                            with g.nc.allow_low_precision(reason="bf16"):
                                if e < 8:
                                    o.E.tensor_mul(he[:, oc * TCW:(oc + 1) * TCW],
                                                   g.ps[4 + oc][:, :], gsl[:])
                                    o(o.E.tensor_scalar_mul(
                                        he[:, oc * TCW:(oc + 1) * TCW],
                                        he[:, oc * TCW:(oc + 1) * TCW],
                                        CW[b][:, tt * 8 + e:tt * 8 + e + 1]))
                                else:
                                    o(o.E.tensor_mul(he[:, oc * TCW:(oc + 1) * TCW],
                                                     g.ps[4 + oc][:, :], gsl[:]))
                    for hk in range(16):
                        with g.op("t", reads=(R_HE,), writes=("psb0",)) as o:
                            o(o.E.transpose(g.ps[0][:].bitcast(BF16)[:, 0:128],
                                            he[:, hk * 128:(hk + 1) * 128], idbf[:]))
                        with g.op("v", reads=("psb0",), writes=(R_HT,)) as o:
                            with g.nc.allow_low_precision(reason="bf16"):
                                o(o.E.tensor_copy(
                                    heT[:, hk * TCW + t4 * 128:
                                        hk * TCW + (t4 + 1) * 128],
                                    g.ps[0][:].bitcast(BF16)[:, 0:128]))
                for hk in range(16):
                    dsrc = dwT[e, hk * 128:(hk + 1) * 128, :] if e < 8 else \
                        sdT[hk * 128:(hk + 1) * 128, :]
                    g.dma("g", wd_sb[hk % 2][:], dsrc, writes=(f"moewd{u}_{hk % 2}",))
                    with g.op("t", reads=(f"moewd{u}_{hk % 2}", R_HT),
                              writes=tuple(f"psb{c}" for c in range(8))) as o:
                        for mo in range(KT):
                            o(o.E.matmul(g.ps[mo][:, :],
                                         wd_sb[hk % 2][:, mo * 128:(mo + 1) * 128],
                                         heT[:, hk * TCW:(hk + 1) * TCW],
                                         start=(hk == 0), stop=(hk == 15)))
                for mo in range(KT):
                    with g.op("v", reads=(f"psb{mo}",), writes=(R_Y,)) as o:
                        if e == 0:
                            o(o.E.tensor_copy(Ysb[:, mo * TCW:(mo + 1) * TCW],
                                              g.ps[mo][:, :]))
                        else:
                            o(o.E.tensor_add(Ysb[:, mo * TCW:(mo + 1) * TCW],
                                             Ysb[:, mo * TCW:(mo + 1) * TCW],
                                             g.ps[mo][:, :]))
            for mo in range(KT):
                g.dma("q", xres[:], xfeat[b, mo * 128:(mo + 1) * 128,
                                          TT4 * TCW:(TT4 + 1) * TCW],
                      reads=("xf",), writes=(f"moex{u}",))
                with g.op("v", reads=(R_Y, f"moex{u}"), writes=(f"moeyt{u}",)) as o:
                    o(o.E.tensor_add(yt[:], Ysb[:, mo * TCW:(mo + 1) * TCW], xres[:]))
                g.dma("q", yfeat[b, mo * 128:(mo + 1) * 128,
                                 TT4 * TCW:(TT4 + 1) * TCW], yt[:],
                      reads=(f"moeyt{u}",), writes=("yf",))


def _lf_emit0(g, sl, xf, yf):
    emit_l1_ffn(g, sl, xf, yf, sl)


SUBLAYER_EMIT["L1_ffn"] = _lf_emit0


CHAIN = ["L0_attn_a", "L0_ffn", "L0_attn_b", "L0_ffn",
         "L1_attn_a", "L1_ffn", "L1_attn_b", "L1_ffn"]


def build_program(kinds):
    nc = bass.Bass("TRN2", target_bir_lowering=False)
    es = ExitStack()
    g = G(nc, es)
    xf = nc.dram_tensor("x_feat", [BPC, D, S], F32, kind="ExternalInput").ap()
    yf = nc.dram_tensor("y_feat", [BPC, D, S], F32, kind="ExternalOutput").ap()
    names = set()
    for k in kinds:
        for w in SUBLAYER_WEIGHTS[k]:
            names.add(w)
    for w in sorted(names):
        shape, dt = WEIGHT_SPECS[w]
        g.dram[w] = nc.dram_tensor(w, shape, dt, kind="ExternalInput").ap()
    scratch = None
    if len(kinds) > 1:
        scratch = nc.dram_tensor("x_scratch", [BPC, D, S], F32, kind="Internal").ap()
    cur = xf
    for i, k in enumerate(kinds):
        last = (i == len(kinds) - 1)
        dst = yf if last else scratch
        with g.scope():
            SUBLAYER_EMIT[k](g, f"{k}{i}", cur, dst)
        cur = dst
    st = g.tr.state.get("yf")
    for e, semv in g.tr.dsem.items():
        if g.tr.dcnt[e] > 0:
            g.tr._wait("q", (semv, g.tr.dcnt[e], None))
    for e, semv in g.tr.sem.items():
        if g.tr.tick[e] > 0:
            g.tr._wait("q", (semv, g.tr.tick[e], e))
    return nc, sorted(names), es


FUSED = True


def kernel(**inputs):
    x = np.asarray(inputs["L0_attn_a_x"])
    tw = prep_weights(inputs)
    shards = []
    for c in range(NCORES):
        xb = x[c * BPC:(c + 1) * BPC]
        shards.append(np.ascontiguousarray(xb.transpose(0, 2, 1)))
    if FUSED:
        groups = [CHAIN]
    else:
        groups = [[k] for k in CHAIN]
    progs = {}
    for kinds in groups:
        key = tuple(kinds)
        if key not in progs:
            progs[key] = build_program(kinds)
    for kinds in groups:
        nc, wnames, es = progs[tuple(kinds)]
        in_maps = []
        for c in range(NCORES):
            m = {"x_feat": shards[c]}
            for w in wnames:
                m[w] = tw[w]
            in_maps.append(m)
        res = run_bass_kernel_spmd(nc, in_maps, core_ids=list(range(NCORES)))
        shards = [res.results[c]["y_feat"] for c in range(NCORES)]
    outs = [s.transpose(0, 2, 1) for s in shards]
    return np.ascontiguousarray(np.concatenate(outs, axis=0)).astype(np.float32)


# version 07
# speedup vs baseline: 1.6366x; 1.6366x over previous
import os
import numpy as np
import ml_dtypes
from contextlib import ExitStack, contextmanager

import concourse.bass as bass
import concourse.mybir as mybir
from concourse.bass_utils import run_bass_kernel_spmd

F32 = mybir.dt.float32
F32R = mybir.dt.float32r
BF16 = mybir.dt.bfloat16
F16 = mybir.dt.float16
I32 = mybir.dt.int32
AF = mybir.ActivationFunctionType
ALU = mybir.AluOpType
AX = mybir.AxisListType

NCORES = 8
BPC = 2
S = 2048
D = 1024
KT = 8
TC = 4
TCW = 512


class TR:
    def __init__(self, nc, es):
        self.nc = nc
        self.eng = {"t": nc.tensor, "v": nc.vector, "s": nc.scalar,
                    "g": nc.gpsimd, "q": nc.sync}
        self.sem = {}
        self.tick = {}
        for e in self.eng:
            self.sem[e] = es.enter_context(nc.semaphore(f"tk_{e}"))
            self.tick[e] = 0
        self.dsem = {}
        self.dcnt = {}
        for e in ("q", "s", "g", "v"):
            self.dsem[e] = es.enter_context(nc.semaphore(f"dm_{e}"))
            self.dcnt[e] = 0
        self.state = {}
        self.waited = {e: {} for e in self.eng}
        self.dma_strict = os.environ.get("DMA_STRICT", "g")

    def _wait(self, e, rec, fence=False):
        sem, val, eng = rec
        if eng == e and not fence:
            return
        wd = self.waited[e]
        k = sem.name
        if wd.get(k, -1) >= val:
            return
        self.eng[e].wait_ge(sem, val)
        wd[k] = val

    def pre(self, e, reads=(), writes=(), fence=False):
        for r in reads:
            st = self.state.get(r)
            if st and st["w"]:
                self._wait(e, st["w"], fence)
        for r in writes:
            st = self.state.get(r)
            if st:
                if st["w"]:
                    self._wait(e, st["w"], fence)
                for rec in st["r"]:
                    self._wait(e, rec, fence)
        return self.eng[e]

    def post(self, e, last_ins, reads=(), writes=(), dma=False):
        if dma:
            self.dcnt[e] += 16
            rec = (self.dsem[e], self.dcnt[e], None)
            last_ins.then_inc(self.dsem[e], 16)
        else:
            self.tick[e] += 1
            rec = (self.sem[e], self.tick[e], e)
            last_ins.then_inc(self.sem[e], 1)
        for r in writes:
            self.state[r] = {"w": rec, "r": []}
        for r in reads:
            st = self.state.setdefault(r, {"w": None, "r": []})
            st["r"].append(rec)
        return rec

    def barrier(self):
        for e in self.eng:
            for e2, sem in self.sem.items():
                if e2 != e and self.tick[e2] > 0:
                    self._wait(e, (sem, self.tick[e2], e2))
            for e2, sem in self.dsem.items():
                if self.dcnt[e2] > 0:
                    self._wait(e, (sem, self.dcnt[e2], None))
        self.state.clear()


class OpCtx:
    def __init__(self, tr, e, reads, writes, dma=False, fence=False):
        self.tr, self.e, self.reads, self.writes, self.dma = tr, e, tuple(reads), tuple(writes), dma
        self.fence = fence
        self.last = None

    def __enter__(self):
        self.E = self.tr.pre(self.e, self.reads, self.writes, self.fence)
        return self

    def __call__(self, ins):
        self.last = ins
        return ins

    def __exit__(self, et, ev, tb):
        if et is None:
            assert self.last is not None, "empty op group"
            self.tr.post(self.e, self.last, self.reads, self.writes, self.dma)
        return False


class G:
    def __init__(self, nc, es):
        self.nc = nc
        self.es = es
        self.tr = TR(nc, es)
        self.uid = 0
        self.ps = [es.enter_context(nc.psum_tensor(f"psb{i}", [128, 512], F32))
                   for i in range(8)]
        self.psr = [f"psb{i}" for i in range(8)]
        self.consts = {}
        self.dram = {}
        self.dbg = {}

        self._scopes = []

    def u(self):
        self.uid += 1
        return self.uid

    @contextmanager
    def scope(self):
        es = ExitStack()
        self._scopes.append(es)
        try:
            yield
        finally:
            self.tr.barrier()
            self._scopes.pop()
            es.close()

    def op(self, e, reads=(), writes=(), fence=False):
        return OpCtx(self.tr, e, reads, writes, fence=fence)

    def const(self, name, arr):
        if name not in self.consts:
            self.consts[name] = self.nc.inline_tensor(np.ascontiguousarray(arr), name).ap()
        return self.consts[name]

    def const_sb(self, name, arr, dt=F32R):
        key = ("sb", name)
        if key not in self.consts:
            ap = self.const(name, arr)
            t = self.sb_root("sb_" + name, list(arr.shape), dt)
            self.dma("g", t[:], ap, writes=("c_" + name,))
            rec = (self.tr.dsem["g"], self.tr.dcnt["g"], None)
            for e in self.tr.eng:
                self.tr._wait(e, rec)
            self.consts[key] = t
        return self.consts[key]

    def fimm(self, val, dt=F32):
        key = ("imm", dt, float(val))
        if key not in self.consts:
            nm = f"imm{len([k for k in self.consts if k[0] == 'imm' if isinstance(k, tuple)])}_{self.u()}"
            t = self.const_sb(nm, np.full((128, 1), val, np.float32), dt)
            self.nc.const_aps.register(dt, float(val), t[:])
            self.consts[key] = float(val)
        return float(val)

    def ones_col(self):
        return self.const_sb("c_ones_col", np.ones((128, 1), np.float32))[:]

    def ones_row(self):
        return self.const_sb("c_ones_row", np.ones((1, 128), np.float32))[:]

    def ident128(self):
        return self.const("c_ident128", np.eye(128, dtype=np.float32))

    def sb(self, name, shape, dt):
        tgt = self._scopes[-1] if self._scopes else self.es
        self._n = getattr(self, "_n", 0) + 1
        return tgt.enter_context(self.nc.sbuf_tensor(f"{name}_{self._n}", shape, dt))

    def sb_root(self, name, shape, dt):
        return self.es.enter_context(self.nc.sbuf_tensor(name, shape, dt, side="right"))

    def dma(self, e, out, in_, reads=(), writes=(), **kw):
        E = self.tr.pre(e, reads, writes)
        if (self.tr.dma_strict in ("1", "all") or e in self.tr.dma_strict) \
                and self.tr.dcnt[e] > 0:
            self.tr._wait(e, (self.tr.dsem[e], self.tr.dcnt[e], None))
        ins = E.dma_start(out=out, in_=in_, **kw)
        self.tr.post(e, ins, reads, writes, dma=True)

    def dmat(self, out, in_, reads=(), writes=()):
        E = self.tr.pre("q", reads, writes)
        ins = E.dma_start_transpose(out, in_)
        self.tr.post("q", ins, reads, writes, dma=True)


def emit_ln_v2(g, xfeat_b, XN, xn_region):
    tr = g.tr
    u = g.u()
    xt = g.sb(f"lnx{u}", [128, 2 * TCW], F32R)
    sq = g.sb(f"lnq{u}", [128, TCW], F32R)
    mu = g.sb(f"lnm{u}", [1, TCW], F32)
    mu2 = g.sb(f"lnm2{u}", [1, TCW], F32)
    var = g.sb(f"lnv{u}", [1, TCW], F32)
    srt = g.sb(f"lns{u}", [1, TCW], F32)
    rst = g.sb(f"lnr{u}", [1, TCW], F32R)
    nmr = g.sb(f"lnn{u}", [1, TCW], F32R)
    rstb = g.sb(f"lnrb{u}", [128, TCW], F32)
    nmrb = g.sb(f"lnnb{u}", [128, TCW], F32)
    tmpn = g.sb(f"lntp{u}", [128, TCW], F32)
    ones = g.ones_col()
    onesr = g.ones_row()
    R_X, R_SQ, R_ROW, R_B = f"lnx{u}_", f"lnq{u}", f"lnrow{u}", f"lnb{u}"

    for tc in range(TC):
        ts = slice(tc * TCW, (tc + 1) * TCW)
        for kt in range(KT):
            pb = kt % 2
            xv = xt[:, pb * TCW:(pb + 1) * TCW]
            g.dma("g", xv, xfeat_b[kt * 128:(kt + 1) * 128, ts],
                  reads=("xf",), writes=(R_X + str(pb),))
            with g.op("s", reads=(R_X + str(pb),), writes=(R_SQ,)) as o:
                o(o.E.activation(sq[:], xv.bitcast(F32), AF.Square))
            with g.op("t", reads=(R_X + str(pb), R_SQ), writes=("psb0", "psb1")) as o:
                o.E.matmul(g.ps[0][0:1, :], ones, xv,
                           start=(kt == 0), stop=(kt == KT - 1))
                o(o.E.matmul(g.ps[1][0:1, :], ones, sq[:],
                             start=(kt == 0), stop=(kt == KT - 1)))
        with g.op("v", reads=("psb0", "psb1"), writes=(R_ROW,)) as o:
            o.E.tensor_scalar_mul(mu[:], g.ps[0][0:1, :], 1.0 / D)
            o(o.E.tensor_scalar_mul(var[:], g.ps[1][0:1, :], 1.0 / D))
        with g.op("v", reads=(R_ROW,), writes=(R_ROW,), fence=True) as o:
            o(o.E.tensor_mul(mu2[:], mu[:], mu[:]))
        with g.op("v", reads=(R_ROW,), writes=(R_ROW,), fence=True) as o:
            o(o.E.tensor_sub(var[:], var[:], mu2[:]))
        with g.op("s", reads=(R_ROW,), writes=(R_ROW,), fence=True) as o:
            o(o.E.activation(srt[:], var[:], AF.Ln, bias=g.fimm(1e-5)))
        with g.op("s", reads=(R_ROW,), writes=(R_ROW,), fence=True) as o:
            with g.nc.allow_low_precision(reason="f32r matmul operand prep"):
                o(o.E.activation(rst[:], srt[:], AF.Exp, scale=-0.5))
        with g.op("v", reads=(R_ROW,), writes=(R_ROW,), fence=True) as o:
            with g.nc.allow_low_precision(reason="f32r matmul operand prep"):
                o(o.E.scalar_tensor_tensor(nmr[:], mu[:], -1.0, rst[:].bitcast(F32),
                                           ALU.mult, ALU.mult))
        with g.op("t", reads=(R_ROW,), writes=("psb2", "psb3")) as o:
            o.E.matmul(g.ps[2][:, :], onesr, rst[:], start=True, stop=True)
            o(o.E.matmul(g.ps[3][:, :], onesr, nmr[:], start=True, stop=True))
        with g.op("v", reads=("psb2", "psb3"), writes=(R_B,)) as o:
            o.E.tensor_copy(rstb[:], g.ps[2][:, :])
            o(o.E.tensor_copy(nmrb[:], g.ps[3][:, :]))
        for kt in range(KT):
            pb = kt % 2
            xv = xt[:, pb * TCW:(pb + 1) * TCW]
            g.dma("g", xv, xfeat_b[kt * 128:(kt + 1) * 128, ts],
                  reads=("xf",), writes=(R_X + str(pb),))
            with g.op("v", reads=(R_X + str(pb), R_B), writes=(xn_region,)) as o:
                dst = XN[:, kt * S + tc * TCW: kt * S + (tc + 1) * TCW]
                o.E.tensor_mul(tmpn[:], xv.bitcast(F32), rstb[:])
                with g.nc.allow_low_precision(reason="stream dtype"):
                    o(o.E.tensor_add(dst, tmpn[:], nmrb[:]))
    return XN


def emit_l0_ffn(g, sl, xfeat, yfeat):
    tr = g.tr
    sc = 1.0 / 127.0
    wgT = g.dram["t_L0_ffn_w_gate"]
    wuT = g.dram["t_L0_ffn_w_up"]
    wdT = g.dram["t_L0_ffn_w_down"]
    u = g.u()
    XN = g.sb(f"ffnxn{u}", [128, KT * S], F32R)
    wg_sb = g.sb(f"ffnwg{u}", [128, KT * TCW], F32R)
    wu_sb = g.sb(f"ffnwu{u}", [128, KT * TCW], F32R)
    wd_sb = [g.sb(f"ffnwd{u}_{i}", [128, D], F32R) for i in range(2)]
    sil = g.sb(f"ffnsil{u}", [128, TCW], F32)
    hT = g.sb(f"ffnh{u}", [128, 24 * TCW], F32R)
    xres = g.sb(f"ffnxr{u}", [128, TCW], F32)
    yt = g.sb(f"ffny{u}", [128, TCW], F32)

    for b in range(BPC):
        with g.scope():
            emit_ln_v2(g, xfeat[b], XN, f"xn{u}")
        for tc in range(TC):
            ts = slice(tc * TCW, (tc + 1) * TCW)
            for oc in range(6):
                osl = slice(oc * TCW, (oc + 1) * TCW)
                for kt in range(KT):
                    g.dma("g", wg_sb[:, kt * TCW:(kt + 1) * TCW],
                          wgT[kt * 128:(kt + 1) * 128, osl],
                          writes=(f"wg{u}_{kt}",))
                    g.dma("g", wu_sb[:, kt * TCW:(kt + 1) * TCW],
                          wuT[kt * 128:(kt + 1) * 128, osl],
                          writes=(f"wu{u}_{kt}",))
                for mo in range(4):
                    ht_idx = oc * 4 + mo
                    pg, pu = g.ps[mo], g.ps[4 + mo]
                    rg, ru = g.psr[mo], g.psr[4 + mo]
                    wreads = tuple(f"wg{u}_{kt}" for kt in range(KT)) + (f"xn{u}",)
                    with g.op("t", reads=wreads, writes=(rg,)) as o:
                        for kt in range(KT):
                            o(o.E.matmul(pg[:, :],
                                         wg_sb[:, kt * TCW + mo * 128: kt * TCW + (mo + 1) * 128],
                                         XN[:, kt * S + tc * TCW: kt * S + (tc + 1) * TCW],
                                         start=(kt == 0), stop=(kt == KT - 1)))
                    ureads = tuple(f"wu{u}_{kt}" for kt in range(KT)) + (f"xn{u}",)
                    with g.op("t", reads=ureads, writes=(ru,)) as o:
                        for kt in range(KT):
                            o(o.E.matmul(pu[:, :],
                                         wu_sb[:, kt * TCW + mo * 128: kt * TCW + (mo + 1) * 128],
                                         XN[:, kt * S + tc * TCW: kt * S + (tc + 1) * TCW],
                                         start=(kt == 0), stop=(kt == KT - 1)))
                    with g.op("s", reads=(rg,), writes=(f"sil{u}",)) as o:
                        o(o.E.activation(sil[:], pg[:, :], AF.Silu, scale=g.fimm(sc)))
                    with g.op("v", reads=(ru, f"sil{u}"), writes=(f"h{u}_{ht_idx}",)) as o:
                        o(o.E.scalar_tensor_tensor(
                            hT[:, ht_idx * TCW:(ht_idx + 1) * TCW],
                            pu[:, :], sc, sil[:], ALU.mult, ALU.mult))
            hreads = tuple(f"h{u}_{i}" for i in range(24))
            for hk in range(24):
                wb = wd_sb[hk % 2]
                g.dma("g", wb[:], wdT[hk * 128:(hk + 1) * 128, :],
                      writes=(f"wd{u}_{hk % 2}",))
                with g.op("t", reads=(f"wd{u}_{hk % 2}", f"h{u}_{hk}"),
                          writes=tuple(g.psr)) as o:
                    for mo in range(8):
                        o(o.E.matmul(g.ps[mo][:, :],
                                     wb[:, mo * 128:(mo + 1) * 128],
                                     hT[:, hk * TCW:(hk + 1) * TCW],
                                     start=(hk == 0), stop=(hk == 23)))
            for mo in range(8):
                g.dma("q", xres[:], xfeat[b, mo * 128:(mo + 1) * 128, ts],
                      reads=("xf",), writes=(f"xr{u}",))
                with g.op("v", reads=(g.psr[mo], f"xr{u}"), writes=(f"y{u}",)) as o:
                    o(o.E.scalar_tensor_tensor(yt[:], g.ps[mo][:, :], sc, xres[:],
                                               ALU.mult, ALU.add))
                g.dma("q", yfeat[b, mo * 128:(mo + 1) * 128, ts], yt[:],
                      reads=(f"y{u}",), writes=("yf",))
        tr.barrier()


def prep_weights(inputs):
    t = {}
    def T(name, arr):
        t[name] = np.ascontiguousarray(arr)
    T("t_L0_ffn_w_gate", inputs["L0_ffn_w_gate"].T)
    T("t_L0_ffn_w_up", inputs["L0_ffn_w_up"].T)
    T("t_L0_ffn_w_down", inputs["L0_ffn_w_down"].T)
    T("t_L1_ffn_ih", inputs["L1_ffn_gru_w_ih"].T)
    T("t_L1_ffn_hh", inputs["L1_ffn_gru_w_hh"].T)
    T("t_L1_ffn_expr", inputs["L1_ffn_expr_w"].T)
    T("t_L1_ffn_ctx", inputs["L1_ffn_ctx_w"].T)
    T("t_L1_ffn_ctxb", inputs["L1_ffn_ctx_b"].reshape(KT, 128).T)
    T("t_L1_ffn_gate", inputs["L1_ffn_gate_w"].transpose(0, 2, 1))
    T("t_L1_ffn_up", inputs["L1_ffn_up_w"].transpose(0, 2, 1))
    T("t_L1_ffn_down", inputs["L1_ffn_down_w"].transpose(0, 2, 1))
    T("t_L1_ffn_sg", inputs["L1_ffn_shared_gate_w"].T)
    T("t_L1_ffn_su", inputs["L1_ffn_shared_up_w"].T)
    T("t_L1_ffn_sd", inputs["L1_ffn_shared_down_w"].T)
    T("t_L0_attn_a_q", inputs["L0_attn_a_q_w"].T)
    T("t_L0_attn_a_k", inputs["L0_attn_a_k_w"].T)
    T("t_L0_attn_a_v", inputs["L0_attn_a_v_w"].T)
    T("t_L0_attn_a_hvg", inputs["L0_attn_a_hvg_w"].T)
    T("t_L0_attn_a_him", inputs["L0_attn_a_him_w"][512:].T)
    T("t_L0_attn_a_hxp", inputs["L0_attn_a_hxp_w"].T)
    T("t_L0_attn_a_hmo", inputs["L0_attn_a_hmo_w"].T)
    T("t_L0_attn_a_out", inputs["L0_attn_a_out_w"].T)
    cw = inputs["L0_attn_a_conv_w"]
    cols = np.zeros((128, 112), np.float32)
    for kt in range(KT):
        for j in range(3):
            cols[:, j * 8 + kt] = cw[kt * 128:(kt + 1) * 128, j]
        cols[:, 24 + kt] = inputs["L0_attn_a_conv_b"][kt * 128:(kt + 1) * 128]
        cols[:, 32 + kt] = inputs["L0_attn_a_dt_bias"][kt * 128:(kt + 1) * 128]
        cols[:, 40 + kt] = inputs["L0_attn_a_D_param"][kt * 128:(kt + 1) * 128]
    cols[:, 48:112] = np.transpose(
        inputs["L0_attn_a_A_log"].reshape(KT, 128, 8), (1, 0, 2)).reshape(128, 64)
    T("t_L0_attn_a_cols", cols)
    row = np.zeros((16, 2), np.float32)
    row[:, 0] = inputs["L0_attn_a_q_gain"]
    row[0, 1] = inputs["L0_attn_a_merge_alpha"][0]
    T("t_L0_attn_a_row", row)
    T("t_L0_attn_b_Wq", inputs["L0_attn_b_Wq"])
    T("t_L0_attn_b_Wk", inputs["L0_attn_b_Wk"])
    T("t_L0_attn_b_Wv", inputs["L0_attn_b_Wv"])
    T("t_L0_attn_b_Wo", inputs["L0_attn_b_Wo"])
    T("t_L0_attn_b_b", np.concatenate(
        [inputs[f"L0_attn_b_b{w}"].reshape(KT, 128).T for w in "qkvo"], axis=1))
    T("t_L1_attn_a_Wv", inputs["L1_attn_a_Wv"].T)
    T("t_L1_attn_a_Wo", inputs["L1_attn_a_Wo"].T)
    T("t_L1_attn_a_Wg", inputs["L1_attn_a_Wg"].T)
    T("t_L1_attn_a_Wb", inputs["L1_attn_a_Wb"].T)
    T("t_L1_attn_b_Wg", inputs["L1_attn_b_Wg"].T)
    T("t_L1_attn_b_bg", inputs["L1_attn_b_bg"].reshape(1, 1))
    T("t_L1_attn_b_Wout", inputs["L1_attn_b_Wout"].T)
    T("t_L1_attn_b_bout", inputs["L1_attn_b_bout"].reshape(KT, 128).T)
    return t


WEIGHT_SPECS = {
    "t_L0_attn_a_q": ([D, D], F32),
    "t_L0_attn_a_k": ([D, 256], F32),
    "t_L0_attn_a_v": ([D, 256], F32),
    "t_L0_attn_a_hvg": ([D, 256], F32),
    "t_L0_attn_a_him": ([D, 2048], F32),
    "t_L0_attn_a_hxp": ([D, 80], F32),
    "t_L0_attn_a_hmo": ([D, D], F32),
    "t_L0_attn_a_out": ([D, D], F32),
    "t_L0_attn_a_cols": ([128, 112], F32),
    "t_L0_attn_a_row": ([16, 2], F32),
    "t_L0_attn_b_Wq": ([D, D], F16),
    "t_L0_attn_b_Wk": ([D, D], F16),
    "t_L0_attn_b_Wv": ([D, D], F16),
    "t_L0_attn_b_Wo": ([D, D], F16),
    "t_L0_attn_b_b": ([128, 4 * KT], F16),
    "t_L1_ffn_ih": ([D, 3072], F32),
    "t_L1_ffn_hh": ([D, 3072], F32),
    "t_L1_ffn_expr": ([D, D], F32),
    "t_L1_ffn_ctx": ([D, D], F32),
    "t_L1_ffn_ctxb": ([128, KT], F32),
    "t_L1_ffn_gate": ([8, D, 2048], F32),
    "t_L1_ffn_up": ([8, D, 2048], F32),
    "t_L1_ffn_down": ([8, 2048, D], F32),
    "t_L1_ffn_sg": ([D, 2048], F32),
    "t_L1_ffn_su": ([D, 2048], F32),
    "t_L1_ffn_sd": ([2048, D], F32),
    "t_L1_attn_a_Wv": ([D, D], F32),
    "t_L1_attn_a_Wo": ([D, D], F32),
    "t_L1_attn_a_Wg": ([D, 16], F32),
    "t_L1_attn_a_Wb": ([D, 16], F32),
    "t_L1_attn_b_Wg": ([D, 1], F32),
    "t_L1_attn_b_bg": ([1, 1], F32),
    "t_L1_attn_b_Wout": ([D, D], F32),
    "t_L1_attn_b_bout": ([128, KT], F32),
    "t_L0_ffn_w_gate": ([D, 3072], I32),
    "t_L0_ffn_w_up": ([D, 3072], I32),
    "t_L0_ffn_w_down": ([3072, D], I32),
}

SUBLAYER_WEIGHTS = {
    "L0_ffn": ["t_L0_ffn_w_gate", "t_L0_ffn_w_up", "t_L0_ffn_w_down"],
    "L1_attn_b": ["t_L1_attn_b_Wg", "t_L1_attn_b_bg", "t_L1_attn_b_Wout",
                  "t_L1_attn_b_bout"],
    "L0_attn_b": ["t_L0_attn_b_Wq", "t_L0_attn_b_Wk", "t_L0_attn_b_Wv",
                  "t_L0_attn_b_Wo", "t_L0_attn_b_b"],
    "L1_ffn": ["t_L1_ffn_ih", "t_L1_ffn_hh", "t_L1_ffn_expr", "t_L1_ffn_ctx",
               "t_L1_ffn_ctxb", "t_L1_ffn_gate", "t_L1_ffn_up", "t_L1_ffn_down",
               "t_L1_ffn_sg", "t_L1_ffn_su", "t_L1_ffn_sd"],
    "L1_attn_a": ["t_L1_attn_a_Wv", "t_L1_attn_a_Wo", "t_L1_attn_a_Wg",
                  "t_L1_attn_a_Wb"],
    "L0_attn_a": ["t_L0_attn_a_q", "t_L0_attn_a_k", "t_L0_attn_a_v",
                  "t_L0_attn_a_hvg", "t_L0_attn_a_him", "t_L0_attn_a_hxp",
                  "t_L0_attn_a_hmo", "t_L0_attn_a_out", "t_L0_attn_a_cols",
                  "t_L0_attn_a_row"],
}

SUBLAYER_EMIT = {}


def _emit_l0_ffn(g, sl, xf, yf):
    emit_l0_ffn(g, sl, xf, yf)


SUBLAYER_EMIT["L0_ffn"] = _emit_l0_ffn



def emit_l1_attn_b(g, sl, xfeat, yfeat):
    tr = g.tr
    u = g.u()
    wgT = g.dram["t_L1_attn_b_Wg"]
    bg = g.dram["t_L1_attn_b_bg"]
    woutT = g.dram["t_L1_attn_b_Wout"]
    bout = g.dram["t_L1_attn_b_bout"]

    XN = g.sb(f"abxn{u}", [128, KT * S], F32R)
    wg_sb = g.sb(f"abwg{u}", [128, KT], F32R)
    bg_sb = g.sb(f"abbg{u}", [1, 1], F32)
    wout_sb = g.sb(f"abwo{u}", [128, KT * D], BF16)
    bout_sb = g.sb(f"abbo{u}", [128, KT], F32)
    grow = g.sb(f"abgr{u}", [1, S], F32R)
    ogrow = g.sb(f"abog{u}", [1, S], F32R)
    ogb = g.sb(f"abogb{u}", [128, S], F32)
    gb = g.sb(f"abgb{u}", [128, S], F32)
    gx = g.sb(f"abgx{u}", [128, S], F32)
    H = g.sb(f"abh{u}", [128, KT * S], BF16)
    xres = g.sb(f"abxr{u}", [128, TCW], F32)
    yt = g.sb(f"aby{u}", [128, TCW], F32)

    for kt in range(KT):
        g.dma("g", wg_sb[:, kt:kt + 1], wgT[kt * 128:(kt + 1) * 128, :],
              writes=(f"abwg{u}",))
        g.dma("g", wout_sb[:, kt * D:(kt + 1) * D], woutT[kt * 128:(kt + 1) * 128, :],
              writes=(f"abwo{u}_{kt}",))
    g.dma("q", bg_sb[:], bg, writes=(f"abbg{u}",))
    g.dma("q", bout_sb[:], bout, writes=(f"abbo{u}",))

    for b in range(BPC):
        with g.scope():
            emit_ln_v2(g, xfeat[b], XN, f"abxn{u}")
        for tc in range(TC):
            with g.op("t", reads=(f"abwg{u}", f"abxn{u}"), writes=("psb0",)) as o:
                for kt in range(KT):
                    o(o.E.matmul(g.ps[0][0:1, :], wg_sb[:, kt:kt + 1],
                                 XN[:, kt * S + tc * TCW: kt * S + (tc + 1) * TCW],
                                 start=(kt == 0), stop=(kt == KT - 1)))
            with g.op("s", reads=("psb0", f"abbg{u}"), writes=(f"abgr{u}",)) as o:
                with g.nc.allow_low_precision(reason="f32r row"):
                    o(o.E.activation(grow[0:1, tc * TCW:(tc + 1) * TCW], g.ps[0][0:1, :],
                                     AF.Sigmoid, bias=bg_sb[0:1, :]))
        with g.op("v", reads=(f"abgr{u}",), writes=(f"abog{u}",)) as o:
            with g.nc.allow_low_precision(reason="row math"):
                o(o.E.tensor_scalar(ogrow[:], grow[:].bitcast(F32), -1.0, 1.0,
                                    ALU.mult, ALU.add))
        for tc in range(TC):
            with g.op("t", reads=(f"abgr{u}", f"abog{u}"), writes=("psb1", "psb2")) as o:
                with g.nc.allow_low_precision(reason="broadcast"):
                    o.E.matmul(g.ps[1][:, :], g.ones_row(),
                               ogrow[0:1, tc * TCW:(tc + 1) * TCW],
                               start=True, stop=True)
                    o(o.E.matmul(g.ps[2][:, :], g.ones_row(),
                                 grow[0:1, tc * TCW:(tc + 1) * TCW],
                                 start=True, stop=True))
            with g.op("v", reads=("psb1", "psb2"), writes=(f"abbc{u}",)) as o:
                o.E.tensor_copy(ogb[:, tc * TCW:(tc + 1) * TCW], g.ps[1][:, :])
                o(o.E.tensor_copy(gb[:, tc * TCW:(tc + 1) * TCW], g.ps[2][:, :]))
        for kt in range(KT):
            with g.op("v", reads=(f"abbc{u}", f"abxn{u}"), writes=(f"abgx{u}",)) as o:
                o(o.E.tensor_mul(gx[:], gb[:],
                                 XN[:, kt * S:(kt + 1) * S].bitcast(F32)))
            with g.op("v", reads=(f"abgx{u}", f"abbc{u}"), writes=(f"abh{u}_{kt}",)) as o:
                with g.nc.allow_low_precision(reason="bf16 scan out"):
                    o(o.E.tensor_tensor_scan(H[:, kt * S:(kt + 1) * S], ogb[:], gx[:],
                                             0.0, ALU.mult, ALU.add))
        hreads = tuple(f"abh{u}_{k}" for k in range(KT))
        for mo in range(KT):
            for tc in range(TC):
                with g.op("t", reads=hreads + (f"abwo{u}_{mo}",), writes=("psb3",)) as o:
                    for kt in range(KT):
                        o(o.E.matmul(g.ps[3][:, :],
                                     wout_sb[:, kt * D + mo * 128: kt * D + (mo + 1) * 128],
                                     H[:, kt * S + tc * TCW: kt * S + (tc + 1) * TCW],
                                     start=(kt == 0), stop=(kt == KT - 1)))
                g.dma("q", xres[:], xfeat[b, mo * 128:(mo + 1) * 128,
                                          tc * TCW:(tc + 1) * TCW],
                      reads=("xf",), writes=(f"abxr{u}",))
                with g.op("v", reads=("psb3", f"abxr{u}", f"abbo{u}"),
                          writes=(f"aby{u}",)) as o:
                    o(o.E.scalar_tensor_tensor(yt[:], g.ps[3][:, :],
                                               bout_sb[:, mo:mo + 1], xres[:],
                                               ALU.add, ALU.add))
                g.dma("q", yfeat[b, mo * 128:(mo + 1) * 128, tc * TCW:(tc + 1) * TCW],
                      yt[:], reads=(f"aby{u}",), writes=("yf",))
        tr.barrier()


SUBLAYER_EMIT["L1_attn_b"] = emit_l1_attn_b


def _rope_tables_b():
    inv = np.exp(np.arange(0, 64, 2, dtype=np.float32) * (-np.log(10000.0) / 64))
    ang = np.arange(S, dtype=np.float32)[:, None] * inv[None, :]
    sin = np.sin(ang).astype(np.float16).astype(np.float32)
    cos = np.cos(ang).astype(np.float16).astype(np.float32)
    cosF = np.empty((128, S), np.float32)
    sinF = np.empty((128, S), np.float32)
    for r in range(128):
        f = r % 32
        cosF[r] = cos[:, f]
        sinF[r] = (-1.0 if (r % 64) < 32 else 1.0) * sin[:, f]
    return cosF, sinF


def _swap_mat():
    swap = np.zeros((128, 128), np.float32)
    for m in range(128):
        swap[(m + 32) if (m % 64) < 32 else (m - 32), m] = 1.0
    return swap


def _emit_qkv_proj_b(g, u, W, wname, bias_sb, bidx, XN, xnr, dst, dstr, rope,
                     Wsb, swp, cosT, sinT, qraw, qsw, qtmp):
    for kt in range(KT):
        g.dma("g", Wsb[:, kt * D:(kt + 1) * D], W[kt * 128:(kt + 1) * 128, :],
              writes=(f"{wname}_{kt}",))
    wreads = tuple(f"{wname}_{kt}" for kt in range(KT))
    for mo in range(KT):
        bcol = bias_sb[:, bidx * KT + mo: bidx * KT + mo + 1]
        for tc in range(TC):
            with g.op("t", reads=wreads + (xnr,), writes=("psb0",)) as o:
                for kt in range(KT):
                    o(o.E.matmul(g.ps[0][:, :],
                                 Wsb[:, kt * D + mo * 128: kt * D + (mo + 1) * 128],
                                 XN[:, kt * S + tc * TCW: kt * S + (tc + 1) * TCW],
                                 start=(kt == 0), stop=(kt == KT - 1)))
            tgt = (qraw if rope else dst)[:, (0 if rope else mo * S) + tc * TCW:
                                          (0 if rope else mo * S) + (tc + 1) * TCW]
            with g.op("v", reads=("psb0", f"qb{u}"),
                      writes=((f"qraw{u}",) if rope else (dstr,))) as o:
                with g.nc.allow_low_precision(reason="bf16 attn"):
                    o(o.E.tensor_scalar_add(tgt, g.ps[0][:, :], bcol))
        if rope:
            for tc in range(TC):
                tsl = slice(tc * TCW, (tc + 1) * TCW)
                with g.op("t", reads=(f"qraw{u}", "c_swapb"), writes=("psb1",)) as o:
                    o(o.E.matmul(g.ps[1][:, :], swp, qraw[:, tsl],
                                 start=True, stop=True))
                with g.op("v", reads=("psb1", "c_ropeb"), writes=(f"qsw{u}",)) as o:
                    with g.nc.allow_low_precision(reason="bf16 attn"):
                        o(o.E.tensor_mul(qsw[:, tsl], g.ps[1][:, :], sinT[:, tsl]))
            with g.op("v", reads=(f"qraw{u}", f"qsw{u}", "c_ropeb"),
                      writes=(dstr, f"qtmp{u}")) as o:
                with g.nc.allow_low_precision(reason="bf16 attn"):
                    o.E.tensor_mul(qtmp[:], qraw[:].bitcast(F32), cosT[:])
                    o(o.E.tensor_add(dst[:, mo * S:(mo + 1) * S], qtmp[:], qsw[:]))


def emit_attn_core(g, u, Q, K, V, ya, rQ, rK, rV, rYA, kv_of_h, sc, mask_arr,
                   mask_name):
    tr = g.tr
    idbf = g.const_sb("c_idbf", np.eye(128, dtype=np.float32), BF16)
    id64s = g.const_sb("c_id64s", np.concatenate(
        [np.eye(64, dtype=np.float32)] * 2, axis=0), BF16)
    mask = g.const_sb(mask_name, mask_arr, F32)
    P = g.sb(f"atP{u}", [128, S], F32)
    Pb = g.sb(f"atPb{u}", [128, S], BF16)
    vtok = g.sb(f"atvt{u}", [128, 16 * 64], BF16)
    ptT = g.sb(f"atpt{u}", [128, 128], BF16)
    mx = g.sb(f"atmx{u}", [128, 8], F32)
    rP, rPb, rVT, rPT, rMX, rMXl = (f"atP{u}", f"atPb{u}", f"atvt{u}",
                                    f"atpt{u}", f"atmx{u}", f"atmx{u}l")
    last_kv = None
    for h in range(16):
        ht, rb = h // 2, (h % 2) * 64
        kt_v, rb_v = kv_of_h(h)
        if (kt_v, rb_v) != last_kv:
            last_kv = (kt_v, rb_v)
            for tt in range(16):
                with g.op("t", reads=(rV,), writes=("psb5",)) as o:
                    o(o.E.transpose(g.ps[5][:].bitcast(BF16)[:, 0:64],
                                    V[rb_v:rb_v + 64,
                                      kt_v * S + tt * 128: kt_v * S + (tt + 1) * 128],
                                    id64s[rb_v:rb_v + 64, 0:64]))
                with g.op("v", reads=("psb5",), writes=(rVT,)) as o:
                    with g.nc.allow_low_precision(reason="bf16 attn"):
                        o(o.E.tensor_copy(vtok[:, tt * 64:(tt + 1) * 64],
                                          g.ps[5][:].bitcast(BF16)[:, 0:64]))
        for qt in range(16):
            nk = (qt + 1) * 128
            nch = (nk + TCW - 1) // TCW
            qsl = slice(ht * S + qt * 128, ht * S + (qt + 1) * 128)
            with g.op("t", reads=(rQ, rK),
                      writes=tuple(f"psb{c}" for c in range(nch))) as o:
                for c in range(nch):
                    w = min(TCW, nk - c * TCW)
                    o(o.E.matmul(g.ps[c][:, 0:w], Q[rb:rb + 64, qsl],
                                 K[rb_v:rb_v + 64,
                                   kt_v * S + c * TCW: kt_v * S + c * TCW + w],
                                 start=True, stop=True))
            dc, off = divmod(qt * 128, TCW)
            with g.op("v", reads=(), writes=(f"psb{dc}",)) as o:
                o(o.E.tensor_add(g.ps[dc][:, off:off + 128],
                                 g.ps[dc][:, off:off + 128], mask[:]))
            with g.op("v", reads=tuple(f"psb{c}" for c in range(nch)),
                      writes=(rMX,)) as o:
                for c in range(nch):
                    w = min(TCW, nk - c * TCW)
                    o(o.E.reduce_max(mx[:, c:c + 1], g.ps[c][:, 0:w], axis=AX.X))
            with g.op("v", reads=(rMX,), writes=(rMX,), fence=True) as o:
                o(o.E.reduce_max(mx[:, 6:7], mx[:, 0:nch], axis=AX.X))
            with g.op("v", reads=(rMX,), writes=(rMX,), fence=True) as o:
                o(o.E.tensor_scalar_mul(mx[:, 6:7], mx[:, 6:7], -sc))
            with g.op("v", reads=(), writes=(rMXl,)) as o:
                o(o.E.memset(mx[:, 0:nch], 0.0))
            with g.op("s", reads=tuple(f"psb{c}" for c in range(nch)) +
                      (rMX,), writes=(rP, rMXl)) as o:
                for c in range(nch):
                    w = min(TCW, nk - c * TCW)
                    o(o.E.activation(P[:, c * TCW:c * TCW + w],
                                     g.ps[c][:, 0:w], AF.Exp,
                                     bias=mx[:, 6:7], scale=sc,
                                     accum_out=mx[:, c:c + 1]))
            with g.op("v", reads=(rMXl,), writes=(rMX,)) as o:
                o(o.E.reduce_sum(mx[:, 6:7], mx[:, 0:nch], axis=AX.X))
            with g.op("s", reads=(rMX,), writes=(rMX,)) as o:
                o(o.E.activation(mx[:, 7:8], mx[:, 6:7], AF.Ln))
            with g.op("s", reads=(rMX,), writes=(rMX,), fence=True) as o:
                o(o.E.activation(mx[:, 7:8], mx[:, 7:8], AF.Exp, scale=-1.0))
            with g.op("v", reads=(rP, rMX), writes=(rPb,)) as o:
                with g.nc.allow_low_precision(reason="bf16 attn"):
                    o(o.E.tensor_scalar_mul(Pb[:, 0:nk], P[:, 0:nk], mx[:, 7:8]))
            for k2 in range(qt + 1):
                with g.op("t", reads=(rPb,), writes=("psb5",)) as o:
                    o(o.E.transpose(g.ps[5][:].bitcast(BF16)[:, 0:128],
                                    Pb[:, k2 * 128:(k2 + 1) * 128], idbf[:, :]))
                with g.op("v", reads=("psb5",), writes=(rPT,)) as o:
                    with g.nc.allow_low_precision(reason="bf16 attn"):
                        o(o.E.tensor_copy(ptT[:], g.ps[5][:].bitcast(BF16)[:, 0:128]))
                with g.op("t", reads=(rPT, rVT), writes=("psb4",)) as o:
                    o(o.E.matmul(g.ps[4][rb:rb + 64, 0:128],
                                 vtok[:, k2 * 64:(k2 + 1) * 64], ptT[:],
                                 start=(k2 == 0), stop=(k2 == qt),
                                 tile_position=(0, rb)))
            with g.op("v", reads=("psb4",), writes=(rYA,)) as o:
                with g.nc.allow_low_precision(reason="bf16 attn"):
                    o(o.E.tensor_copy(
                        ya[rb:rb + 64, ht * S + qt * 128: ht * S + (qt + 1) * 128],
                        g.ps[4][rb:rb + 64, 0:128]))


def emit_l0_attn_b(g, sl, xfeat, yfeat):
    tr = g.tr
    u = g.u()
    Wq = g.dram["t_L0_attn_b_Wq"]
    Wk = g.dram["t_L0_attn_b_Wk"]
    Wv = g.dram["t_L0_attn_b_Wv"]
    Wo = g.dram["t_L0_attn_b_Wo"]
    bqkvo = g.dram["t_L0_attn_b_b"]
    cosF, sinF = _rope_tables_b()

    Q = g.sb(f"bQ{u}", [128, KT * S], BF16)
    K = g.sb(f"bK{u}", [128, KT * S], BF16)
    V = g.sb(f"bV{u}", [128, KT * S], BF16)

    for b in range(BPC):
        with g.scope():
            bias_sb = g.sb(f"bqb{u}", [128, 4 * KT], F32)
            g.dma("g", bias_sb[:], bqkvo, writes=(f"qb{u}",))
            XN = g.sb(f"bxn{u}", [128, KT * S], BF16)
            with g.scope():
                emit_ln_v2(g, xfeat[b], XN, f"bxn{u}")
            with g.scope():
                cosT = g.sb(f"bcos{u}", [128, S], F32)
                sinT = g.sb(f"bsin{u}", [128, S], F32)
                g.dma("q", cosT[:], g.const("c_cosFb", cosF), writes=("c_ropeb",))
                g.dma("q", sinT[:], g.const("c_sinFb", sinF), writes=("c_ropeb",))
                swp = g.const_sb("c_swapb", _swap_mat(), F32R)
                Wsb = g.sb(f"bW{u}", [128, KT * D], BF16)
                qraw = g.sb(f"bqr{u}", [128, S], F32R)
                qsw = g.sb(f"bqs{u}", [128, S], F32)
                qtmp = g.sb(f"bqt{u}", [128, S], F32)
                _emit_qkv_proj_b(g, u, Wq, f"bW{u}", bias_sb, 0, XN, f"bxn{u}",
                                 Q, f"bQ{u}", True, Wsb, swp[:], cosT[:], sinT[:],
                                 qraw, qsw, qtmp)
                _emit_qkv_proj_b(g, u, Wk, f"bW{u}", bias_sb, 1, XN, f"bxn{u}",
                                 K, f"bK{u}", True, Wsb, swp[:], cosT[:], sinT[:],
                                 qraw, qsw, qtmp)
                _emit_qkv_proj_b(g, u, Wv, f"bW{u}", bias_sb, 2, XN, f"bxn{u}",
                                 V, f"bV{u}", False, Wsb, swp[:], cosT[:], sinT[:],
                                 qraw, qsw, qtmp)
            with g.scope():
                ya = g.sb(f"bya{u}", [128, KT * S], BF16)
                emit_attn_core(g, u, Q, K, V, ya, f"bQ{u}", f"bK{u}", f"bV{u}",
                               f"bya{u}", lambda h: (h // 2, (h % 2) * 64), 0.125,
                               np.triu(np.full((128, 128), -8e4, np.float32), 1),
                               "c_maskb")
                Wosb = g.sb(f"bWo{u}", [128, KT * D], BF16)
                f16t = g.sb(f"bf16{u}", [128, TCW], F16)
                xres = g.sb(f"bxr{u}", [128, TCW], F32)
                yt = g.sb(f"byy{u}", [128, TCW], F32)
                for kt in range(KT):
                    g.dma("g", Wosb[:, kt * D:(kt + 1) * D],
                          Wo[kt * 128:(kt + 1) * 128, :], writes=(f"bwo{u}_{kt}",))
                woreads = tuple(f"bwo{u}_{kt}" for kt in range(KT))
                for mo in range(KT):
                    bcol = bias_sb[:, 3 * KT + mo: 3 * KT + mo + 1]
                    for tc in range(TC):
                        with g.op("t", reads=woreads + (f"bya{u}",), writes=("psb0",)) as o:
                            for kt in range(KT):
                                o(o.E.matmul(g.ps[0][:, :],
                                             Wosb[:, kt * D + mo * 128: kt * D + (mo + 1) * 128],
                                             ya[:, kt * S + tc * TCW: kt * S + (tc + 1) * TCW],
                                             start=(kt == 0), stop=(kt == KT - 1)))
                        g.dma("q", xres[:], xfeat[b, mo * 128:(mo + 1) * 128,
                                                  tc * TCW:(tc + 1) * TCW],
                              reads=("xf",), writes=(f"bxr{u}",))
                        with g.op("v", reads=("psb0", f"qb{u}"), writes=(f"bf16{u}",)) as o:
                            with g.nc.allow_low_precision(reason="f16 output cast"):
                                o(o.E.tensor_scalar_add(f16t[:], g.ps[0][:, :], bcol))
                        with g.op("v", reads=(f"bf16{u}", f"bxr{u}"), writes=(f"byy{u}",)) as o:
                            o(o.E.tensor_add(yt[:], xres[:], f16t[:]))
                        g.dma("q", yfeat[b, mo * 128:(mo + 1) * 128,
                                         tc * TCW:(tc + 1) * TCW], yt[:],
                              reads=(f"byy{u}",), writes=("yf",))
        tr.barrier()


SUBLAYER_EMIT["L0_attn_b"] = emit_l0_attn_b



ROPE_A = 32


def _rope_tables_a():
    inv = 1.0 / 10000.0 ** (np.arange(0, ROPE_A, 2, dtype=np.float32) / ROPE_A)
    ang = np.arange(S, dtype=np.float32)[:, None] * inv[None, :]
    sin, cos = np.sin(ang), np.cos(ang)
    cosF = np.ones((128, S), np.float32)
    sinF = np.zeros((128, S), np.float32)
    for r in range(128):
        rr = r % 64
        if rr < 16:
            cosF[r], sinF[r] = cos[:, rr], sin[:, rr]
        elif rr < 32:
            cosF[r], sinF[r] = cos[:, rr - 16], -sin[:, rr - 16]
    return cosF, sinF


def _swap_mat_a():
    sw = np.zeros((128, 128), np.float32)
    for m in range(128):
        rr = m % 64
        p = m + 16 if rr < 16 else (m - 16 if rr < 32 else m)
        sw[p, m] = 1.0
    return sw


def _hmask():
    hm = np.zeros((128, 128), np.float32)
    hm[0:64, 0] = 1.0
    hm[64:128, 64] = 1.0
    return hm


def _onesrow2():
    c = np.zeros((128, 64), np.float32)
    c[0, :] = 1.0
    c[64, :] = 1.0
    return c


def _gain_sel(kt):
    selg = np.zeros((16, 128), np.float32)
    selg[2 * kt, 0:64] = 1.0
    selg[2 * kt + 1, 64:128] = 1.0
    return selg


def _dup_mat(half):
    dd = np.zeros((128, 128), np.float32)
    for m in range(128):
        dd[half * 64 + (m % 64), m] = 1.0
    return dd


def _bc_sel(n):
    sel = np.zeros((16, 128), np.float32)
    sel[n, :] = 1.0
    return sel


def _proj_a(g, u, wT, wname, XN, xnr, Wsb, dout, emit_evac):
    mot = dout // 128
    for kt in range(KT):
        g.dma("g", Wsb[:, kt * dout:(kt + 1) * dout],
              wT[kt * 128:(kt + 1) * 128, :], writes=(f"{wname}_{kt}",))
    wreads = tuple(f"{wname}_{kt}" for kt in range(KT))
    for mo in range(mot):
        for tc in range(TC):
            with g.op("t", reads=wreads + (xnr,), writes=("psb0",)) as o:
                for kt in range(KT):
                    o(o.E.matmul(g.ps[0][:, :],
                                 Wsb[:, kt * dout + mo * 128: kt * dout + (mo + 1) * 128],
                                 XN[:, kt * S + tc * TCW: kt * S + (tc + 1) * TCW],
                                 start=(kt == 0), stop=(kt == KT - 1)))
            emit_evac(mo, tc, g.ps[0][:, :])


def emit_softplus(g, out_ap, x_ap, t1, t2, rin, rout, rtmp):
    with g.op("s", reads=(rin,), writes=(rtmp,)) as o:
        o(o.E.activation(t1, x_ap, AF.Abs))
    with g.op("s", reads=(rtmp,), writes=(rtmp + "2",), fence=True) as o:
        o(o.E.activation(t2, t1, AF.Exp, scale=-1.0))
    with g.op("s", reads=(rtmp + "2",), writes=(rtmp + "2",), fence=True) as o:
        o(o.E.activation(t2, t2, AF.Ln, bias=1.0))
    with g.op("v", reads=(rin,), writes=(rout,)) as o:
        o(o.E.tensor_scalar_max(out_ap, x_ap, 0.0))
    with g.op("v", reads=(rout, rtmp + "2"), writes=(rout,)) as o:
        o(o.E.tensor_add(out_ap, out_ap, t2))


def emit_l0_attn_a(g, sl, xfeat, yfeat):
    tr = g.tr
    u = g.u()
    qT = g.dram["t_L0_attn_a_q"]
    kT = g.dram["t_L0_attn_a_k"]
    vT = g.dram["t_L0_attn_a_v"]
    hvgT = g.dram["t_L0_attn_a_hvg"]
    himT = g.dram["t_L0_attn_a_him"]
    hxpT = g.dram["t_L0_attn_a_hxp"]
    hmoT = g.dram["t_L0_attn_a_hmo"]
    outT = g.dram["t_L0_attn_a_out"]
    colsD = g.dram["t_L0_attn_a_cols"]
    rowD = g.dram["t_L0_attn_a_row"]
    cosF, sinF = _rope_tables_a()
    nc = g.nc
    gate_hbm = nc.dram_tensor(f"aa_gate{u}", [D, S], BF16, kind="Internal").ap()
    xs_hbm = nc.dram_tensor(f"aa_xs{u}", [D, S], BF16, kind="Internal").ap()
    u_hbm = nc.dram_tensor(f"aa_u{u}", [D, S], F32, kind="Internal").ap()
    EPS_RMS = 1.1920929e-07

    for b in range(BPC):
        with g.scope():
            cols = g.sb(f"aacol{u}", [128, 112], F32)
            rowi = g.sb(f"aarow{u}", [16, 2], F32)
            g.dma("q", cols[:], colsD, writes=(f"aacol{u}",))
            g.dma("q", rowi[:], rowD, writes=(f"aarow{u}",))
            der = g.sb(f"aader{u}", [128, 96], F32)
            R_DER = f"aader{u}"
            with g.op("s", reads=(f"aacol{u}",), writes=(R_DER,)) as o:
                o(o.E.activation(der[:, 8:72], cols[:, 48:112], AF.Exp))
            with g.op("v", reads=(R_DER,), writes=(R_DER,), fence=True) as o:
                o(o.E.tensor_scalar_mul(der[:, 8:72], der[:, 8:72], -1.0))
            spt1 = g.sb(f"aasp{u}", [128, 16], F32)
            emit_softplus(g, der[:, 0:8], cols[:, 32:40], spt1[:, 0:8], spt1[:, 8:16],
                          f"aacol{u}", R_DER, f"aasp{u}")
            gr32 = g.sb(f"aag32{u}", [16, 2], F32R)
            wsc = g.sb(f"aaw32{u}", [1, 2], F32R)
            with g.op("v", reads=(f"aarow{u}",), writes=(f"aagr{u}",)) as o:
                with g.nc.allow_low_precision(reason="gain f32r"):
                    o.E.tensor_copy(gr32[:, 0:1], rowi[:, 0:1])
                    o(o.E.tensor_copy(gr32[:, 1:2], rowi[:, 0:1]))
            for kt in range(KT):
                selg = g.const_sb(f"c_selg{kt}", _gain_sel(kt), F32R)
                with g.op("t", reads=(f"aagr{u}",), writes=("psb7",)) as o:
                    o(o.E.matmul(g.ps[7][:, 0:2], selg[:], gr32[:],
                                 start=True, stop=True))
                with g.op("v", reads=("psb7",), writes=(R_DER,)) as o:
                    o(o.E.tensor_copy(der[:, 72 + kt:73 + kt], g.ps[7][:, 0:1]))
            with g.op("s", reads=(f"aarow{u}",), writes=(f"aaw{u}",)) as o:
                with g.nc.allow_low_precision(reason="f32r row"):
                    o.E.activation(wsc[:, 0:1], rowi[0:1, 1:2], AF.Sigmoid)
                    o(o.E.activation(wsc[:, 1:2], rowi[0:1, 1:2], AF.Sigmoid))
            with g.op("t", reads=(f"aaw{u}",), writes=("psb7",)) as o:
                o(o.E.matmul(g.ps[7][:, 0:2], g.ones_row(), wsc[:],
                             start=True, stop=True))
            with g.op("v", reads=("psb7",), writes=(R_DER,)) as o:
                o.E.tensor_copy(der[:, 80:81], g.ps[7][:, 0:1])
                o(o.E.tensor_scalar(der[:, 81:82], g.ps[7][:, 0:1], -1.0, 1.0,
                                    ALU.mult, ALU.add))

            ya = g.sb(f"aaya{u}", [128, KT * S], BF16)

            with g.scope():
                Q = g.sb(f"aaQ{u}", [128, KT * S], BF16)
                Kb = g.sb(f"aaK2{u}", [128, 4 * S], BF16)
                Vb = g.sb(f"aaV2{u}", [128, 4 * S], BF16)
                with g.scope():
                    XN = g.sb(f"aaxn{u}", [128, KT * S], BF16)
                    with g.scope():
                        emit_ln_v2(g, xfeat[b], XN, f"aaxn{u}")
                    with g.scope():
                        Wsb = g.sb(f"aaW{u}", [128, KT * 2048], BF16)
                        gtmp = g.sb(f"aagt{u}", [128, TCW], BF16)

                        def ev_him(mo, tc, ps):
                            with g.op("v", reads=("psb0",), writes=(f"aagt{u}",)) as o:
                                with g.nc.allow_low_precision(reason="bf16"):
                                    o(o.E.tensor_copy(gtmp[:], ps))
                            tgt = xs_hbm if mo < 8 else gate_hbm
                            g.dma("q", tgt[(mo % 8) * 128:(mo % 8 + 1) * 128,
                                           tc * TCW:(tc + 1) * TCW],
                                  gtmp[:], reads=(f"aagt{u}",), writes=("gh",))
                        _proj_a(g, u, himT, f"aaW{u}", XN, f"aaxn{u}", Wsb, 2048, ev_him)
                    with g.scope():
                        Wsb = g.sb(f"aaW2{u}", [128, KT * D], BF16)
                        qpre = g.sb(f"aaqp{u}", [128, S], F32R)
                        sq = g.sb(f"aasq{u}", [128, TCW], F32R)
                        rstd_sc = g.sb(f"aars{u}", [128, S], F32)
                        rstdb = g.sb(f"aarb{u}", [128, S], F32)
                        ropet = g.sb(f"aart{u}", [128, S], F32)
                        qsw = rstd_sc
                        cosT = g.sb(f"aacos{u}", [128, S], F32)
                        sinT = g.sb(f"aasin{u}", [128, S], F32)
                        g.dma("g", cosT[:], g.const("c_cosFa", cosF), writes=("c_ropea",))
                        g.dma("g", sinT[:], g.const("c_sinFa", sinF), writes=("c_ropea",))
                        swpa = g.const_sb("c_swapa", _swap_mat_a(), F32R)
                        hmsk = g.const_sb("c_hmask", _hmask(), F32R)
                        or2 = g.const_sb("c_onesrow2f", _onesrow2(), F32)
                        RQP, RSQ, RRS, RRB = (f"aaqp{u}", f"aasq{u}", f"aars{u}",
                                              f"aarb{u}")
                        RRT, RQS = f"aart{u}", f"aars{u}"

                        def tile_finalize(dst, dstr, dcol, gain, mo=0):
                            for tc in range(TC):
                                tsl = slice(tc * TCW, (tc + 1) * TCW)
                                with g.op("s", reads=(RQP,), writes=(RSQ,)) as o:
                                    with g.nc.allow_low_precision(reason="f32r"):
                                        o(o.E.activation(sq[:], qpre[:, tsl].bitcast(F32),
                                                         AF.Square))
                                with g.op("t", reads=(RSQ,), writes=("psb6",)) as o:
                                    o(o.E.matmul(g.ps[6][:, :], hmsk[:], sq[:],
                                                 start=True, stop=True))
                                with g.op("v", reads=("psb6",), writes=(RRS,)) as o:
                                    o(o.E.tensor_copy(rstd_sc[:, tsl], g.ps[6][:, :]))
                            for rb2 in (0, 64):
                                row = rstd_sc[rb2:rb2 + 1, :]
                                with g.op("v", reads=(RRS,), writes=(RRS,)) as o:
                                    o(o.E.tensor_scalar(row, row, 1.0 / 64, EPS_RMS,
                                                        ALU.mult, ALU.add))
                                with g.op("s", reads=(RRS,), writes=(RRS,)) as o:
                                    o(o.E.activation(row, row, AF.Sqrt))
                                with g.op("v", reads=(RRS,), writes=(RRS,), fence=True) as o:
                                    o(o.E.reciprocal(row, row))
                                for tc in range(TC):
                                    tsl = slice(tc * TCW, (tc + 1) * TCW)
                                    with g.op("t", reads=(RRS,), writes=("psb7",)) as o:
                                        o(o.E.matmul(
                                            g.ps[7][rb2:rb2 + 64, :],
                                            or2[rb2:rb2 + 1, :],
                                            rstd_sc[rb2:rb2 + 1, tsl],
                                            start=True, stop=True,
                                            tile_position=(rb2, rb2)))
                                    with g.op("v", reads=("psb7",), writes=(RRB,)) as o:
                                        o(o.E.tensor_copy(rstdb[rb2:rb2 + 64, tsl],
                                                          g.ps[7][rb2:rb2 + 64, :]))
                            for tc in range(TC):
                                tsl = slice(tc * TCW, (tc + 1) * TCW)
                                with g.op("t", reads=(RQP,), writes=("psb1",)) as o:
                                    o(o.E.matmul(g.ps[1][:, :], swpa[:], qpre[:, tsl],
                                                 start=True, stop=True))
                                with g.op("v", reads=("psb1", "c_ropea"),
                                          writes=(RQS,)) as o:
                                    o(o.E.tensor_mul(qsw[:, tsl], g.ps[1][:, :],
                                                     sinT[:, tsl]))
                            with g.op("v", reads=(RQP, RQS, "c_ropea"), writes=(RRT,)) as o:
                                o.E.tensor_mul(ropet[:], qpre[:].bitcast(F32), cosT[:])
                                o(o.E.tensor_add(ropet[:], ropet[:], qsw[:]))
                            with g.op("v", reads=(RRT, RRB, R_DER), writes=(dstr,)) as o:
                                with g.nc.allow_low_precision(reason="bf16"):
                                    if gain:
                                        o.E.tensor_mul(ropet[:], ropet[:], rstdb[:])
                                        o(o.E.tensor_scalar(
                                            dst[:, dcol * S:(dcol + 1) * S], ropet[:],
                                            der[:, 72 + mo:73 + mo], 0.125,
                                            ALU.mult, ALU.mult))
                                    else:
                                        o(o.E.tensor_mul(dst[:, dcol * S:(dcol + 1) * S],
                                                         ropet[:], rstdb[:]))

                        def ev_q(mo, tc, ps):
                            with g.op("v", reads=("psb0",), writes=(RQP,)) as o:
                                with g.nc.allow_low_precision(reason="f32r"):
                                    o(o.E.tensor_copy(qpre[:, tc * TCW:(tc + 1) * TCW], ps))
                            if tc == TC - 1:
                                tile_finalize(Q, f"aaQ{u}", mo, True, mo)

                        _proj_a(g, u, qT, f"aaW2{u}", XN, f"aaxn{u}", Wsb, D, ev_q)

                        def ev_k(mo, tc, ps):
                            with g.op("v", reads=("psb0",), writes=(RQP,)) as o:
                                with g.nc.allow_low_precision(reason="f32r"):
                                    o(o.E.tensor_copy(qpre[:, tc * TCW:(tc + 1) * TCW], ps))
                            if tc == TC - 1:
                                tile_finalize(Kb, f"aaK2{u}", 2 * mo, False)

                        _proj_a(g, u, kT, f"aaW2{u}", XN, f"aaxn{u}", Wsb, 256, ev_k)
                        for kt in range(KT):
                            g.dma("g", Wsb[:, kt * 256:(kt + 1) * 256],
                                  vT[kt * 128:(kt + 1) * 128, :], writes=(f"aaW2{u}_{kt}",))
                            g.dma("g", Wsb[:, KT * 256 + kt * 256: KT * 256 + (kt + 1) * 256],
                                  hvgT[kt * 128:(kt + 1) * 128, :],
                                  writes=(f"aaW2{u}_h{kt}",))
                        vwr = tuple(f"aaW2{u}_{k}" for k in range(KT))
                        hwr = tuple(f"aaW2{u}_h{k}" for k in range(KT))
                        sig = ropet
                        for mo in range(2):
                            for tc in range(TC):
                                with g.op("t", reads=vwr + (f"aaxn{u}",),
                                          writes=("psb0",)) as o:
                                    for kt in range(KT):
                                        o(o.E.matmul(
                                            g.ps[0][:, :],
                                            Wsb[:, kt * 256 + mo * 128: kt * 256 + (mo + 1) * 128],
                                            XN[:, kt * S + tc * TCW: kt * S + (tc + 1) * TCW],
                                            start=(kt == 0), stop=(kt == KT - 1)))
                                with g.op("t", reads=hwr + (f"aaxn{u}",),
                                          writes=("psb1",)) as o:
                                    for kt in range(KT):
                                        o(o.E.matmul(
                                            g.ps[1][:, :],
                                            Wsb[:, KT * 256 + kt * 256 + mo * 128:
                                                KT * 256 + kt * 256 + (mo + 1) * 128],
                                            XN[:, kt * S + tc * TCW: kt * S + (tc + 1) * TCW],
                                            start=(kt == 0), stop=(kt == KT - 1)))
                                with g.op("s", reads=("psb1",), writes=(f"aart{u}",)) as o:
                                    o(o.E.activation(sig[:, 0:TCW], g.ps[1][:, :],
                                                     AF.Sigmoid))
                                with g.op("v", reads=("psb0", f"aart{u}"),
                                          writes=(f"aaV2{u}",)) as o:
                                    with g.nc.allow_low_precision(reason="bf16"):
                                        o(o.E.tensor_mul(
                                            Vb[:, 2 * mo * S + tc * TCW:
                                               2 * mo * S + (tc + 1) * TCW],
                                            g.ps[0][:, :], sig[:, 0:TCW]))
                        dup0 = g.const_sb("c_dup0", _dup_mat(0), BF16)
                        dup1 = g.const_sb("c_dup1", _dup_mat(1), BF16)
                        for gkv in (1, 3, 0, 2):
                            dup = (dup0, dup1)[gkv % 2]
                            st = (gkv // 2) * 2
                            for tc in range(TC):
                                tsl0 = slice(st * S + tc * TCW, st * S + (tc + 1) * TCW)
                                tsl1 = slice(gkv * S + tc * TCW, gkv * S + (tc + 1) * TCW)
                                for dstb, rdst in ((Kb, f"aaK2{u}"), (Vb, f"aaV2{u}")):
                                    with g.op("t", reads=(rdst,), writes=("psb2",)) as o:
                                        o(o.E.matmul(g.ps[2][:, :], dup[:],
                                                     dstb[:, tsl0], start=True, stop=True))
                                    with g.op("v", reads=("psb2",), writes=(rdst,)) as o:
                                        with g.nc.allow_low_precision(reason="bf16"):
                                            o(o.E.tensor_copy(dstb[:, tsl1], g.ps[2][:, :]))
                with g.scope():
                    emit_attn_core(g, u, Q, Kb, Vb, ya, f"aaQ{u}", f"aaK2{u}",
                                   f"aaV2{u}", f"aaya{u}",
                                   lambda h: (h // 4, (h % 2) * 64), 1.0,
                                   np.triu(np.full((128, 128), -3e4, np.float32), 1),
                                   "c_maska")

            Y = g.sb(f"aaY{u}", [128, KT * S], BF16)
            with g.scope():
                dt_sb = g.sb(f"aadt{u}", [64, S], F32)
                bc_sb = g.sb(f"aabc{u}", [16, S], BF16)
                with g.scope():
                    xkt = g.sb(f"aaxk{u}", [128, 2050], BF16)
                    ukt = g.sb(f"aau{u}", [128, S], F32R)
                    cva = g.sb(f"aacv{u}", [128, S], F32)
                    cvb = g.sb(f"aacvb{u}", [128, S], F32)
                    hxA = g.sb(f"aahxA{u}", [128, KT * 64], F32R)
                    hxB = g.sb(f"aahxB{u}", [128, KT * 16], F32R)
                    for kt in range(KT):
                        g.dma("g", hxA[:, kt * 64:(kt + 1) * 64],
                              hxpT[kt * 128:(kt + 1) * 128, 0:64], writes=(f"aahx{u}",))
                        g.dma("g", hxB[:, kt * 16:(kt + 1) * 16],
                              hxpT[kt * 128:(kt + 1) * 128, 64:80], writes=(f"aahx{u}",))
                    RU, RCV = f"aau{u}", f"aacv{u}"
                    with g.op("v", reads=(), writes=(f"aaxk{u}",)) as o:
                        o(o.E.memset(xkt[:, 0:2], 0.0))
                    for kt in range(KT):
                        g.dma("q", xkt[:, 2:2 + S], xs_hbm[kt * 128:(kt + 1) * 128, :],
                              reads=("gh",), writes=(f"aaxk{u}",))
                        xs0 = xkt[:, 0: S]
                        xs1 = xkt[:, 1: 1 + S]
                        xs2 = xkt[:, 2: 2 + S]
                        w0c, w1c, w2c, cbc = (cols[:, kt:kt + 1], cols[:, 8 + kt:9 + kt],
                                              cols[:, 16 + kt:17 + kt], cols[:, 24 + kt:25 + kt])
                        with g.op("v", reads=(f"aaxk{u}",), writes=(RCV,)) as o:
                            o.E.tensor_scalar_mul(cva[:], xs0, w0c)
                            o.E.tensor_scalar_mul(cvb[:], xs1, w1c)
                            o.E.tensor_add(cva[:], cva[:], cvb[:])
                            o.E.tensor_scalar_mul(cvb[:], xs2, w2c)
                            o(o.E.tensor_add(cva[:], cva[:], cvb[:]))
                        with g.op("s", reads=(RCV,), writes=(RU,)) as o:
                            with g.nc.allow_low_precision(reason="f32r"):
                                o(o.E.activation(ukt[:], cva[:], AF.Silu, bias=cbc))
                        g.dma("q", u_hbm[kt * 128:(kt + 1) * 128, :], ukt[:].bitcast(F32),
                              reads=(RU,), writes=("uh",))
                        for tc in range(TC):
                            tsl = slice(tc * TCW, (tc + 1) * TCW)
                            with g.op("t", reads=(RU, f"aahx{u}"),
                                      writes=(f"psb{tc}", f"psb{4 + tc}")) as o:
                                o.E.matmul(g.ps[tc][0:64, :],
                                           hxA[:, kt * 64:(kt + 1) * 64], ukt[:, tsl],
                                           start=(kt == 0), stop=(kt == KT - 1))
                                o(o.E.matmul(g.ps[4 + tc][0:16, :],
                                             hxB[:, kt * 16:(kt + 1) * 16], ukt[:, tsl],
                                             start=(kt == 0), stop=(kt == KT - 1)))
                for tc in range(TC):
                    tsl = slice(tc * TCW, (tc + 1) * TCW)
                    with g.op("v", reads=(f"psb{tc}", f"psb{4 + tc}"),
                              writes=(f"aadt{u}", f"aabc{u}")) as o:
                        o.E.tensor_copy(dt_sb[:, tsl], g.ps[tc][0:64, :])
                        with g.nc.allow_low_precision(reason="bf16"):
                            o(o.E.tensor_copy(bc_sb[:, tsl], g.ps[4 + tc][0:16, :]))
                delta0 = g.sb(f"aad0{u}", [64, S], F32)
                with g.scope():
                    d0a = g.sb(f"aad0a{u}", [64, S], F32)
                    d0b = g.sb(f"aad0b{u}", [64, S], F32)
                    with g.op("v", reads=(f"aadt{u}", f"aacol{u}"), writes=(f"aadt{u}",)) as o:
                        o(o.E.tensor_scalar_add(dt_sb[:], dt_sb[:], cols[0:64, 32:33]))
                    emit_softplus(g, delta0[:], dt_sb[:], d0a[:], d0b[:],
                                  f"aadt{u}", f"aad0{u}", f"aad0t{u}")
                delta = g.sb(f"aadel{u}", [128, S], F32)
                du = g.sb(f"aadu{u}", [128, S], F32)
                uktf = g.sb(f"aauf{u}", [128, S], F32)
                asb = g.sb(f"aaa{u}", [128, S], F32)
                bt = g.sb(f"aabt{u}", [128, S], F32)
                hsb = g.sb(f"aah{u}", [128, S], F32)
                ysb = g.sb(f"aay{u}", [128, S], F32)
                gkt = g.sb(f"aagk{u}", [128, S], BF16)
                gsl = g.sb(f"aagsl{u}", [128, S], F32)
                RDL, RDU, RUF = f"aadel{u}", f"aadu{u}", f"aauf{u}"
                RA, RBT, RH, RY = f"aaa{u}", f"aabt{u}", f"aah{u}", f"aay{u}"
                for n in range(16):
                    g.const_sb(f"c_seln{n}", _bc_sel(n), BF16)
                for kt in range(KT):
                    g.dma("q", uktf[:], u_hbm[kt * 128:(kt + 1) * 128, :],
                          reads=("uh",), writes=(RUF,))
                    g.dma("q", gkt[:], gate_hbm[kt * 128:(kt + 1) * 128, :],
                          reads=("gh",), writes=(f"aagk{u}",))
                    with g.op("v", reads=(RUF, R_DER), writes=(RDL,)) as o:
                        o(o.E.tensor_scalar(delta[:], uktf[:], 0.0,
                                            der[:, kt:kt + 1], ALU.mult, ALU.add))
                    if kt == 0:
                        with g.op("v", reads=(f"aad0{u}",), writes=(RDL,)) as o:
                            o(o.E.tensor_copy(delta[0:64, :], delta0[:]))
                    with g.op("v", reads=(RDL, RUF), writes=(RDU,)) as o:
                        o(o.E.tensor_mul(du[:], delta[:], uktf[:]))
                    for n in range(8):
                        selB = g.consts[("sb", f"c_seln{n}")]
                        selC = g.consts[("sb", f"c_seln{8 + n}")]
                        for tc in range(TC):
                            tsl = slice(tc * TCW, (tc + 1) * TCW)
                            with g.op("t", reads=(f"aabc{u}",),
                                      writes=(f"psb{tc}", f"psb{4 + tc}")) as o:
                                o.E.matmul(g.ps[tc][:, :], selB[:], bc_sb[:, tsl],
                                           start=True, stop=True)
                                o(o.E.matmul(g.ps[4 + tc][:, :], selC[:], bc_sb[:, tsl],
                                             start=True, stop=True))
                        with g.op("s", reads=(RDL, R_DER), writes=(RA,)) as o:
                            o(o.E.activation(asb[:], delta[:], AF.Exp,
                                             scale=der[:, 8 + kt * 8 + n: 9 + kt * 8 + n]))
                        for tc in range(TC):
                            tsl = slice(tc * TCW, (tc + 1) * TCW)
                            with g.op("v", reads=(RDU, f"psb{tc}"), writes=(RBT,)) as o:
                                o(o.E.tensor_mul(bt[:, tsl], du[:, tsl], g.ps[tc][:, :]))
                        with g.op("v", reads=(RA, RBT), writes=(RH,)) as o:
                            o(o.E.tensor_tensor_scan(hsb[:], asb[:], bt[:], 0.0,
                                                     ALU.mult, ALU.add))
                        for tc in range(TC):
                            tsl = slice(tc * TCW, (tc + 1) * TCW)
                            if n == 0:
                                with g.op("v", reads=(RH, f"psb{4 + tc}"),
                                          writes=(RY,)) as o:
                                    o(o.E.tensor_mul(ysb[:, tsl], hsb[:, tsl],
                                                     g.ps[4 + tc][:, :]))
                            else:
                                with g.op("v", reads=(RH, f"psb{4 + tc}"),
                                          writes=(RBT,)) as o:
                                    o(o.E.tensor_mul(bt[:, tsl], hsb[:, tsl],
                                                     g.ps[4 + tc][:, :]))
                                with g.op("v", reads=(RBT,), writes=(RY,)) as o:
                                    o(o.E.tensor_add(ysb[:, tsl], ysb[:, tsl],
                                                     bt[:, tsl]))
                    with g.op("v", reads=(RUF, RY, f"aacol{u}"), writes=(RY,)) as o:
                        o(o.E.scalar_tensor_tensor(ysb[:], uktf[:],
                                                   cols[:, 40 + kt:41 + kt], ysb[:],
                                                   ALU.mult, ALU.add))
                    with g.op("s", reads=(f"aagk{u}",), writes=(f"aagsl{u}",)) as o:
                        o(o.E.activation(gsl[:], gkt[:], AF.Silu))
                    with g.op("v", reads=(RY, f"aagsl{u}"), writes=(f"aaY{u}",)) as o:
                        with g.nc.allow_low_precision(reason="bf16"):
                            o(o.E.tensor_mul(Y[:, kt * S:(kt + 1) * S], ysb[:], gsl[:]))

            with g.scope():
                Wsb3 = g.sb(f"aaW3{u}", [128, KT * D], BF16)
                m1 = g.sb(f"aam1{u}", [128, TCW], F32)
                m2 = g.sb(f"aam2{u}", [128, TCW], F32)
                RM = f"aam{u}"

                def ev_ym(mo, tc, ps):
                    ysl = slice(mo * S + tc * TCW, mo * S + (tc + 1) * TCW)
                    with g.op("v", reads=("psb0", f"aaya{u}", R_DER),
                              writes=(RM,)) as o:
                        o.E.tensor_scalar_mul(m1[:], ps, der[:, 81:82])
                        o(o.E.tensor_scalar_mul(m2[:], ya[:, ysl], der[:, 80:81]))
                    with g.op("v", reads=(RM,), writes=(f"aaya{u}",)) as o:
                        with g.nc.allow_low_precision(reason="bf16"):
                            o(o.E.tensor_add(ya[:, ysl], m1[:], m2[:]))

                _proj_a(g, u, hmoT, f"aaW3{u}", Y, f"aaY{u}", Wsb3, D, ev_ym)
                xres = g.sb(f"aaxr{u}", [128, TCW], F32)
                yt = g.sb(f"aayt{u}", [128, TCW], F32)

                def ev_out(mo, tc, ps):
                    g.dma("q", xres[:], xfeat[b, mo * 128:(mo + 1) * 128,
                                              tc * TCW:(tc + 1) * TCW],
                          reads=("xf",), writes=(f"aaxr{u}",))
                    with g.op("v", reads=("psb0", f"aaxr{u}"), writes=(f"aayt{u}",)) as o:
                        o(o.E.tensor_add(yt[:], ps, xres[:]))
                    g.dma("q", yfeat[b, mo * 128:(mo + 1) * 128,
                                     tc * TCW:(tc + 1) * TCW], yt[:],
                          reads=(f"aayt{u}",), writes=("yf",))

                _proj_a(g, u, outT, f"aaW3{u}", ya, f"aaya{u}", Wsb3, D, ev_out)
        tr.barrier()


SUBLAYER_EMIT["L0_attn_a"] = emit_l0_attn_a


def emit_l1_attn_a(g, sl, xfeat, yfeat):
    tr = g.tr
    u = g.u()
    WvT = g.dram["t_L1_attn_a_Wv"]
    WoT = g.dram["t_L1_attn_a_Wo"]
    WgT = g.dram["t_L1_attn_a_Wg"]
    WbT = g.dram["t_L1_attn_a_Wb"]
    L = 128
    NCH = S // L

    maskSL = np.tril(np.ones((L, L), np.float32), -1)
    maskTI = np.triu(np.ones((L, L), np.float32), 0)

    for b in range(BPC):
        with g.scope():
            RT = g.sb(f"daR{u}", [128, KT * (S + 1)], BF16)
            VT = g.sb(f"daV{u}", [128, KT * S], BF16)
            GB = g.sb(f"daGB{u}", [128, 2 * 16 * 16], F32)
            SSf = g.sb(f"daSf{u}", [128, 16 * 64], F32)
            SSb = g.sb(f"daSb{u}", [128, 16 * 64], BF16)
            R_RT, R_VT, R_OT, R_GB = f"daR{u}", f"daV{u}", f"daO{u}", f"daGB{u}"
            R_SF, R_SB = f"daSf{u}", f"daSb{u}"
            with g.op("v", reads=(), writes=(R_SF, R_SB, R_RT)) as o:
                o.E.memset(SSf[:], 0.0)
                with g.nc.allow_low_precision(reason="bf16"):
                    o.E.memset(SSb[:], 0.0)
                    for kt in range(KT):
                        o(o.E.memset(RT[:, kt * (S + 1):kt * (S + 1) + 1], 0.0))
            with g.scope():
                XN = g.sb(f"daxn{u}", [128, KT * S], F32R)
                with g.scope():
                    emit_ln_v2(g, xfeat[b], XN, f"daxn{u}")
                with g.scope():
                    sq = g.sb(f"dasq{u}", [128, TCW], F32R)
                    rstd_sc = g.sb(f"dars{u}", [128, S], F32)
                    hmsk = g.consts.get(("sb", "c_hmask")) or g.const_sb(
                        "c_hmask", _hmask(), F32R)
                    or2 = g.consts.get(("sb", "c_onesrow2f")) or g.const_sb(
                        "c_onesrow2f", _onesrow2(), F32)
                    RSQ, RRS = f"dasq{u}", f"dars{u}"
                    for kt in range(KT):
                        for tc in range(TC):
                            tsl = slice(tc * TCW, (tc + 1) * TCW)
                            with g.op("s", reads=(f"daxn{u}",), writes=(RSQ,)) as o:
                                with g.nc.allow_low_precision(reason="f32r"):
                                    o(o.E.activation(
                                        sq[:], XN[:, kt * S + tc * TCW:
                                                  kt * S + (tc + 1) * TCW].bitcast(F32),
                                        AF.Square))
                            with g.op("t", reads=(RSQ,), writes=("psb6",)) as o:
                                o(o.E.matmul(g.ps[6][:, :], hmsk[:], sq[:],
                                             start=True, stop=True))
                            with g.op("v", reads=("psb6",), writes=(RRS,)) as o:
                                o(o.E.tensor_copy(rstd_sc[:, tsl], g.ps[6][:, :]))
                        for rb2 in (0, 64):
                            row = rstd_sc[rb2:rb2 + 1, :]
                            with g.op("s", reads=(RRS,), writes=(RRS,)) as o:
                                o(o.E.activation(row, row, AF.Sqrt))
                            with g.op("v", reads=(RRS,), writes=(RRS,), fence=True) as o:
                                o.E.tensor_scalar_max(row, row, 1e-12)
                                o(o.E.reciprocal(row, row))
                            for tc in range(TC):
                                tsl = slice(tc * TCW, (tc + 1) * TCW)
                                with g.op("t", reads=(RRS,), writes=("psb7",)) as o:
                                    o(o.E.matmul(g.ps[7][rb2:rb2 + 64, :],
                                                 or2[rb2:rb2 + 1, :],
                                                 rstd_sc[rb2:rb2 + 1, tsl],
                                                 start=True, stop=True,
                                                 tile_position=(rb2, rb2)))
                                with g.op("v", reads=("psb7", f"daxn{u}"),
                                          writes=(R_RT,)) as o:
                                    with g.nc.allow_low_precision(reason="bf16"):
                                        o(o.E.tensor_mul(
                                            RT[rb2:rb2 + 64,
                                               kt * (S + 1) + 1 + tc * TCW:
                                               kt * (S + 1) + 1 + (tc + 1) * TCW],
                                            XN[rb2:rb2 + 64, kt * S + tc * TCW:
                                               kt * S + (tc + 1) * TCW].bitcast(F32),
                                            g.ps[7][rb2:rb2 + 64, :]))
                with g.scope():
                    Wsb = g.sb(f"daW{u}", [128, KT * D], F32R)

                    def ev_v(mo, tc, ps):
                        with g.op("v", reads=("psb0",), writes=(R_VT,)) as o:
                            with g.nc.allow_low_precision(reason="bf16"):
                                o(o.E.tensor_copy(
                                    VT[:, mo * S + tc * TCW: mo * S + (tc + 1) * TCW],
                                    ps))
                    _proj_a(g, u, WvT, f"daW{u}", XN, f"daxn{u}", Wsb, D, ev_v)
                with g.scope():
                    Wgb = g.sb(f"dawg{u}", [128, KT * 32], F32R)
                    gbt = g.sb(f"dagt{u}", [16, TCW], F32R)
                    idf = g.const_sb("c_idf32", np.eye(128, dtype=np.float32), F32R)
                    for kt in range(KT):
                        g.dma("g", Wgb[:, kt * 32:kt * 32 + 16],
                              WgT[kt * 128:(kt + 1) * 128, :], writes=(f"dawg{u}",))
                        g.dma("g", Wgb[:, kt * 32 + 16:kt * 32 + 32],
                              WbT[kt * 128:(kt + 1) * 128, :], writes=(f"dawg{u}",))
                    for which in range(2):
                        for tc in range(TC):
                            with g.op("t", reads=(f"dawg{u}", f"daxn{u}"),
                                      writes=("psb0",)) as o:
                                for kt in range(KT):
                                    o(o.E.matmul(
                                        g.ps[0][0:16, :],
                                        Wgb[:, kt * 32 + 16 * which:
                                            kt * 32 + 16 * which + 16],
                                        XN[:, kt * S + tc * TCW: kt * S + (tc + 1) * TCW],
                                        start=(kt == 0), stop=(kt == KT - 1)))
                            with g.op("s", reads=("psb0",), writes=(f"dagt{u}",)) as o:
                                with g.nc.allow_low_precision(reason="f32r"):
                                    o(o.E.activation(gbt[:], g.ps[0][0:16, :], AF.Sigmoid))
                            for j in range(4):
                                blk = tc * 4 + j
                                with g.op("t", reads=(f"dagt{u}",), writes=("psb1",)) as o:
                                    o(o.E.transpose(
                                        g.ps[1][:].bitcast(F32R)[:, 0:16],
                                        gbt[:, j * 128:(j + 1) * 128], idf[0:16, 0:16]))
                                with g.op("v", reads=("psb1",), writes=(R_GB,)) as o:
                                    o(o.E.tensor_copy(
                                        GB[:, which * 256 + blk * 16:
                                           which * 256 + blk * 16 + 16],
                                        g.ps[1][:].bitcast(F32R)[:, 0:16]))
            OT = g.sb(f"daO{u}", [128, KT * S], BF16)
            with g.scope():
                idbf = g.const_sb("c_idbf", np.eye(128, dtype=np.float32), BF16)
                id64s = g.const_sb("c_id64s", np.concatenate(
                    [np.eye(64, dtype=np.float32)] * 2, axis=0), BF16)
                idr = g.const_sb("c_idr", np.eye(128, dtype=np.float32), F32R)
                mSL = g.const_sb("c_maskSL", maskSL, F32)
                mTI = g.const_sb("c_maskTI", maskTI, F32)
                Bp = g.sb(f"daB{u}", [128, 128], F32R)
                Btmp = g.sb(f"daBm{u}", [128, 128], F32)
                BpT = g.sb(f"daBT{u}", [128, 128], F32R)
                BT_all = g.sb(f"daBTs{u}", [128, 7 * 128], F32R)
                Ut = g.sb(f"daU{u}", [128, 64], F32)
                Ub = g.sb(f"daUb{u}", [128, 64], BF16)
                Ur = g.sb(f"daUr{u}", [128, 64], F32R)
                Vt32 = g.sb(f"daVt{u}", [128, 64], F32)
                GTm = g.sb(f"daG{u}", [128, 128], F32R)
                Wtok = g.sb(f"daWt{u}", [128, 64], BF16)
                Otok = g.sb(f"daOt{u}", [128, 64], BF16)
                RB, RBT, RBS = f"daB{u}", f"daBT{u}", f"daBTs{u}"
                RU, RUB, RURR = f"daU{u}", f"daUb{u}", f"daUr{u}"
                RVt, RG, RWt, ROt = f"daVt{u}", f"daG{u}", f"daWt{u}", f"daOt{u}"
                for h in range(16):
                    ht, rb = h // 2, (h % 2) * 64
                    for c in range(NCH):
                        wsl = slice(ht * (S + 1) + c * L, ht * (S + 1) + (c + 1) * L)
                        rsl = slice(ht * (S + 1) + 1 + c * L, ht * (S + 1) + 1 + (c + 1) * L)
                        bcol = GB[:, 256 + c * 16 + h:256 + c * 16 + h + 1]
                        gcol = GB[:, c * 16 + h:c * 16 + h + 1]
                        ssl = slice(h * 64, (h + 1) * 64)
                        with g.op("t", reads=(R_RT,), writes=("psb0",)) as o:
                            o(o.E.matmul(g.ps[0][:, :128], RT[rb:rb + 64, wsl],
                                         RT[rb:rb + 64, wsl], start=True, stop=True))
                        with g.op("v", reads=("psb0", R_GB), writes=(RB + "m",)) as o:
                            o(o.E.tensor_mul(Btmp[:], g.ps[0][:, :128], mSL[:]))
                        with g.op("v", reads=(RB + "m", R_GB), writes=(RB,),
                                  fence=True) as o:
                            with g.nc.allow_low_precision(reason="f32r"):
                                o(o.E.tensor_scalar_mul(Bp[:], Btmp[:], bcol))
                        with g.op("t", reads=(RB,), writes=("psb1",)) as o:
                            o(o.E.matmul(g.ps[1][:, :128], Bp[:], idr[:],
                                         start=True, stop=True))
                        with g.op("v", reads=("psb1",), writes=(RBT, RBS)) as o:
                            with g.nc.allow_low_precision(reason="f32r"):
                                o.E.tensor_copy(BpT[:], g.ps[1][:, :128])
                                o(o.E.tensor_copy(BT_all[:, 0:128], g.ps[1][:, :128]))
                        for j in range(1, 7):
                            with g.op("t", reads=(RB, RBT), writes=("psb2", "psb3")) as o:
                                o.E.matmul(g.ps[2][:, :128], BpT[:], Bp[:],
                                           start=True, stop=True)
                                o(o.E.matmul(g.ps[3][:, :128], Bp[:], BpT[:],
                                             start=True, stop=True))
                            with g.op("v", reads=("psb2", "psb3"),
                                      writes=(RB, RBT, RBS)) as o:
                                with g.nc.allow_low_precision(reason="f32r"):
                                    o.E.tensor_copy(Bp[:], g.ps[2][:, :128])
                                    o.E.tensor_copy(BpT[:], g.ps[3][:, :128])
                                    o(o.E.tensor_copy(BT_all[:, j * 128:(j + 1) * 128],
                                                      g.ps[3][:, :128]))
                        with g.op("t", reads=(R_VT,), writes=("psb4",)) as o:
                            o(o.E.transpose(
                                g.ps[4][:].bitcast(BF16)[:, 0:64],
                                VT[rb:rb + 64, ht * S + c * L: ht * S + (c + 1) * L],
                                (g.consts[("sb", "c_id64s")][rb:rb + 64, 0:64])))
                        with g.op("v", reads=("psb4",), writes=(RVt,)) as o:
                            o(o.E.tensor_copy(Vt32[:], g.ps[4][:].bitcast(BF16)[:, 0:64]))
                        with g.op("t", reads=(R_RT, R_SB), writes=("psb5",)) as o:
                            o(o.E.matmul(g.ps[5][:, 0:64], RT[rb:rb + 64, wsl],
                                         SSb[rb:rb + 64, ssl], start=True, stop=True))
                        with g.op("v", reads=("psb5", RVt, R_GB), writes=(RU,)) as o:
                            o(o.E.tensor_sub(Ut[:], Vt32[:], g.ps[5][:, 0:64]))
                        with g.op("v", reads=(RU, R_GB), writes=(RU,), fence=True) as o:
                            o(o.E.tensor_scalar_mul(Ut[:], Ut[:], bcol))
                        for j in range(6, -1, -1):
                            with g.op("v", reads=(RU,), writes=(RURR,)) as o:
                                with g.nc.allow_low_precision(reason="f32r"):
                                    o(o.E.tensor_copy(Ur[:], Ut[:]))
                            with g.op("t", reads=(RURR, RBS), writes=("psb2",)) as o:
                                o(o.E.matmul(g.ps[2][:, 0:64],
                                             BT_all[:, j * 128:(j + 1) * 128], Ur[:],
                                             start=True, stop=True))
                            with g.op("v", reads=("psb2",), writes=(RU,)) as o:
                                if j == 0:
                                    o(o.E.tensor_sub(Ut[:], Ut[:], g.ps[2][:, 0:64]))
                                else:
                                    o(o.E.tensor_add(Ut[:], Ut[:], g.ps[2][:, 0:64]))
                        with g.op("t", reads=(R_RT,), writes=("psb0",)) as o:
                            o(o.E.matmul(g.ps[0][:, :128], RT[rb:rb + 64, wsl],
                                         RT[rb:rb + 64, rsl], start=True, stop=True))
                        with g.op("v", reads=("psb0",), writes=(RG,)) as o:
                            with g.nc.allow_low_precision(reason="f32r"):
                                o(o.E.tensor_mul(GTm[:], g.ps[0][:, :128], mTI[:]))
                        with g.op("v", reads=(RU,), writes=(RURR, RUB)) as o:
                            with g.nc.allow_low_precision(reason="f32r"):
                                o.E.tensor_copy(Ur[:], Ut[:])
                                o(o.E.tensor_copy(Ub[:], Ut[:]))
                        with g.op("t", reads=(RG, RURR, R_RT, R_SB), writes=("psb5",)) as o:
                            o.E.matmul(g.ps[5][:, 0:64], GTm[:], Ur[:],
                                       start=True, stop=False)
                            o(o.E.matmul(g.ps[5][:, 0:64], RT[rb:rb + 64, rsl],
                                         SSb[rb:rb + 64, ssl], start=False, stop=True))
                        with g.op("v", reads=("psb5", R_GB), writes=(ROt,)) as o:
                            with g.nc.allow_low_precision(reason="bf16"):
                                o(o.E.tensor_scalar_mul(Otok[:], g.ps[5][:, 0:64], gcol))
                        with g.op("t", reads=(ROt,), writes=("psb6",)) as o:
                            o(o.E.transpose(g.ps[6][:].bitcast(BF16)[rb:rb + 64, 0:128],
                                            Otok[:], idbf[:, :], tile_position=(0, rb)))
                        with g.op("v", reads=("psb6",), writes=(R_OT,)) as o:
                            with g.nc.allow_low_precision(reason="bf16"):
                                o(o.E.tensor_copy(
                                    OT[rb:rb + 64, ht * S + c * L: ht * S + (c + 1) * L],
                                    g.ps[6][:].bitcast(BF16)[rb:rb + 64, 0:128]))
                        with g.op("t", reads=(R_RT,), writes=("psb7",)) as o:
                            o(o.E.transpose(
                                g.ps[7][:].bitcast(BF16)[:, 0:64],
                                RT[rb:rb + 64, wsl],
                                g.consts[("sb", "c_id64s")][rb:rb + 64, 0:64]))
                        with g.op("v", reads=("psb7",), writes=(RWt,)) as o:
                            with g.nc.allow_low_precision(reason="bf16"):
                                o(o.E.tensor_copy(Wtok[:], g.ps[7][:].bitcast(BF16)[:, 0:64]))
                        with g.op("t", reads=(RWt, RUB), writes=("psb3",)) as o:
                            o.E.matmul(g.ps[3][0:64, 0:64], Wtok[:], Ub[:],
                                       start=True, stop=True)
                            o(o.E.matmul(g.ps[3][64:128, 0:64], Wtok[:], Ub[:],
                                         start=True, stop=True, tile_position=(0, 64)))
                        with g.op("v", reads=("psb3", R_SF), writes=(R_SF,),
                                  fence=True) as o:
                            o(o.E.tensor_add(SSf[:, ssl], SSf[:, ssl],
                                             g.ps[3][:, 0:64]))
                        with g.op("v", reads=(R_SF,), writes=(R_SB,), fence=True) as o:
                            with g.nc.allow_low_precision(reason="bf16"):
                                o(o.E.tensor_copy(SSb[:, ssl], SSf[:, ssl]))
            with g.scope():
                Wsb2 = g.sb(f"daW2{u}", [128, KT * D], BF16)
                xres = g.sb(f"daxr{u}", [128, TCW], F32)
                yt = g.sb(f"dayt{u}", [128, TCW], F32)

                def ev_out(mo, tc, ps):
                    g.dma("q", xres[:], xfeat[b, mo * 128:(mo + 1) * 128,
                                              tc * TCW:(tc + 1) * TCW],
                          reads=("xf",), writes=(f"daxr{u}",))
                    with g.op("v", reads=("psb0", f"daxr{u}"), writes=(f"dayt{u}",)) as o:
                        o(o.E.tensor_add(yt[:], ps, xres[:]))
                    g.dma("q", yfeat[b, mo * 128:(mo + 1) * 128,
                                     tc * TCW:(tc + 1) * TCW], yt[:],
                          reads=(f"dayt{u}",), writes=("yf",))
                _proj_a(g, u, WoT, f"daW2{u}", OT, f"daO{u}", Wsb2, D, ev_out)
        tr.barrier()


SUBLAYER_EMIT["L1_attn_a"] = emit_l1_attn_a


def emit_l1_ffn(g, sl, xfeat, yfeat, inst):
    tr = g.tr
    u = g.u()
    nc = g.nc
    ihT = g.dram["t_L1_ffn_ih"]
    hhT = g.dram["t_L1_ffn_hh"]
    exprT = g.dram["t_L1_ffn_expr"]
    ctxT = g.dram["t_L1_ffn_ctx"]
    ctxb = g.dram["t_L1_ffn_ctxb"]
    gwT = g.dram["t_L1_ffn_gate"]
    uwT = g.dram["t_L1_ffn_up"]
    dwT = g.dram["t_L1_ffn_down"]
    sgT = g.dram["t_L1_ffn_sg"]
    suT = g.dram["t_L1_ffn_su"]
    sdT = g.dram["t_L1_ffn_sd"]
    G3 = 3072
    xw_hbm = nc.dram_tensor(f"lf_xw{u}_{inst}", [2, 24, 128, S], F32,
                            kind="Internal").ap()
    hs_hbm = nc.dram_tensor(f"lf_hs{u}_{inst}", [2, D, S], F32, kind="Internal").ap()
    xn_hbm = nc.dram_tensor(f"lf_xn{u}_{inst}", [2, 128, KT, S], BF16,
                            kind="Internal").ap()

    CW = [g.sb(f"lfcw{u}_{b}", [128, 16 * 8], F32) for b in range(BPC)]
    _pre = g.scope()
    _pre.__enter__()
    XN0 = g.sb(f"lfxn0{u}", [128, KT * S], BF16)
    XN1 = g.sb(f"lfxn1{u}", [128, KT * S], BF16)
    XNs = [XN0, XN1]
    RXN = [f"lfxn0{u}", f"lfxn1{u}"]
    for b in range(BPC):
        with g.scope():
            emit_ln_v2(g, xfeat[b], XNs[b], RXN[b])
        g.dma("q", xn_hbm[b, :, :, :],
              XNs[b][:].rearrange("p (kt s) -> p kt s", kt=KT),
              reads=(RXN[b],), writes=("lfxnh",))

    hfm = g.sb(f"lfhf{u}", [128, 2 * KT], F32)
    h_sb = g.sb(f"lfhsb{u}", [128, 2 * KT], BF16)
    R_HR, R_HSB = f"lfhf{u}", f"lfhsb{u}"

    with g.scope():
        ctx_sb = g.sb(f"lfctx{u}", [128, KT * D], F32R)
        ctb = g.sb(f"lfctb{u}", [128, KT], F32)
        xmean = g.sb(f"lfxm{u}", [128, 2 * KT], F32R)
        g.dma("q", ctb[:], ctxb, writes=(f"lfctb{u}",))
        for kt in range(KT):
            g.dma("g", ctx_sb[:, kt * D:(kt + 1) * D],
                  ctxT[kt * 128:(kt + 1) * 128, :], writes=(f"lfctx{u}",))
        for b in range(BPC):
            with g.op("v", reads=(RXN[b],), writes=(f"lfxm{u}",)) as o:
                for kt in range(KT):
                    with g.nc.allow_low_precision(reason="f32r"):
                        o(o.E.reduce_sum(xmean[:, 2 * kt + b:2 * kt + b + 1],
                                         XNs[b][:, kt * S:(kt + 1) * S], axis=AX.X))
        with g.op("v", reads=(f"lfxm{u}",), writes=(f"lfxm{u}",), fence=True) as o:
            with g.nc.allow_low_precision(reason="f32r"):
                o(o.E.tensor_scalar_mul(xmean[:], xmean[:].bitcast(F32), 1.0 / S))
        for mo in range(KT):
            with g.op("t", reads=(f"lfctx{u}", f"lfxm{u}"), writes=("psb0",)) as o:
                for kt in range(KT):
                    o(o.E.matmul(g.ps[0][:, 0:2],
                                 ctx_sb[:, kt * D + mo * 128: kt * D + (mo + 1) * 128],
                                 xmean[:, 2 * kt:2 * kt + 2],
                                 start=(kt == 0), stop=(kt == KT - 1)))
            with g.op("v", reads=("psb0", f"lfctb{u}"),
                      writes=(R_HSB, R_HR)) as o:
                with g.nc.allow_low_precision(reason="bf16"):
                    o.E.tensor_scalar_add(h_sb[:, 2 * mo:2 * mo + 2], g.ps[0][:, 0:2],
                                          ctb[:, mo:mo + 1])
                o(o.E.tensor_scalar_add(hfm[:, 2 * mo:2 * mo + 2], g.ps[0][:, 0:2],
                                        ctb[:, mo:mo + 1]))

    if "h0row" in g.dbg:
        g.dma("q", g.dbg["h0row"], hrow[:], reads=(R_HR,), writes=("dbgh0",))
    with g.scope():
        ihsb = g.sb(f"lfih{u}", [128, KT * G3], BF16)
        xwt = g.sb(f"lfxw{u}", [128, TCW], F32)
        for kt in range(KT):
            g.dma("g", ihsb[:, kt * G3:(kt + 1) * G3],
                  ihT[kt * 128:(kt + 1) * 128, :], writes=(f"lfih{u}",))
        for b in range(BPC):
            for ot in range(24):
                for tc in range(TC):
                    with g.op("t", reads=(f"lfih{u}", RXN[b]), writes=("psb0",)) as o:
                        for kt in range(KT):
                            o(o.E.matmul(
                                g.ps[0][:, :],
                                ihsb[:, kt * G3 + ot * 128: kt * G3 + (ot + 1) * 128],
                                XNs[b][:, kt * S + tc * TCW: kt * S + (tc + 1) * TCW],
                                start=(kt == 0), stop=(kt == KT - 1)))
                    with g.op("v", reads=("psb0",), writes=(f"lfxwt{u}",)) as o:
                        o(o.E.tensor_copy(xwt[:], g.ps[0][:, :]))
                    g.dma("q", xw_hbm[b, ot, :, tc * TCW:(tc + 1) * TCW], xwt[:],
                          reads=(f"lfxwt{u}",), writes=("lfxwh",))

    with g.scope():
        whh = g.sb(f"lfwh{u}", [128, KT * G3], BF16)
        for kt in range(KT):
            g.dma("g", whh[:, kt * G3:(kt + 1) * G3],
                  hhT[kt * 128:(kt + 1) * 128, :], writes=(f"lfwh{u}",))
        CH = 64
        xwc = g.sb(f"lfxc{u}", [128, 2 * 48 * CH], F32)
        hwrow = g.sb(f"lfhw{u}", [2, G3], F32R)
        hwt = g.sb(f"lfht{u}", [128, 48], F32)
        n2 = g.sb(f"lfn2{u}", [128, 16], F32)
        dsc = g.sb(f"lfd2{u}", [128, 16], F32)
        HS = g.sb(f"lfHS{u}", [128, 64 * 2 * KT], F32)
        idr = g.const_sb("c_idr", np.eye(128, dtype=np.float32), F32R)
        RHW = [f"lfhw{i}{u}" for i in range(4)]
        R_HT, R_N2, R_D2, R_HS = (f"lfht{u}", f"lfn2{u}",
                                  f"lfd2{u}", f"lfHS{u}")

        def load_chunk(ci):
            slot = ci % 2
            for b2 in range(2):
                sb_side = xwc[:, slot * 48 * CH + b2 * CH:
                              slot * 48 * CH + b2 * CH + (24 - 1) * 2 * CH + CH]
                sb_side = xwc[:, slot * 48 * CH:(slot + 1) * 48 * CH].rearrange(
                    "p (o b t) -> p o b t", o=24, b=2)[:, :, b2]
                g.dma("q", sb_side,
                      xw_hbm[b2, :, :, ci * CH:(ci + 1) * CH].rearrange(
                          "o p t -> p o t"),
                      reads=("lfxwh",), writes=(f"lfxc{u}_{slot}",))

        load_chunk(0)
        load_chunk(1)
        for st in range(S):
            ci, tin = st // CH, st % CH
            slot = ci % 2
            base = slot * 48 * CH
            xwv = xwc[:, base:base + 48 * CH].rearrange("p (o b t) -> p o b t",
                                                        o=24, b=2)
            xws_rz = xwv[:, 0:16, :, tin].rearrange("p o b -> p (o b)")
            xws_n = xwv[:, 16:24, :, tin].rearrange("p o b -> p (o b)")
            RXW = f"lfxc{u}_{slot}"
            with g.op("t", reads=(f"lfwh{u}", R_HSB),
                      writes=("psb0", "psb1", "psb2")) as o:
                for oc in range(3):
                    for kt in range(KT):
                        o(o.E.matmul(
                            g.ps[oc][0:2, :], h_sb[:, 2 * kt:2 * kt + 2],
                            whh[:, kt * G3 + oc * TCW: kt * G3 + (oc + 1) * TCW],
                            start=(kt == 0), stop=(kt == KT - 1)))
            with g.op("t", reads=(f"lfwh{u}", R_HSB),
                      writes=("psb3", "psb4", "psb5")) as o:
                for oc in range(3, 6):
                    for kt in range(KT):
                        o(o.E.matmul(
                            g.ps[oc][0:2, :], h_sb[:, 2 * kt:2 * kt + 2],
                            whh[:, kt * G3 + oc * TCW: kt * G3 + (oc + 1) * TCW],
                            start=(kt == 0), stop=(kt == KT - 1)))
                for _d in range(6):
                    o.E.matmul(g.ps[7][0:2, :], h_sb[:, 0:2], whh[:, 0:TCW],
                               start=True, stop=True)
            with g.op("s", reads=("psb0", "psb1"), writes=(RHW[0],)) as o:
                with g.nc.allow_low_precision(reason="f32r"):
                    for c in range(2):
                        o(o.E.copy(hwrow[:, c * TCW:(c + 1) * TCW],
                                   g.ps[c][0:2, :]))
            with g.op("v", reads=("psb2",), writes=(RHW[1],)) as o:
                with g.nc.allow_low_precision(reason="f32r"):
                    o(o.E.tensor_copy(hwrow[:, 2 * TCW:3 * TCW], g.ps[2][0:2, :]))
            with g.op("s", reads=("psb3", "psb4"), writes=(RHW[2],)) as o:
                with g.nc.allow_low_precision(reason="f32r"):
                    for c in range(3, 5):
                        o(o.E.copy(hwrow[:, c * TCW:(c + 1) * TCW],
                                   g.ps[c][0:2, :]))
            with g.op("v", reads=("psb5",), writes=(RHW[3],)) as o:
                with g.nc.allow_low_precision(reason="f32r"):
                    o(o.E.tensor_copy(hwrow[:, 5 * TCW:6 * TCW], g.ps[5][0:2, :]))
            with g.op("t", reads=tuple(RHW), writes=("psb6",)) as o:
                for ot in range(24):
                    o(o.E.matmul(g.ps[6][:, 2 * ot:2 * ot + 2],
                                 hwrow[:, ot * 128:(ot + 1) * 128],
                                 idr[0:2, 0:2], start=True, stop=True))
                for _d in range(8):
                    o.E.matmul(g.ps[7][0:2, :], h_sb[:, 0:2], whh[:, 0:TCW],
                               start=True, stop=True)
            with g.op("v", reads=("psb6", RXW), writes=(R_HT,)) as o:
                o(o.E.tensor_add(hwt[:, 0:32], g.ps[6][:, 0:32], xws_rz))
            with g.op("s", reads=(R_HT,), writes=(R_HT,)) as o:
                o(o.E.activation(hwt[:, 0:32], hwt[:, 0:32], AF.Sigmoid))
            with g.op("v", reads=(R_HT, "psb6", RXW), writes=(R_N2,)) as o:
                o(o.E.tensor_mul(n2[:], hwt[:, 0:16], g.ps[6][:, 32:48]))
            with g.op("v", reads=(R_N2, RXW), writes=(R_N2,), fence=True) as o:
                o(o.E.tensor_add(n2[:], n2[:], xws_n))
            with g.op("s", reads=(R_N2,), writes=(R_N2,), fence=True) as o:
                o(o.E.activation(n2[:], n2[:], AF.Tanh))
            with g.op("v", reads=(R_N2, R_HR), writes=(R_D2,), fence=True) as o:
                o(o.E.tensor_sub(dsc[:], hfm[:], n2[:]))
            with g.op("v", reads=(R_D2, R_HT), writes=(R_D2,), fence=True) as o:
                o(o.E.tensor_mul(dsc[:], dsc[:], hwt[:, 16:32]))
            with g.op("v", reads=(R_D2, R_N2), writes=(R_HR,), fence=True) as o:
                o(o.E.tensor_add(hfm[:], n2[:], dsc[:]))
            with g.op("v", reads=(R_HR,), writes=(R_HSB, R_HS), fence=True) as o:
                with g.nc.allow_low_precision(reason="bf16"):
                    o.E.tensor_copy(h_sb[:], hfm[:])
                o(o.E.tensor_copy(
                    HS[:].rearrange("p (kb t) -> p kb t", t=64)[:, :, st % 64],
                    hfm[:]))
            if st % 64 == 63:
                c0 = st - 63
                for kt in range(KT):
                    for b2 in range(2):
                        g.dma("q", hs_hbm[b2, kt * 128:(kt + 1) * 128, c0:c0 + 64],
                              HS[:, (kt * 2 + b2) * 64:(kt * 2 + b2 + 1) * 64],
                              reads=(R_HS,), writes=("lfhsh",))
            if (st + 1) % CH == 0 and ci + 2 < S // CH:
                load_chunk(ci + 2)

    with g.scope():
        RO = g.sb(f"lfro{u}", [128, S], F32R)
        EX = g.sb(f"lfex{u}", [128, S], F32R)
        prod = g.sb(f"lfpr{u}", [128, TCW], F32R)
        exw = g.sb(f"lfexw{u}", [128, KT * D], BF16)
        stall = g.sb(f"lfsa{u}", [128, 16 * 32], F32)
        sc8 = g.sb(f"lfs8{u}", [128, 16], F32)
        srow = g.sb(f"lfsr{u}", [1, S], F32)
        rtile = g.sb(f"lfrt{u}", [128, S], F32R)
        ones2 = g.const_sb("c_ones2r", np.ones((128, 2), np.float32), F32R)
        onesrf = g.const_sb("c_onesrowf", np.ones((1, 128), np.float32), F32)
        R_RO, R_EX, R_PR, R_SA = f"lfro{u}", f"lfex{u}", f"lfpr{u}", f"lfsa{u}"
        R_SR, R_RT2, R_S8 = f"lfsr{u}", f"lfrt{u}", f"lfs8{u}"
        for kt in range(KT):
            g.dma("g", exw[:, kt * D:(kt + 1) * D],
                  exprT[kt * 128:(kt + 1) * 128, :], writes=(f"lfexw{u}",))
        for b in range(BPC):
            for e in range(KT):
                g.dma("g", rtile[:], hs_hbm[b, e * 128:(e + 1) * 128, :],
                      reads=("lfhsh",), writes=(R_RT2,))
                for tc in range(TC):
                    tsl = slice(tc * TCW, (tc + 1) * TCW)
                    with g.op("v", reads=(R_RT2,), writes=(R_PR,)) as o:
                        with g.nc.allow_low_precision(reason="f32r"):
                            o(o.E.tensor_mul(prod[:], rtile[:, tsl].bitcast(F32),
                                             rtile[:, tsl].bitcast(F32)))
                    with g.op("t", reads=(R_PR,), writes=("psb6",)) as o:
                        o(o.E.matmul(g.ps[6][0:1, :], g.ones_col(), prod[:],
                                     start=True, stop=True))
                    with g.op("v", reads=("psb6",), writes=(R_SR,)) as o:
                        o(o.E.tensor_copy(srow[0:1, tsl], g.ps[6][0:1, :]))
                with g.op("s", reads=(R_SR,), writes=(R_SR,)) as o:
                    o(o.E.activation(srow[:], srow[:], AF.Sqrt))
                with g.op("v", reads=(R_SR,), writes=(R_SR,), fence=True) as o:
                    o.E.tensor_scalar_max(srow[:], srow[:], 1e-12)
                    o(o.E.reciprocal(srow[:], srow[:]))
                for tc in range(TC):
                    tsl = slice(tc * TCW, (tc + 1) * TCW)
                    with g.op("t", reads=(R_SR,), writes=("psb7",)) as o:
                        o(o.E.matmul(g.ps[7][:, :], onesrf[:], srow[0:1, tsl],
                                     start=True, stop=True))
                    with g.op("v", reads=("psb7", R_RT2), writes=(R_RO,)) as o:
                        with g.nc.allow_low_precision(reason="f32r"):
                            o(o.E.tensor_mul(RO[:, tsl], rtile[:, tsl].bitcast(F32),
                                             g.ps[7][:, :]))
                for tc in range(TC):
                    with g.op("t", reads=(f"lfexw{u}", RXN[b]), writes=("psb0",)) as o:
                        for kt in range(KT):
                            o(o.E.matmul(
                                g.ps[0][:, :],
                                exw[:, kt * D + e * 128: kt * D + (e + 1) * 128],
                                XNs[b][:, kt * S + tc * TCW: kt * S + (tc + 1) * TCW],
                                start=(kt == 0), stop=(kt == KT - 1)))
                    with g.op("v", reads=("psb0",), writes=(R_EX,)) as o:
                        with g.nc.allow_low_precision(reason="f32r"):
                            o(o.E.tensor_copy(EX[:, tc * TCW:(tc + 1) * TCW],
                                              g.ps[0][:, :]))
                for tt in range(16):
                    tsl = slice(tt * 128, (tt + 1) * 128)
                    for stat in range(3):
                        with g.op("v", reads=(R_EX, R_RO), writes=(R_PR,)) as o:
                            with g.nc.allow_low_precision(reason="f32r"):
                                aa = (EX if stat != 2 else RO)[:, tsl].bitcast(F32)
                                bb = (RO if stat != 1 else EX)[:, tsl].bitcast(F32)
                                o(o.E.tensor_mul(prod[:, 0:128], aa, bb))
                        with g.op("t", reads=(R_PR,), writes=("psb5",)) as o:
                            o(o.E.matmul(g.ps[5][:, 2 * stat:2 * stat + 2],
                                         prod[:, 0:128], ones2[:],
                                         start=True, stop=True))
                    with g.op("v", reads=("psb5",), writes=(R_SA,)) as o:
                        o(o.E.tensor_copy(
                            stall[:, tt * 32 + e:tt * 32 + 24:8], g.ps[5][:, 0:6:2]))
            if b == 0 and "stall" in g.dbg:
                g.dma("q", g.dbg["stall"], stall[:], reads=(R_SA,), writes=("dbgsa",))
            for tt in range(16):
                _emit_topk(g, u, tt, stall, sc8, CW[b], R_SA, R_S8, f"lfcw{u}_{b}")

    if "cw0" in g.dbg:
        g.dma("q", g.dbg["cw0"], CW[0][:], reads=(f"lfcw{u}_0",), writes=("dbgcw",))
    _pre.__exit__(None, None, None)
    _emit_moe(g, u, xn_hbm, CW, xfeat, yfeat, gwT, uwT, dwT, sgT, suT, sdT)
    tr.barrier()


STALL_STRIDE = 32


def _emit_topk(g, u, tt, stall, sc8, CWb, R_SA, R_S8, R_CW):
    EPSC = 1e-08
    o0 = tt * STALL_STRIDE
    num = stall[:, o0:o0 + 8]
    sE = stall[:, o0 + 8:o0 + 16]
    sR = stall[:, o0 + 16:o0 + 24]
    with g.op("s", reads=(R_SA,), writes=(R_SA,)) as o:
        o.E.activation(sE, sE, AF.Sqrt)
        o(o.E.activation(sR, sR, AF.Sqrt))
    with g.op("v", reads=(R_SA,), writes=(R_SA,), fence=True) as o:
        o.E.tensor_scalar_max(sE, sE, EPSC)
        o(o.E.tensor_scalar_max(sR, sR, EPSC))
    with g.op("v", reads=(R_SA,), writes=(R_SA,), fence=True) as o:
        o(o.E.tensor_mul(sE, sE, sR))
    with g.op("s", reads=(R_SA,), writes=(R_SA,), fence=True) as o:
        o(o.E.activation(sR, sE, AF.Ln))
    with g.op("s", reads=(R_SA,), writes=(R_SA,), fence=True) as o:
        o(o.E.activation(sR, sR, AF.Exp, scale=-1.0))
    with g.op("v", reads=(R_SA,), writes=(R_S8,), fence=True) as o:
        o(o.E.tensor_mul(sc8[:, 0:8], num, sR))
    with g.op("v", reads=(R_S8,), writes=(R_S8,), fence=True) as o:
        o(o.E.tensor_scalar(sc8[:, 0:8], sc8[:, 0:8], -9.0, 9.0, ALU.mult, ALU.add))
    with g.op("v", reads=(R_S8,), writes=(R_S8,), fence=True) as o:
        o(o.E.reduce_max(sc8[:, 8:9], sc8[:, 0:8], axis=AX.X))
    with g.op("v", reads=(R_S8,), writes=(R_SA + "1",), fence=True) as o:
        o(o.E.tensor_scalar(stall[:, o0 + 8:o0 + 16], sc8[:, 0:8], sc8[:, 8:9],
                            None, ALU.is_ge))
    with g.op("v", reads=(R_SA + "1", R_S8), writes=(R_S8,), fence=True) as o:
        o(o.E.scalar_tensor_tensor(sc8[:, 0:8], stall[:, o0 + 8:o0 + 16], -1e30,
                                   sc8[:, 0:8], ALU.mult, ALU.add))
    with g.op("v", reads=(R_S8,), writes=(R_S8,), fence=True) as o:
        o(o.E.reduce_max(sc8[:, 9:10], sc8[:, 0:8], axis=AX.X))
    with g.op("v", reads=(R_S8,), writes=(R_SA + "2",), fence=True) as o:
        o(o.E.tensor_scalar(stall[:, o0 + 16:o0 + 24], sc8[:, 0:8], sc8[:, 9:10],
                            None, ALU.is_ge))
    with g.op("v", reads=(R_S8,), writes=(R_S8,), fence=True) as o:
        o(o.E.tensor_scalar_mul(sc8[:, 10:11], sc8[:, 9:10], -1.0))
    with g.op("s", reads=(R_S8,), writes=(R_S8,), fence=True) as o:
        o(o.E.activation(sc8[:, 11:12], sc8[:, 8:9], AF.Sigmoid, bias=sc8[:, 10:11]))
    with g.op("v", reads=(R_S8,), writes=(R_S8,), fence=True) as o:
        o(o.E.tensor_scalar(sc8[:, 12:13], sc8[:, 11:12], -1.0, 1.0, ALU.mult,
                            ALU.add))
    with g.op("v", reads=(R_S8, R_SA + "1", R_SA + "2"), writes=(R_SA,),
              fence=True) as o:
        o.E.tensor_scalar_mul(stall[:, o0 + 8:o0 + 16], stall[:, o0 + 8:o0 + 16],
                              sc8[:, 11:12])
        o(o.E.tensor_scalar_mul(stall[:, o0 + 16:o0 + 24], stall[:, o0 + 16:o0 + 24],
                                sc8[:, 12:13]))
    with g.op("v", reads=(R_SA,), writes=(R_CW,), fence=True) as o:
        o(o.E.tensor_add(CWb[:, tt * 8:(tt + 1) * 8], stall[:, o0 + 8:o0 + 16],
                         stall[:, o0 + 16:o0 + 24]))


def _emit_moe(g, u, xn_hbm, CW, xfeat, yfeat, gwT, uwT, dwT, sgT, suT, sdT):
    F_ = 2048
    gsl = g.sb(f"moes{u}", [128, TCW], F32)
    he = g.sb(f"moeh{u}", [128, F_], BF16)
    heT = g.sb(f"moet{u}", [128, 16 * TCW], BF16)
    Ysb = [g.sb(f"moey{u}_{b}", [128, KT * TCW], F32) for b in range(BPC)]
    xnt = [g.sb(f"moexn{u}_{b}", [128, KT * TCW], BF16) for b in range(BPC)]
    wg_sb = g.sb(f"moewg{u}", [128, KT * F_], BF16)
    wu_sb = g.sb(f"moewu{u}", [128, KT * F_], BF16)
    wd_sb = g.sb(f"moewd{u}", [128, 16 * D], BF16)
    xres = g.sb(f"moex{u}", [128, 2 * TCW], F32)
    idbf = g.const_sb("c_idbf", np.eye(128, dtype=np.float32), BF16)
    R_GS, R_HE, R_HT, R_WD = (f"moes{u}", f"moeh{u}", f"moet{u}",
                              f"moewd{u}")
    R_Y = [f"moey{u}_{b}" for b in range(BPC)]
    R_XN = [f"moexn{u}_{b}" for b in range(BPC)]
    RW = {"gA": f"moewgA{u}", "gB": f"moewgB{u}",
          "uA": f"moewuA{u}", "uB": f"moewuB{u}"}
    wgv = wg_sb[:].rearrange("p (kt f) -> p kt f", kt=KT)
    wuv = wu_sb[:].rearrange("p (kt f) -> p kt f", kt=KT)
    for TT4 in range(4):
        for b in range(BPC):
            g.dma("q", xnt[b][:].rearrange("p (kt t) -> p kt t", kt=KT),
                  xn_hbm[b, :, :, TT4 * TCW:(TT4 + 1) * TCW],
                  reads=("lfxnh",), writes=(R_XN[b],))
        for e in range(9):
            gsrc = gwT[e, :, :] if e < 8 else sgT
            usrc = uwT[e, :, :] if e < 8 else suT
            dsrc = dwT[e, :, :] if e < 8 else sdT
            gv = gsrc.rearrange("(kt p) f -> p kt f", p=128)
            uv = usrc.rearrange("(kt p) f -> p kt f", p=128)
            dv = dsrc.rearrange("(hk p) d -> p hk d", p=128)
            HF = 2 * TCW
            g.dma("g", wgv[:, :, 0:HF], gv[:, :, 0:HF], writes=(RW["gA"],))
            g.dma("g", wuv[:, :, 0:HF], uv[:, :, 0:HF], writes=(RW["uA"],))
            g.dma("g", wgv[:, :, HF:F_], gv[:, :, HF:F_], writes=(RW["gB"],))
            g.dma("g", wuv[:, :, HF:F_], uv[:, :, HF:F_], writes=(RW["uB"],))
            g.dma("g", wd_sb[:].rearrange("p (hk d) -> p hk d", hk=16),
                  dv, writes=(R_WD,))
            for b in range(BPC):
                XNb = xnt[b]
                for t4 in range(4):
                    tt = TT4 * 4 + t4
                    for oc in range(4):
                        wh = "A" if oc < 2 else "B"
                        with g.op("t", reads=(RW["g" + wh], RW["u" + wh],
                                              R_XN[b]),
                                  writes=(f"psb{oc}", f"psb{4 + oc}")) as o:
                            for kt in range(KT):
                                o.E.matmul(
                                    g.ps[oc][:, :],
                                    XNb[:, kt * TCW + t4 * 128:
                                        kt * TCW + (t4 + 1) * 128],
                                    wg_sb[:, kt * F_ + oc * TCW:
                                          kt * F_ + (oc + 1) * TCW],
                                    start=(kt == 0), stop=(kt == KT - 1))
                                o(o.E.matmul(
                                    g.ps[4 + oc][:, :],
                                    XNb[:, kt * TCW + t4 * 128:
                                        kt * TCW + (t4 + 1) * 128],
                                    wu_sb[:, kt * F_ + oc * TCW:
                                          kt * F_ + (oc + 1) * TCW],
                                    start=(kt == 0), stop=(kt == KT - 1)))
                    for oc in range(4):
                        with g.op("s", reads=(f"psb{oc}",), writes=(R_GS,)) as o:
                            o(o.E.activation(gsl[:], g.ps[oc][:, :], AF.Silu))
                        rds = (R_GS, f"psb{4 + oc}")
                        if e < 8:
                            rds = rds + (f"lfcw{u}_{b}",)
                        with g.op("v", reads=rds, writes=(R_HE,)) as o:
                            with g.nc.allow_low_precision(reason="bf16"):
                                if e < 8:
                                    o.E.tensor_mul(he[:, oc * TCW:(oc + 1) * TCW],
                                                   g.ps[4 + oc][:, :], gsl[:])
                                    o(o.E.tensor_scalar_mul(
                                        he[:, oc * TCW:(oc + 1) * TCW],
                                        he[:, oc * TCW:(oc + 1) * TCW],
                                        CW[b][:, tt * 8 + e:tt * 8 + e + 1]))
                                else:
                                    o(o.E.tensor_mul(he[:, oc * TCW:(oc + 1) * TCW],
                                                     g.ps[4 + oc][:, :], gsl[:]))
                    for half in range(2):
                        psv = g.ps[half][:].bitcast(BF16)
                        with g.op("t", reads=(R_HE,), writes=(f"psb{half}",)) as o:
                            for j in range(8):
                                hk = half * 8 + j
                                o(o.E.transpose(psv[:, j * 128:(j + 1) * 128],
                                                he[:, hk * 128:(hk + 1) * 128],
                                                idbf[:]))
                        with g.op("v", reads=(f"psb{half}",), writes=(R_HT,)) as o:
                            with g.nc.allow_low_precision(reason="bf16"):
                                o(o.E.tensor_copy(
                                    heT[:].rearrange("p (hk t) -> p hk t", t=TCW)
                                       [:, half * 8:(half + 1) * 8,
                                        t4 * 128:(t4 + 1) * 128],
                                    psv.rearrange("p (j t) -> p j t", t=128)))
                for hk in range(16):
                    with g.op("t", reads=(R_WD, R_HT),
                              writes=tuple(f"psb{c}" for c in range(8))) as o:
                        for mo in range(KT):
                            o(o.E.matmul(
                                g.ps[mo][:, :],
                                wd_sb[:, hk * D + mo * 128:
                                      hk * D + (mo + 1) * 128],
                                heT[:, hk * TCW:(hk + 1) * TCW],
                                start=(hk == 0), stop=(hk == 15)))
                for mo in range(KT):
                    with g.op("v", reads=(f"psb{mo}",), writes=(R_Y[b],)) as o:
                        if e == 0:
                            o(o.E.tensor_copy(Ysb[b][:, mo * TCW:(mo + 1) * TCW],
                                              g.ps[mo][:, :]))
                        else:
                            o(o.E.tensor_add(Ysb[b][:, mo * TCW:(mo + 1) * TCW],
                                             Ysb[b][:, mo * TCW:(mo + 1) * TCW],
                                             g.ps[mo][:, :]))
        for b in range(BPC):
            for mo in range(KT):
                sl2 = (mo % 2) * TCW
                xr = xres[:, sl2:sl2 + TCW]
                rx = f"moex{u}_{mo % 2}"
                g.dma("q", xr, xfeat[b, mo * 128:(mo + 1) * 128,
                                     TT4 * TCW:(TT4 + 1) * TCW],
                      reads=("xf",), writes=(rx,))
                with g.op("v", reads=(R_Y[b],), writes=(rx,)) as o:
                    o(o.E.tensor_add(xr, Ysb[b][:, mo * TCW:(mo + 1) * TCW], xr))
                g.dma("q", yfeat[b, mo * 128:(mo + 1) * 128,
                                 TT4 * TCW:(TT4 + 1) * TCW], xr,
                      reads=(rx,), writes=("yf",))


def _lf_emit0(g, sl, xf, yf):
    emit_l1_ffn(g, sl, xf, yf, sl)


SUBLAYER_EMIT["L1_ffn"] = _lf_emit0


CHAIN = ["L0_attn_a", "L0_ffn", "L0_attn_b", "L0_ffn",
         "L1_attn_a", "L1_ffn", "L1_attn_b", "L1_ffn"]


def build_program(kinds):
    nc = bass.Bass("TRN2", target_bir_lowering=False)
    es = ExitStack()
    g = G(nc, es)
    xf = nc.dram_tensor("x_feat", [BPC, D, S], F32, kind="ExternalInput").ap()
    yf = nc.dram_tensor("y_feat", [BPC, D, S], F32, kind="ExternalOutput").ap()
    names = set()
    for k in kinds:
        for w in SUBLAYER_WEIGHTS[k]:
            names.add(w)
    for w in sorted(names):
        shape, dt = WEIGHT_SPECS[w]
        g.dram[w] = nc.dram_tensor(w, shape, dt, kind="ExternalInput").ap()
    scratch = None
    if len(kinds) > 1:
        scratch = nc.dram_tensor("x_scratch", [BPC, D, S], F32, kind="Internal").ap()
    cur = xf
    for i, k in enumerate(kinds):
        last = (i == len(kinds) - 1)
        dst = yf if last else scratch
        with g.scope():
            SUBLAYER_EMIT[k](g, f"{k}{i}", cur, dst)
        cur = dst
    st = g.tr.state.get("yf")
    for e, semv in g.tr.dsem.items():
        if g.tr.dcnt[e] > 0:
            g.tr._wait("q", (semv, g.tr.dcnt[e], None))
    for e, semv in g.tr.sem.items():
        if g.tr.tick[e] > 0:
            g.tr._wait("q", (semv, g.tr.tick[e], e))
    return nc, sorted(names), es


FUSED = True


def kernel(**inputs):
    x = np.asarray(inputs["L0_attn_a_x"])
    tw = prep_weights(inputs)
    shards = []
    for c in range(NCORES):
        xb = x[c * BPC:(c + 1) * BPC]
        shards.append(np.ascontiguousarray(xb.transpose(0, 2, 1)))
    if FUSED:
        groups = [CHAIN]
    else:
        groups = [[k] for k in CHAIN]
    progs = {}
    for kinds in groups:
        key = tuple(kinds)
        if key not in progs:
            progs[key] = build_program(kinds)
    for kinds in groups:
        nc, wnames, es = progs[tuple(kinds)]
        in_maps = []
        for c in range(NCORES):
            m = {"x_feat": shards[c]}
            for w in wnames:
                m[w] = tw[w]
            in_maps.append(m)
        res = run_bass_kernel_spmd(nc, in_maps, core_ids=list(range(NCORES)))
        shards = [res.results[c]["y_feat"] for c in range(NCORES)]
    outs = [s.transpose(0, 2, 1) for s in shards]
    return np.ascontiguousarray(np.concatenate(outs, axis=0)).astype(np.float32)
